# version 06
# baseline (speedup 1.0000x reference)
import numpy as np
import ml_dtypes
import concourse.bass as bass
import concourse.mybir as mybir
from concourse.bass_utils import run_bass_kernel_spmd

F32 = mybir.dt.float32
BF16 = mybir.dt.bfloat16
AF = mybir.ActivationFunctionType
ALU = mybir.AluOpType
AX = mybir.AxisListType

N_CORES = 8
D = 1024
NDC = 8
T_OWN = 2048
NT_OWN = 16
NT_ALL = 64
H = 16
E_EXP = 14
KEPS = 0.001
LN1EPS = 1e-6
PLNEPS = 1e-5


class Prog:

    ENGINES = ("t", "v", "s", "g", "y")

    def __init__(self, nc):
        self.nc = nc
        self.ops = {e: [] for e in self.ENGINES}
        self.sems = {}
        self.dma_count = {}
        self.group_sems = set()

    def make_sems(self, ctx, dma_streams, group_streams):
        for e in self.ENGINES:
            self.sems[e] = ctx.enter_context(self.nc.semaphore(f"sem_{e}"))
        for nm in dma_streams:
            self.sems[nm] = ctx.enter_context(self.nc.semaphore(f"sem_{nm}"))
            self.dma_count[nm] = 0
        self.group_sems = set(group_streams)

    def op(self, eng, fn, waits=(), dma=False):
        assert eng in self.ENGINES
        w = [x for x in waits if x is not None]
        if dma:
            assert isinstance(dma, str) and dma in self.dma_count, dma
            self.dma_count[dma] += 1
            self.ops[eng].append([fn, w, True, dma])
            return ("D", dma, self.dma_count[dma])
        self.ops[eng].append([fn, w, False, None])
        return (eng, len(self.ops[eng]) - 1)

    def upto(self, ref):
        if ref[0] == "D":
            return ref
        eng, i = ref
        self.ops[eng][i][2] = True
        return ref

    def last(self, eng):
        assert self.ops[eng]
        return (eng, len(self.ops[eng]) - 1)

    def upto_last(self, eng):
        lst = self.ops[eng]
        for i in range(len(lst) - 1, -1, -1):
            if lst[i][3] is None:
                lst[i][2] = True
                return (eng, i)
        raise AssertionError(f"no compute op on {eng}")

    def emit(self, block):
        cum = {}
        for e in self.ENGINES:
            c = 0
            lst = []
            for rec in self.ops[e]:
                if rec[2] and rec[3] is None:
                    c += 1
                lst.append(c)
            cum[e] = lst

        def wait_val(ref):
            if ref[0] == "D":
                _, nm, idx = ref
                if nm in self.group_sems:
                    return (nm, 16 * self.dma_count[nm])
                return (nm, 16 * idx)
            we, wi = ref
            return (we, cum[we][wi])

        def run(eng_obj, e):
            for rec in self.ops[e]:
                fn, waits, marked, dmasem = rec
                done = set()
                for ref in waits:
                    nm, val = wait_val(ref)
                    if (nm, val) in done:
                        continue
                    done.add((nm, val))
                    eng_obj.wait_ge(self.sems[nm], val)
                instr = fn(eng_obj)
                if dmasem is not None:
                    instr.then_inc(self.sems[dmasem], 16)
                elif marked:
                    assert instr is not None, f"marked op on {e} returned None"
                    instr.then_inc(self.sems[e], 1)

        block.tensor(lambda t: run(t, "t"))
        block.vector(lambda v: run(v, "v"))
        block.scalar(lambda s: run(s, "s"))
        block.gpsimd(lambda g: run(g, "g"))
        block.sync(lambda y: run(y, "y"))


class _StopBuild(Exception):
    pass


def build_nc(debug_taps=(), stop_after=None):
    nc = bass.Bass("TRN2", target_bir_lowering=False)
    taps = set(debug_taps)

    xb = nc.dram_tensor("xb", [8192, D], F32, kind="ExternalInput").ap()
    wq_d = nc.dram_tensor("wq", [D, D], F32, kind="ExternalInput").ap()
    wkv_d = nc.dram_tensor("wkv", [D, 2 * D], F32, kind="ExternalInput").ap()
    bq_d = nc.dram_tensor("bqv", [D], F32, kind="ExternalInput").ap()
    bkv_d = nc.dram_tensor("bkv", [2 * D], F32, kind="ExternalInput").ap()
    ln1s_d = nc.dram_tensor("ln1s", [D], F32, kind="ExternalInput").ap()
    ln1b_d = nc.dram_tensor("ln1b", [D], F32, kind="ExternalInput").ap()
    wo_d = nc.dram_tensor("wo", [D, D], F32, kind="ExternalInput").ap()
    bo_d = nc.dram_tensor("bo", [D], F32, kind="ExternalInput").ap()
    ln2s_d = nc.dram_tensor("ln2s", [D], F32, kind="ExternalInput").ap()
    ln2b_d = nc.dram_tensor("ln2b", [D], F32, kind="ExternalInput").ap()
    w1_d = nc.dram_tensor("w1", [D, 4096], F32, kind="ExternalInput").ap()
    b1_d = nc.dram_tensor("b1", [4096], F32, kind="ExternalInput").ap()
    w2_d = nc.dram_tensor("w2", [4096, D], F32, kind="ExternalInput").ap()
    b2_d = nc.dram_tensor("b2", [D], F32, kind="ExternalInput").ap()
    gw_d = nc.dram_tensor("gw", [E_EXP, D], F32, kind="ExternalInput").ap()
    mw1_d = nc.dram_tensor("mw1", [E_EXP, D, 2048], F32, kind="ExternalInput").ap()
    mb1_d = nc.dram_tensor("mb1", [E_EXP, 2048], F32, kind="ExternalInput").ap()
    mw2_d = nc.dram_tensor("mw2", [E_EXP, 2048, D], F32, kind="ExternalInput").ap()
    mb2_d = nc.dram_tensor("mb2", [E_EXP, D], F32, kind="ExternalInput").ap()
    out_d = nc.dram_tensor("out", [T_OWN, D], F32, kind="ExternalOutput").ap()

    tap_t = {}
    tap_shapes = dict(
        xnT=([128, NDC * T_OWN], F32), phiq=([128, NDC * T_OWN], F32),
        kvsb=([128, 8 * 136], F32), attn=([128, NT_OWN * 1024], F32),
        xres=([128, NT_OWN * 1024], F32), x1=([128, NT_OWN * 1024], F32),
        mask=([128, NT_OWN * E_EXP], F32), acc0=([128, 8 * 1024], F32),
        y2T=([128, NDC * 1024], F32),
    )
    for name in taps:
        shp, dt_ = tap_shapes[name]
        tap_t[name] = nc.dram_tensor("tap_" + name, shp, dt_, kind="ExternalOutput").ap()

    identb = nc.inline_tensor(np.eye(128, dtype=ml_dtypes.bfloat16), name="identb")
    identf = nc.inline_tensor(np.eye(128, dtype=np.float32), name="identf")
    ones128 = nc.inline_tensor(np.ones((128, 128), dtype=ml_dtypes.bfloat16), name="ones128")

    A0, B0, C0, D0, E0 = 20480, 86016, 151552, 184320, 217088

    def sb(name, shape, dtype, off):
        return nc.alloc_sbuf_tensor_at(name, shape, dtype, offset=off)

    wkv_sb = sb("wkv_sb", [128, NDC * 2048], BF16, A0)
    xt_sb = [sb(f"xt{i}", [128, 1024], F32, A0 + 32768 + i * 4096) for i in range(2)]
    xc_sb = [sb(f"xc{i}", [128, 1024], BF16, A0 + 40960 + i * 2048) for i in range(2)]
    phik_sb = [sb(f"phik{i}", [128, 1024], BF16, A0 + 45056 + i * 2048) for i in range(2)]
    vtok_sb = [sb(f"vtok{i}", [128, 1024], BF16, A0 + 49152 + i * 2048) for i in range(2)]
    brep_sb = sb("brep", [128, 1024], BF16, A0 + 53248)
    kvbias_b = sb("kvbias_b", [128, 2048], F32, A0 + 55296)
    gwraw_sb = sb("gwraw", [E_EXP, 1024], BF16, A0 + 63488)
    kv_blk = sb("kv_blk", [128, 8 * 136], BF16, A0 + 55296)
    wo_sb = sb("wo_sb", [128, NDC * 1024], BF16, A0)
    x0s_sb = [sb(f"x0s{i}", [128, 1024], F32, A0 + 16384 + i * 4096) for i in range(3)]
    bo_b = sb("bo_b", [128, 1024], F32, A0 + 28672)
    b2_b = sb("b2_b", [128, 1024], F32, A0 + 32768)
    mb1raw_sb = sb("mb1raw", [E_EXP, 2048], F32, D0 + 24576)
    acc_sb = sb("acc_sb", [128, 8 * 1024], F32, A0)
    pool_c = sb("pool_c", [128, 1024], F32, A0 + 36864)
    pool_o = sb("pool_o", [128, 1024], F32, A0 + 40960)

    wq_sb = sb("wq_sb", [128, NDC * 1024], BF16, B0)
    vraw_sb = sb("vraw", [8, 5 * 128], F32, B0 + 16384)
    braw_sb = sb("braw", [32, 128], F32, B0 + 20480)
    xres_sb = sb("xres_sb", [128, NT_OWN * 1024], F32, B0)

    xnT_sb = sb("xnT_sb", [128, NDC * T_OWN], BF16, C0)
    attn_sb = sb("attn_sb", [128, NT_OWN * 1024], BF16, C0)
    y2T_sb = sb("y2T_sb", [128, NDC * 2048], BF16, C0)
    upool_sb = [sb(f"upool{i}", [128, 1024], F32, A0 + i * 4096) for i in range(2)]
    x1T_sb = sb("x1T_sb", [128, NDC * T_OWN], BF16, C0)

    phiq_sb = sb("phiq_sb", [128, NDC * T_OWN], BF16, D0)
    attnT_sb = sb("attnT_sb", [128, NDC * T_OWN], BF16, D0)
    w1p_sb = [sb(f"w1p{i}", [128, NDC * 256], BF16, D0 + i * 4096) for i in range(2)]
    w2p_sb = [sb(f"w2p{i}", [128, 2 * 1024], BF16, D0 + 8192 + i * 4096) for i in range(2)]
    h1T_sb = [sb(f"h1T{i}", [128, 2 * 2048], BF16, D0 + 16384 + i * 8192) for i in range(2)]
    junk_sb = sb("junk_sb", [128, 1024], BF16, A0 + 49152)
    maskT_sb = sb("maskT_sb", [E_EXP, NT_OWN * 128], BF16, A0 + 57472)
    mw1_sb = [sb(f"mw1s{i}", [128, NDC * 256], BF16, D0 + i * 4096) for i in range(2)]
    mw2_sb = [sb(f"mw2s{i}", [128, 4 * 1024], BF16, D0 + 8192 + i * 8192) for i in range(2)]
    hhT_sb = sb("hhT_sb", [128, 4 * 1024], BF16, D0 + 24576)

    e = E0
    identb_sb = sb("identb_sb", [128, 128], BF16, e); e += 256
    identf_sb = sb("identf_sb", [128, 128], F32, e); e += 512
    ones_sb = sb("ones_sb", [128, 128], BF16, e); e += 256
    gwt_sb = sb("gwt_sb", [128, NDC * E_EXP], BF16, e); e += 256
    b1m_sb = sb("b1m_sb", [128, 16 * E_EXP], F32, e); e += 896
    mask_sb = sb("mask_sb", [128, NT_OWN * E_EXP], F32, e); e += 896
    b2m_sb = sb("b2m_sb", [E_EXP, 1024], BF16, e); e += 2048
    kv_sb = sb("kv_sb", [128, 8 * 72], BF16, e); e += 1184
    bqe_sb = sb("bqe_sb", [128, NDC], F32, e); e += 32
    g1c_sb = sb("g1c_sb", [128, NDC], F32, e); e += 32
    b1c_sb = sb("b1c_sb", [128, NDC], BF16, e); e += 32
    b1cf_sb = sb("b1cf_sb", [128, NDC], F32, e); e += 32
    g2c_sb = sb("g2c_sb", [128, NDC], F32, e); e += 32
    b2c_sb = sb("b2c_sb", [128, NDC], F32, e); e += 32
    bqc_sb = sb("bqc_sb", [128, NDC], F32, e); e += 32
    b1mlp_sb = sb("b1mlp_sb", [128, 32], F32, e); e += 128
    st_sb = [sb(f"st{i}", [128, 12], F32, e + i * 64) for i in range(2)]; e += 128
    mv_sb = [sb(f"mv{i}", [128, 8], F32, e + i * 32) for i in range(4)]; e += 128
    r1_sb = [sb(f"r1_{i}", [128, 8], F32, e + i * 32) for i in range(4)]; e += 128
    rs_sb = [sb(f"rs{i}", [128, 8], F32, e + i * 32) for i in range(4)]; e += 128
    rd_sb = [sb(f"rd{i}", [128, 16], F32, e + i * 64) for i in range(2)]; e += 128
    sums_sb = [sb(f"sums{i}", [128, 8], F32, e + i * 32) for i in range(4)]; e += 128
    ssq_sb = [sb(f"ssq{i}", [128, 8], F32, e + i * 32) for i in range(4)]; e += 128
    mu_sb = [sb(f"mu{i}", [128, 8], F32, e + i * 32) for i in range(4)]; e += 128
    tp1_sb = [sb(f"tp1_{i}", [128, 8], F32, e + i * 32) for i in range(4)]; e += 128
    m1_sb = sb("m1_sb", [128, 8], F32, e); e += 32
    m2_sb = sb("m2_sb", [128, 8], F32, e); e += 32
    w0_sb = sb("w0_sb", [128, 8], F32, e); e += 32
    w1w_sb = sb("w1w_sb", [128, 8], F32, e); e += 32
    eq1_sb = sb("eq1_sb", [128, E_EXP], F32, e); e += 64
    eq2_sb = sb("eq2_sb", [128, E_EXP], F32, e); e += 64
    lm_sb = sb("lm_sb", [128, E_EXP], F32, e); e += 64
    assert e <= 229344, f"E overflow: {e}"

    ps = nc.alloc_psum_tensor("ps", [128, 4096], F32)

    def bank(b0, nb):
        return ps[:, b0 * 512:(b0 + nb) * 512]

    ptr_bf = ps[:, 7 * 512:8 * 512].bitcast(BF16)
    ptr_bf2 = ps[:, 6 * 512:7 * 512].bitcast(BF16)
    ptr_of = (ptr_bf, ptr_bf2)
    kv_ps = ps[0:64, 4 * 512:4 * 512 + 1024]
    ks_ps = ps[0:64, 6 * 512:6 * 512 + 16]

    P = Prog(nc)
    import contextlib
    ctx = contextlib.ExitStack()
    DMA_STREAMS = ["cst", "w0", "wo", "wol", "bo", "b2b", "mb1", "xb0", "xb1",
                   "x00", "x01", "x02", "st", "w1a", "w1b", "w2a", "w2b",
                   "m1a", "m1b", "m2", "m2b"] + ["tap_" + t for t in
                   ("xnT", "phiq", "kvsb", "attn", "xres", "x1", "mask", "acc0", "y2T")]
    P.make_sems(ctx, DMA_STREAMS, group_streams=["cst", "w0"])

    def rec(eng, fn, waits=(), dma=False):
        return P.op(eng, fn, waits=waits, dma=dma)

    def _maybe_stop(stage, wait=None):
        if stop_after != stage:
            return
        last = None
        for tc in range(NT_OWN):
            last = rec("y", (lambda tc: lambda y: y.dma_start(
                out_d[tc * 128:(tc + 1) * 128, :], acc_sb[:, 0:1024]))(tc),
                waits=([wait] if (wait is not None and tc == 0) else ()), dma="st")
        rec("y", lambda y: y.nop(), waits=[P.upto(last)])
        raise _StopBuild()

    try:
        ld_ident = rec("y", lambda y: y.dma_start(identb_sb[:], identb.ap()), dma="cst")
        ld_identf = rec("y", lambda y: y.dma_start(identf_sb[:], identf.ap()), dma="cst")
        ld_ones = rec("y", lambda y: y.dma_start(ones_sb[:], ones128.ap()), dma="cst")
        vsrcs = [ln1s_d, ln1b_d, ln2s_d, ln2b_d, bq_d]
        ld_vraw = None
        for i, src in enumerate(vsrcs):
            ld_vraw = rec("y", (lambda i, src: lambda y: y.dma_start(
                vraw_sb[0:8, i * 128:(i + 1) * 128],
                src.rearrange("(a b) -> a b", a=8)))(i, src), dma="cst")
        ld_braw = rec("y", lambda y: y.dma_start(
            braw_sb[:], b1_d.rearrange("(a b) -> a b", a=32)), dma="cst")
        ld_bias_tiles = rec("y", lambda y: y.dma_start(
            kvbias_b[:], bkv_d[None, :].broadcast_to([128, 2048])), dma="cst")
        ld_gw = rec("g", lambda g: g.dma_start(gwraw_sb[:], gw_d), dma="w0")
        ld_b2m = rec("g", lambda g: g.dma_start(b2m_sb[:], mb2_d), dma="w0")
        ld_wkv = rec("g", lambda g: g.dma_start(
            wkv_sb.ap().rearrange("p (c f) -> p c f", c=NDC),
            wkv_d.rearrange("(c p) f -> p c f", p=128)), dma="w0")
        ld_wq = rec("g", lambda g: g.dma_start(
            wq_sb.ap().rearrange("p (c f) -> p c f", c=NDC),
            wq_d.rearrange("(c p) f -> p c f", p=128)), dma="w0")

        tp_ev = {}
        vec_tr = None
        for i, dstc in enumerate([g1c_sb, None, g2c_sb, b2c_sb, bqc_sb]):
            vec_tr = rec("t", (lambda i: lambda t: t.transpose(
                ps[0:128, 6 * 512 + i * 8: 6 * 512 + (i + 1) * 8],
                vraw_sb[0:8, i * 128:(i + 1) * 128], identf_sb[0:8, 0:8]))(i),
                waits=[P.upto(ld_vraw), P.upto(ld_identf)] if i == 0 else ())
        vec_tr2 = rec("t", lambda t: t.transpose(
            ps[0:128, 6 * 512 + 40: 6 * 512 + 72], braw_sb[:], identf_sb[0:32, 0:32]),
            waits=[P.upto(ld_braw)])
        evs = []
        for i, dstc in enumerate([g1c_sb, None, g2c_sb, b2c_sb, bqc_sb]):
            if dstc is None:
                evs.append(rec("v", (lambda i: lambda v: v.tensor_copy(
                    b1c_sb[:, 0:NDC], ps[0:128, 6 * 512 + i * 8: 6 * 512 + i * 8 + 8]))(i),
                    waits=[P.upto(vec_tr2)] if i == 0 else ()))
                evs.append(rec("v", (lambda i: lambda v: v.tensor_copy(
                    b1cf_sb[:, 0:NDC], ps[0:128, 6 * 512 + i * 8: 6 * 512 + i * 8 + 8]))(i)))
            elif dstc is bqc_sb:
                evs.append(rec("v", (lambda i, dstc: lambda v: v.tensor_scalar(
                    dstc[:, 0:NDC], ps[0:128, 6 * 512 + i * 8: 6 * 512 + i * 8 + 8],
                    KEPS, None, ALU.add))(i, dstc),
                    waits=[P.upto(vec_tr2)] if i == 0 else ()))
            else:
                evs.append(rec("v", (lambda i, dstc: lambda v: v.tensor_copy(
                    dstc[:, 0:NDC], ps[0:128, 6 * 512 + i * 8: 6 * 512 + i * 8 + 8]))(i, dstc),
                    waits=[P.upto(vec_tr2)] if i == 0 else ()))
        evs.append(rec("v", lambda v: v.tensor_copy(
            b1mlp_sb[:], ps[0:128, 6 * 512 + 40: 6 * 512 + 72])))
        cols_done = P.upto_last("v")

        sfold = None
        for dc in range(NDC):
            sfold = rec("s", (lambda dc: lambda s: s.activation(
                wkv_sb[:, dc * 2048:(dc + 1) * 2048], wkv_sb[:, dc * 2048:(dc + 1) * 2048],
                AF.Copy, scale=g1c_sb[:, dc:dc + 1]))(dc),
                waits=[P.upto(ld_wkv), cols_done] if dc == 0 else ())
        for dc in range(NDC):
            sfold = rec("s", (lambda dc: lambda s: s.activation(
                wq_sb[:, dc * 1024:(dc + 1) * 1024], wq_sb[:, dc * 1024:(dc + 1) * 1024],
                AF.Copy, scale=g1c_sb[:, dc:dc + 1]))(dc),
                waits=[P.upto(ld_wq)] if dc == 0 else ())
        for dc in range(NDC):
            sfold = rec("s", (lambda dc: lambda s: s.activation(
                brep_sb[:, dc * 128:(dc + 1) * 128], ones_sb[:],
                AF.Copy, scale=b1cf_sb[:, dc:dc + 1]))(dc),
                waits=[P.upto(ld_ones)] if dc == 0 else ())
        w_folded = P.upto(sfold)

        mm = None
        for dc in range(NDC):
            for bk_ in range(4):
                mm = rec("t", (lambda dc, bk_: lambda t: t.matmul(
                    bank(bk_, 1), brep_sb[:, dc * 128:(dc + 1) * 128],
                    wkv_sb[:, dc * 2048 + bk_ * 512: dc * 2048 + (bk_ + 1) * 512],
                    start=(dc == 0), stop=(dc == NDC - 1), skip_group_check=True))(dc, bk_),
                    waits=[w_folded] if (dc == 0 and bk_ == 0) else ())
        kvb_add = rec("v", lambda v: v.tensor_tensor(
            kvbias_b[:], kvbias_b[:], bank(0, 4), ALU.add),
            waits=[P.upto(mm), P.upto(ld_bias_tiles)])
        kvbias_done = P.upto(kvb_add)

        mm = None
        for mc in range(NDC):
            for dc in range(NDC):
                mm = rec("t", (lambda mc, dc: lambda t: t.matmul(
                    bank(4, 1)[:, mc:mc + 1],
                    wq_sb[:, dc * 1024 + mc * 128: dc * 1024 + (mc + 1) * 128],
                    b1c_sb[:, dc:dc + 1],
                    start=(dc == 0), stop=(dc == NDC - 1), skip_group_check=True))(mc, dc),
                    waits=[w_folded] if (mc == 0 and dc == 0) else ())
        bq1 = rec("v", lambda v: v.tensor_tensor(
            bqe_sb[:, 0:NDC], bank(4, 1)[:, 0:NDC], bqc_sb[:, 0:NDC], ALU.add),
            waits=[P.upto(mm)])
        bqe_done = P.upto(bq1)

        _maybe_stop("p0", bqe_done)

        last_xt_use = {}
        last_xnsrc_use = {}
        last_ptr_use = None
        last_kvuse = {}
        last_pk = None
        last_pv = None

        for t_i in range(NT_ALL):
            sl = t_i % 2
            own = t_i < NT_OWN
            wts = [cols_done]
            if t_i >= 2:
                wts.append(P.upto(last_xt_use[sl]))
            x0ld = rec("y", (lambda t_i, sl: lambda y: y.dma_start(
                xt_sb[sl][:], xb[t_i * 128:(t_i + 1) * 128, :]))(t_i, sl), waits=wts,
                dma=("xb0" if sl == 0 else "xb1"))
            st = st_sb[sl]; mv = mv_sb[sl]; r1 = r1_sb[sl]; rs = rs_sb[sl]
            rec("v", (lambda sl, st: lambda v: v.bn_stats(st[:, 0:6], xt_sb[sl][:, 0:512]))(sl, st),
                waits=[P.upto(x0ld)])
            o1 = rec("v", (lambda sl, st: lambda v: v.bn_stats(st[:, 6:12], xt_sb[sl][:, 512:1024]))(sl, st))
            o2 = rec("v", (lambda st, mv: lambda v: v.bn_aggr(mv[:, 0:2], st[:]))(st, mv),
                     waits=[P.upto(o1)])
            o3 = rec("v", (lambda mv, r1: lambda v: v.tensor_scalar(
                r1[:, 0:1], mv[:, 1:2], LN1EPS, None, ALU.add))(mv, r1), waits=[P.upto(o2)])
            o4 = rec("v", (lambda r1: lambda v: v.reciprocal(r1[:, 0:1], r1[:, 0:1]))(r1),
                     waits=[P.upto(o3)])
            sq = rec("s", (lambda r1, rs: lambda s: s.activation(rs[:, 0:1], r1[:, 0:1], AF.Sqrt))(r1, rs),
                     waits=[P.upto(o4)])
            wts = [P.upto(sq)]
            if t_i >= 2:
                wts.append(P.upto(last_xnsrc_use[sl]))
            ap_op = rec("v", (lambda sl, mv, rs: lambda v: v.tensor_scalar(
                xc_sb[sl][:], xt_sb[sl][:], mv[:, 0:1], rs[:, 0:1], ALU.subtract, ALU.mult))(sl, mv, rs),
                waits=wts)
            last_xt_use[sl] = ap_op
            wts = [P.upto(ap_op)]
            if last_ptr_use is not None:
                wts.append(P.upto(last_ptr_use))
            tr = None
            for k in range(NDC):
                tr = rec("t", (lambda sl, k: lambda t: t.transpose(
                    ptr_bf[:, k * 128:(k + 1) * 128], xc_sb[sl][:, k * 128:(k + 1) * 128],
                    identb_sb[:]))(sl, k),
                    waits=(wts + [P.upto(ld_ident)]) if k == 0 else ())
            if own:
                dstap = xnT_sb.ap().rearrange("p (c f) -> p c f", c=NDC)[:, :, t_i * 128:(t_i + 1) * 128]
                ev = rec("s", (lambda dstap: lambda s: s.activation(
                    dstap, ptr_bf.rearrange("p (c f) -> p c f", c=NDC), AF.Copy))(dstap),
                    waits=[P.upto(tr)])
            else:
                ev = rec("s", (lambda sl: lambda s: s.activation(
                    xc_sb[sl][:], ptr_bf[:], AF.Copy))(sl), waits=[P.upto(tr)])
            last_ptr_use = ev

            def mk_lhs(dc, own=own, t_i=t_i, sl=sl):
                if own:
                    return xnT_sb[:, dc * T_OWN + t_i * 128: dc * T_OWN + (t_i + 1) * 128]
                return xc_sb[sl][:, dc * 128:(dc + 1) * 128]

            wts_k = [P.upto(ev), w_folded]
            if t_i == 0:
                wts_k += [kvbias_done, bqe_done]
            if last_pk is not None:
                wts_k.append(P.upto(last_pk))
            mmk = None
            for dc in range(NDC):
                lhs_ap = mk_lhs(dc)
                for half in range(2):
                    mmk = rec("t", (lambda lhs_ap, dc, half: lambda t: t.matmul(
                        bank(half, 1), lhs_ap,
                        wkv_sb[:, dc * 2048 + half * 512: dc * 2048 + (half + 1) * 512],
                        start=(dc == 0), stop=(dc == NDC - 1), skip_group_check=True))(lhs_ap, dc, half),
                        waits=(wts_k if (dc == 0 and half == 0) else ()))
            wts_v = []
            if last_pv is not None:
                wts_v.append(P.upto(last_pv))
            mmv = None
            for dc in range(NDC):
                lhs_ap = mk_lhs(dc)
                for half in range(2):
                    mmv = rec("t", (lambda lhs_ap, dc, half: lambda t: t.matmul(
                        bank(2 + half, 1), lhs_ap,
                        wkv_sb[:, dc * 2048 + 1024 + half * 512: dc * 2048 + 1024 + (half + 1) * 512],
                        start=(dc == 0), stop=(dc == NDC - 1), skip_group_check=True))(lhs_ap, dc, half),
                        waits=(wts_v if (dc == 0 and half == 0) else ()))
            wts = [P.upto(mmk)]
            if t_i >= 2:
                wts.append(P.upto(last_kvuse[sl]))
            pk1 = rec("v", (lambda sl: lambda v: v.scalar_tensor_tensor(
                phik_sb[sl][:], bank(0, 2), KEPS, kvbias_b[:, 0:1024], ALU.add, ALU.add))(sl),
                waits=wts)
            vv = rec("v", (lambda sl: lambda v: v.tensor_tensor(
                vtok_sb[sl][:], bank(2, 2), kvbias_b[:, 1024:2048], ALU.add))(sl),
                waits=[P.upto(mmv)])
            pk2 = rec("v", (lambda sl: lambda v: v.tensor_scalar(
                phik_sb[sl][:], phik_sb[sl][:], KEPS, None, ALU.max))(sl),
                waits=[P.upto(pk1)])
            last_pk = pk2
            last_pv = vv
            kvmm = None
            wts = [P.upto(pk2), P.upto(vv)]
            for h in range(H):
                kvmm = rec("t", (lambda sl, h, t_i: lambda t: t.matmul(
                    kv_ps[:, h * 64:(h + 1) * 64], phik_sb[sl][:, h * 64:(h + 1) * 64],
                    vtok_sb[sl][:, h * 64:(h + 1) * 64],
                    start=(t_i == 0 and h in (0, 8)), stop=(t_i == NT_ALL - 1),
                    skip_group_check=True))(sl, h, t_i),
                    waits=(wts if h == 0 else ()))
                kvmm = rec("t", (lambda sl, h, t_i: lambda t: t.matmul(
                    ks_ps[:, h:h + 1], phik_sb[sl][:, h * 64:(h + 1) * 64], ones_sb[:, 0:1],
                    start=(t_i == 0 and h == 0), stop=(t_i == NT_ALL - 1),
                    skip_group_check=True))(sl, h, t_i))
            last_kvuse[sl] = kvmm
            last_xnsrc_use[sl] = tr if own else mmv

        kv_done = P.upto_last("t")

        kvp = kv_ps.rearrange("p (g two c) -> p g two c", two=2, c=64)
        ksp = ks_ps.rearrange("p (g two) -> p g two", two=2)
        blk_t = kv_blk.ap().rearrange("p (g c) -> p g c", c=136)
        zz = rec("v", lambda v: v.memset(kv_blk[:], 0.0), waits=[kv_done])
        zz2 = rec("v", lambda v: v.memset(kv_sb[:], 0.0))
        rec("v", lambda v: v.tensor_copy(blk_t[0:64, :, 0:64], kvp[:, :, 0, :]),
            waits=[P.upto(zz)])
        rec("v", lambda v: v.tensor_copy(blk_t[0:64, :, 128:129], ksp[:, :, 0:1]))
        kvr = kv_sb[0:64, :].rearrange("p (g c) -> p g c", c=72)
        rec("v", lambda v: v.tensor_copy(kvr[:, :, 0:64], kvp[:, :, 1, :]),
            waits=[P.upto(zz2)])
        ev_kv = rec("v", lambda v: v.tensor_copy(kvr[:, :, 64:65], ksp[:, :, 1:2]))
        st_done = P.upto(ev_kv)
        d1m = rec("g", lambda g: g.dma_start(
            blk_t[64:128, :, 64:128],
            kv_sb[0:64, :].rearrange("p (g c) -> p g c", c=72)[:, :, 0:64]),
            waits=[st_done], dma="wo")
        d2m = rec("g", lambda g: g.dma_start(
            blk_t[64:128, :, 129:130],
            kv_sb[0:64, :].rearrange("p (g c) -> p g c", c=72)[:, :, 64:65],
            allow_slow_non_contiguous=True),
            waits=[], dma="wo")
        kvsb_done = P.upto(d2m)

        tap_refs = {}
        _maybe_stop("A", kvsb_done)

        if "kvsb" in taps:
            tap_refs["kvsb"] = rec("g", lambda g: g.dma_start(tap_t["kvsb"], kv_blk[:]), waits=[kvsb_done], dma="tap_kvsb")

        qreg_use = {}
        qev = None
        for i in range(NDC * 2):
            mc, th = i // 2, i % 2
            reg = i % 3
            wts = []
            if i == 0:
                wts = [kv_done, kvsb_done, w_folded, bqe_done]
            if reg in qreg_use:
                wts.append(P.upto(qreg_use[reg]))
            mm = None
            for dc in range(NDC):
                for half in range(2):
                    mm = rec("t", (lambda mc, th, dc, half, reg: lambda t: t.matmul(
                        bank(reg * 2 + half, 1),
                        wq_sb[:, dc * 1024 + mc * 128: dc * 1024 + (mc + 1) * 128],
                        xnT_sb[:, dc * T_OWN + th * 1024 + half * 512: dc * T_OWN + th * 1024 + (half + 1) * 512],
                        start=(dc == 0), stop=(dc == NDC - 1), skip_group_check=True))(mc, th, dc, half, reg),
                        waits=(wts if (dc == 0 and half == 0) else ()))
            qev = rec("v", (lambda mc, th, reg: lambda v: v.tensor_scalar(
                phiq_sb[:, mc * T_OWN + th * 1024: mc * T_OWN + (th + 1) * 1024],
                bank(reg * 2, 2), bqe_sb[:, mc:mc + 1], KEPS, ALU.add, ALU.max))(mc, th, reg),
                waits=[P.upto(mm)])
            qreg_use[reg] = qev
        phiq_done = P.upto(qev)

        if "xnT" in taps:
            tap_refs["xnT"] = rec("g", lambda g: g.dma_start(tap_t["xnT"], xnT_sb[:]), waits=[phiq_done], dma="tap_xnT")
        if "phiq" in taps:
            tap_refs["phiq"] = rec("g", lambda g: g.dma_start(tap_t["phiq"], phiq_sb[:]), waits=[phiq_done], dma="tap_phiq")

        _maybe_stop("q", phiq_done)

        num_reg_use = {}
        recip_of = {}
        mm_of = {}
        mul_last = None

        def emit_attn_muls(tc):
            nonlocal mul_last
            par = tc % 2
            evl = None
            for h in range(H):
                w = [P.upto(recip_of[tc]), tap_refs.get("xnT")] if h == 0 else ()
                evl = rec("v", (lambda tc, par, h: lambda v: v.tensor_scalar(
                    attn_sb[:, tc * 1024 + h * 64: tc * 1024 + (h + 1) * 64],
                    bank(par * 2, 2)[:, h * 64:(h + 1) * 64],
                    rd_sb[par][:, h:h + 1], None, ALU.mult))(tc, par, h), waits=w)
            num_reg_use[par] = evl
            mul_last = evl

        for tc in range(NT_OWN):
            par = tc % 2
            wts = [kvsb_done, phiq_done, tap_refs.get("xnT")]
            if par in num_reg_use:
                wts.append(P.upto(num_reg_use[par]))
            mm = None
            for mc in range(8):
                lhs = phiq_sb[:, mc * T_OWN + tc * 128: mc * T_OWN + (tc + 1) * 128]
                rkv = kv_blk[:, mc * 136: mc * 136 + 128]
                rks = kv_blk[:, mc * 136 + 128: mc * 136 + 130]
                mm = rec("t", (lambda lhs, rkv, par, mc: lambda t: t.matmul(
                    bank(par * 2, 2)[:, mc * 128:(mc + 1) * 128], lhs, rkv,
                    start=True, stop=True, skip_group_check=True))(lhs, rkv, par, mc),
                    waits=(wts if mc == 0 else ()))
                mm = rec("t", (lambda lhs, rks, par, mc: lambda t: t.matmul(
                    bank(4 + par, 1)[:, 2 * mc: 2 * mc + 2], lhs, rks,
                    start=True, stop=True, skip_group_check=True))(lhs, rks, par, mc))
            mm_of[tc] = mm
            recip_of[tc] = rec("v", (lambda par: lambda v: v.reciprocal(
                rd_sb[par][:], bank(4 + par, 1)[:, 0:16]))(par),
                waits=[P.upto(mm)])
            if tc >= 1:
                emit_attn_muls(tc - 1)
        emit_attn_muls(NT_OWN - 1)
        attn_done = P.upto(mul_last)

        if "attn" in taps:
            tap_refs["attn"] = rec("g", lambda g: g.dma_start(tap_t["attn"], attn_sb[:]), waits=[attn_done], dma="tap_attn")

        _maybe_stop("num", attn_done)

        ld_wo = rec("g", lambda g: g.dma_start(
            wo_sb.ap().rearrange("p (c f) -> p c f", c=NDC),
            wo_d.rearrange("(c p) f -> p c f", p=128)), waits=[kv_done], dma="wol")
        ld_bo = rec("y", lambda y: y.dma_start(
            bo_b[:], bo_d[None, :].broadcast_to([128, 1024])), waits=[kv_done], dma="bo")
        ld_b2b = rec("y", lambda y: y.dma_start(
            b2_b[:], b2_d[None, :].broadcast_to([128, 1024])), waits=[P.upto_last("v")], dma="b2b")

        x0_use = {}
        o_reg_use = {}
        trev = None
        trev_of = {}
        e2 = None
        for tc in range(NT_OWN):
            sl3 = tc % 3
            par = tc % 2
            wts = [attn_done]
            if sl3 in x0_use:
                wts.append(P.upto(x0_use[sl3]))
            x0ld = rec("y", (lambda tc, sl3: lambda y: y.dma_start(
                x0s_sb[sl3][:], xb[tc * 128:(tc + 1) * 128, :]))(tc, sl3), waits=wts,
                dma=("x00", "x01", "x02")[sl3])
            wts = [attn_done]
            if (tc % 2) in trev_of:
                wts.append(P.upto(trev_of[tc % 2]))
            pb_ = ptr_of[tc % 2]
            tr = None
            for k in range(NDC):
                tr = rec("t", (lambda tc, k, pb_: lambda t: t.transpose(
                    pb_[:, k * 128:(k + 1) * 128],
                    attn_sb[:, tc * 1024 + k * 128: tc * 1024 + (k + 1) * 128],
                    identb_sb[:]))(tc, k, pb_), waits=(wts if k == 0 else ()))
            ev = rec("s", (lambda tc, pb_: lambda s: s.activation(
                attnT_sb.ap().rearrange("p (c f) -> p c f", c=NDC)[:, :, tc * 128:(tc + 1) * 128],
                pb_.rearrange("p (c f) -> p c f", c=NDC), AF.Copy))(tc, pb_),
                waits=[P.upto(tr), tap_refs.get("phiq")])
            trev_of[tc % 2] = ev
            trev = ev
            wts = [P.upto(ev), P.upto(ld_wo)]
            if par in o_reg_use:
                wts.append(P.upto(o_reg_use[par]))
            mm = None
            for fc in range(NDC):
                for half in range(2):
                    mm = rec("t", (lambda tc, fc, half, par: lambda t: t.matmul(
                        bank(par * 2 + half, 1),
                        attnT_sb[:, fc * T_OWN + tc * 128: fc * T_OWN + (tc + 1) * 128],
                        wo_sb[:, fc * 1024 + half * 512: fc * 1024 + (half + 1) * 512],
                        start=(fc == 0), stop=(fc == NDC - 1), skip_group_check=True))(tc, fc, half, par),
                        waits=(wts if (fc == 0 and half == 0) else ()))
            e1 = rec("v", (lambda tc, par, sl3: lambda v: v.tensor_tensor(
                xres_sb[:, tc * 1024:(tc + 1) * 1024], bank(par * 2, 2), x0s_sb[sl3][:], ALU.add))(tc, par, sl3),
                waits=[P.upto(mm), P.upto(x0ld), P.upto(ld_bo)])
            e2 = rec("v", (lambda tc: lambda v: v.tensor_tensor(
                xres_sb[:, tc * 1024:(tc + 1) * 1024], xres_sb[:, tc * 1024:(tc + 1) * 1024],
                bo_b[:], ALU.add))(tc), waits=[P.upto(e1)])
            o_reg_use[par] = e2
            x0_use[sl3] = e1
        xres_done = P.upto(e2)

        if "xres" in taps:
            tap_refs["xres"] = rec("g", lambda g: g.dma_start(tap_t["xres"], xres_sb[:]), waits=[xres_done], dma="tap_xres")

        _maybe_stop("o", xres_done)

        y2xc_use = {}
        h1_done_of = {}
        w2ld_of = {}
        h1reg_use = {}
        yreg_use = {}
        w1buf_use = {}
        w2buf_use = {}
        h1buf_use = {}
        yacc_of = {}
        pinslot_use = {}
        poolo_use = {}
        x0b_use = {}
        ln2_tr_prev = None
        ln2_tr_of = {}
        xa_of = {}
        x1_last = None

        for th in range(1):
            ln2_ev = None
            for ti in range(16):
                tc = ti
                sl = ti % 2
                st = st_sb[sl]; mv = mv_sb[sl]; r1 = r1_sb[sl]; rs = rs_sb[sl]
                xr = xres_sb[:, tc * 1024:(tc + 1) * 1024]
                rec("v", (lambda st, xr: lambda v: v.bn_stats(st[:, 0:6], xr[:, 0:512]))(st, xr),
                    waits=[xres_done, tap_refs.get("attn")])
                o1 = rec("v", (lambda st, xr: lambda v: v.bn_stats(st[:, 6:12], xr[:, 512:1024]))(st, xr))
                o2 = rec("v", (lambda st, mv: lambda v: v.bn_aggr(mv[:, 0:2], st[:]))(st, mv),
                         waits=[P.upto(o1)])
                o3 = rec("v", (lambda mv, r1: lambda v: v.tensor_scalar(
                    r1[:, 0:1], mv[:, 1:2], LN1EPS, None, ALU.add))(mv, r1), waits=[P.upto(o2)])
                o4 = rec("v", (lambda r1: lambda v: v.reciprocal(r1[:, 0:1], r1[:, 0:1]))(r1),
                         waits=[P.upto(o3)])
                sq = rec("s", (lambda r1, rs: lambda s: s.activation(rs[:, 0:1], r1[:, 0:1], AF.Sqrt))(r1, rs),
                         waits=[P.upto(o4)])
                wts = [P.upto(sq)]
                if sl in y2xc_use:
                    wts.append(P.upto(y2xc_use[sl]))
                ap_op = rec("v", (lambda sl, mv, rs, xr: lambda v: v.tensor_scalar(
                    xc_sb[sl][:], xr, mv[:, 0:1], rs[:, 0:1], ALU.subtract, ALU.mult))(sl, mv, rs, xr),
                    waits=wts)
                wts = [P.upto(ap_op)]
                if (ti % 2) in ln2_tr_of:
                    wts.append(P.upto(ln2_tr_of[ti % 2]))
                pb_ = ptr_of[ti % 2]
                tr = None
                for k in range(NDC):
                    tr = rec("t", (lambda sl, k, pb_: lambda t: t.transpose(
                        pb_[:, k * 128:(k + 1) * 128], xc_sb[sl][:, k * 128:(k + 1) * 128],
                        identb_sb[:]))(sl, k, pb_), waits=(wts if k == 0 else ()))
                y2xc_use[sl] = tr
                evl = None
                for k in range(NDC):
                    evl = rec("s", (lambda ti, k, pb_: lambda s: s.activation(
                        y2T_sb[:, k * 2048 + ti * 128: k * 2048 + (ti + 1) * 128],
                        pb_[:, k * 128:(k + 1) * 128], AF.Identity,
                        bias=b2c_sb[:, k:k + 1], scale=g2c_sb[:, k:k + 1]))(ti, k, pb_),
                        waits=[P.upto(tr)] if k == 0 else ())
                ln2_tr_of[ti % 2] = evl
                ln2_tr_prev = evl
            y2_done = P.upto(ln2_tr_prev)

            if th == 0 and "y2T" in taps:
                rec("g", lambda g: g.dma_start(tap_t["y2T"], y2T_sb[:]), waits=[y2_done], dma="tap_y2T")

            for p in range(16):
                pp = p % 2
                wts = [] if (th or p) else [xres_done]
                if pp in w1buf_use:
                    wts.append(P.upto(w1buf_use[pp]))
                ld1 = rec("g", (lambda p, pp: lambda g: g.dma_start(
                    w1p_sb[pp].ap().rearrange("q (c f) -> q c f", c=NDC),
                    w1_d.rearrange("(c q) f -> q c f", q=128)[:, :, p * 256:(p + 1) * 256]))(p, pp),
                    waits=wts, dma=("w1a" if pp == 0 else "w1b"))
                wts = [] if (th or p) else [xres_done]
                if pp in w2buf_use:
                    wts.append(P.upto(w2buf_use[pp]))
                ld2 = rec("g", (lambda p, pp: lambda g: g.dma_start(
                    w2p_sb[pp].ap().rearrange("q (c f) -> q c f", c=2),
                    w2_d.rearrange("(c q) f -> q c f", q=128)[:, 2 * p:2 * p + 2, :]))(p, pp),
                    waits=wts, dma=("w2a" if pp == 0 else "w2b"))
                hev = None
                for fc in range(2):
                    for tb in range(2):
                        reg = (2 * fc + tb) % 2
                        wts = [P.upto(ld1), y2_done]
                        if reg in h1reg_use:
                            wts.append(P.upto(h1reg_use[reg]))
                        if pp in h1buf_use:
                            wts.append(P.upto(h1buf_use[pp]))
                        mm = None
                        for dc in range(NDC):
                            for half in range(2):
                                mm = rec("t", (lambda p, pp, fc, dc, half, reg, tb: lambda t: t.matmul(
                                    bank(reg * 2 + half, 1),
                                    w1p_sb[pp][:, dc * 256 + fc * 128: dc * 256 + (fc + 1) * 128],
                                    y2T_sb[:, dc * 2048 + tb * 1024 + half * 512: dc * 2048 + tb * 1024 + (half + 1) * 512],
                                    start=(dc == 0), stop=(dc == NDC - 1), skip_group_check=True))(p, pp, fc, dc, half, reg, tb),
                                    waits=(wts if (dc == 0 and half == 0) else ()))
                        hev = rec("s", (lambda p, pp, fc, tb, reg: lambda s: s.activation(
                            h1T_sb[pp][:, fc * 2048 + tb * 1024: fc * 2048 + (tb + 1) * 1024],
                            bank(reg * 2, 2),
                            AF.Gelu_apprx_tanh, bias=b1mlp_sb[:, p * 2 + fc: p * 2 + fc + 1]))(p, pp, fc, tb, reg),
                            waits=[P.upto(mm)])
                        h1reg_use[reg] = hev
                w1buf_use[pp] = mm
                h1_done_of[pp] = P.upto(hev)
                w2ld_of[pp] = ld2
                if pp == 0:
                    continue
                ya = None
                for ti in range(16):
                    tc = ti
                    yreg = 2 + (ti % 2)
                    wts = [h1_done_of[0], h1_done_of[1], P.upto(w2ld_of[0]), P.upto(w2ld_of[1])]
                    if yreg in yreg_use:
                        wts.append(P.upto(yreg_use[yreg]))
                    mm = None
                    for qq in range(2):
                        for fc in range(2):
                            for half in range(2):
                                mm = rec("t", (lambda qq, ti, fc, half, yreg: lambda t: t.matmul(
                                    bank(yreg * 2 + half, 1),
                                    h1T_sb[qq][:, fc * 2048 + ti * 128: fc * 2048 + (ti + 1) * 128],
                                    w2p_sb[qq][:, fc * 1024 + half * 512: fc * 1024 + (half + 1) * 512],
                                    start=(qq == 0 and fc == 0), stop=(qq == 1 and fc == 1),
                                    skip_group_check=True))(qq, ti, fc, half, yreg),
                                    waits=(wts if (qq == 0 and fc == 0 and half == 0) else ()))
                    wts = [P.upto(mm), tap_refs.get("xres")]
                    if tc in yacc_of:
                        wts.append(P.upto(yacc_of[tc]))
                    ya = rec("v", (lambda tc, yreg: lambda v: v.scalar_tensor_tensor(
                        xres_sb[:, tc * 1024:(tc + 1) * 1024], bank(yreg * 2, 2), 1.0,
                        xres_sb[:, tc * 1024:(tc + 1) * 1024], ALU.mult, ALU.add))(tc, yreg),
                        waits=wts)
                    yacc_of[tc] = ya
                    yreg_use[yreg] = ya
                for _b in (0, 1):
                    w2buf_use[_b] = mm
                    h1buf_use[_b] = mm

        ptr5_bf = ps[:, 5 * 512:6 * 512].bitcast(BF16)
        gw_ev_tr = None
        for k in range(NDC):
            gw_ev_tr = rec("t", (lambda k: lambda t: t.transpose(
                ptr5_bf[:, k * 14:(k + 1) * 14], gwraw_sb[0:E_EXP, k * 128:(k + 1) * 128],
                identb_sb[0:E_EXP, 0:E_EXP]))(k),
                waits=[P.upto(yacc_of[14]), P.upto(ld_gw)] if k == 0 else ())
        gw_ev = rec("v", lambda v: v.tensor_copy(
            gwt_sb[:, 0:NDC * E_EXP], ptr5_bf[:, 0:NDC * E_EXP]), waits=[P.upto(gw_ev_tr)])
        gwt_done = P.upto(gw_ev)
        ld_mb1 = rec("y", lambda y: y.dma_start(mb1raw_sb[:], mb1_d),
                     waits=[P.upto(h1buf_use[1])], dma="mb1")
        mmtr = None
        for fc in range(16):
            mmtr = rec("t", (lambda fc: lambda t: t.transpose(
                ps[0:128, 5 * 512 + 256 + fc * 14: 5 * 512 + 256 + (fc + 1) * 14],
                mb1raw_sb[0:E_EXP, fc * 128:(fc + 1) * 128], identf_sb[0:E_EXP, 0:E_EXP]))(fc),
                waits=[P.upto(ld_mb1)] if fc == 0 else ())
        b1m_ev = rec("v", lambda v: v.tensor_copy(
            b1m_sb[:, 0:224], ps[0:128, 5 * 512 + 256: 5 * 512 + 256 + 224]),
            waits=[P.upto(mmtr)])
        b1m_done = P.upto(b1m_ev)

        x1c_use = {}
        x1t_tr_of = {}
        x1tev_of = {}
        greg_use = {}
        gate_mm_of = {}
        mk_last = None

        def rec_mask_chain(tc):
            par = tc % 2
            lg = bank(4 + par, 1)[:, 0:E_EXP]
            d1 = rec("v", (lambda lg: lambda v: v.tensor_reduce(m1_sb[:, 0:1], lg, AX.X, ALU.max))(lg),
                waits=[P.upto(gate_mm_of[tc])])
            rec("v", (lambda lg: lambda v: v.tensor_copy(lm_sb[:], lg))(lg))
            d2 = rec("v", (lambda lg: lambda v: v.tensor_scalar(
                eq1_sb[:], lg, m1_sb[:, 0:1], None, ALU.is_equal))(lg),
                waits=[P.upto(d1)])
            d3 = rec("v", lambda v: v.scalar_tensor_tensor(
                lm_sb[:], eq1_sb[:], -1e30, lm_sb[:], ALU.mult, ALU.add), waits=[P.upto(d2)])
            d4 = rec("v", lambda v: v.tensor_reduce(m2_sb[:, 0:1], lm_sb[:], AX.X, ALU.max),
                waits=[P.upto(d3)])
            d5 = rec("v", lambda v: v.tensor_tensor(w1w_sb[:, 0:1], m2_sb[:, 0:1], m1_sb[:, 0:1], ALU.subtract),
                waits=[P.upto(d4)])
            d6 = rec("v", (lambda lg: lambda v: v.tensor_scalar(
                eq2_sb[:], lm_sb[:], m2_sb[:, 0:1], None, ALU.is_equal))(lg))
            sg = rec("s", lambda s: s.activation(w0_sb[:, 0:1], w1w_sb[:, 0:1], AF.Sigmoid),
                     waits=[P.upto(d6)])
            d8 = rec("v", lambda v: v.tensor_scalar(w1w_sb[:, 0:1], w0_sb[:, 0:1], -1.0, 1.0, ALU.mult, ALU.add),
                waits=[P.upto(sg)])
            rec("v", lambda v: v.tensor_copy(lm_sb[:], eq1_sb[:]))
            d7 = rec("v", (lambda tc: lambda v: v.tensor_scalar(
                mask_sb[:, tc * E_EXP:(tc + 1) * E_EXP], eq1_sb[:], w1w_sb[:, 0:1], None,
                ALU.mult))(tc), waits=[P.upto(d8)])
            mk = rec("v", (lambda tc: lambda v: v.scalar_tensor_tensor(
                mask_sb[:, tc * E_EXP:(tc + 1) * E_EXP], eq2_sb[:], w0_sb[:, 0:1],
                mask_sb[:, tc * E_EXP:(tc + 1) * E_EXP], ALU.mult, ALU.add))(tc),
                waits=[P.upto(d7)])
            mtr = rec("t", (lambda tc: lambda t: t.transpose(
                ps[0:E_EXP, (6 + tc % 2) * 512: (6 + tc % 2) * 512 + 128],
                mask_sb[:, tc * E_EXP:(tc + 1) * E_EXP], identf_sb[:]))(tc),
                waits=[P.upto(mk)])
            mk2 = rec("v", (lambda tc: lambda v: v.tensor_copy(
                maskT_sb[0:E_EXP, tc * 128:(tc + 1) * 128],
                ps[0:E_EXP, (6 + tc % 2) * 512: (6 + tc % 2) * 512 + 128]))(tc),
                waits=[P.upto(mtr)])
            greg_use[par] = mk2
            return mk2

        def rec_finish(tc):
            slp = tc % 2
            sl4 = tc % 4
            xa = rec("v", (lambda tc, slp, sl4: lambda v: v.tensor_scalar(
                xres_sb[:, tc * 1024:(tc + 1) * 1024], upool_sb[slp][:],
                mu_sb[sl4][:, 0:1], rs_sb[sl4][:, 0:1], ALU.subtract, ALU.mult))(tc, slp, sl4),
                waits=[P.upto(sq2_of[tc])])
            pinslot_use[slp] = xa
            xa_of[tc] = xa
            sl = tc % 2
            wts = [P.upto(xa)]
            if sl in x1c_use:
                wts.append(P.upto(x1c_use[sl]))
            cp = rec("s", (lambda tc, sl: lambda s: s.activation(
                xc_sb[sl][:], xres_sb[:, tc * 1024:(tc + 1) * 1024], AF.Copy))(tc, sl),
                waits=wts)
            wts = [P.upto(cp)]
            if (tc % 2) in x1t_tr_of:
                wts.append(P.upto(x1t_tr_of[tc % 2]))
            pb_ = ptr_of[tc % 2]
            tr = None
            for k in range(NDC):
                tr = rec("t", (lambda sl, k, pb_: lambda t: t.transpose(
                    pb_[:, k * 128:(k + 1) * 128], xc_sb[sl][:, k * 128:(k + 1) * 128],
                    identb_sb[:]))(sl, k, pb_), waits=(wts if k == 0 else ()))
            x1c_use[sl] = tr
            ev = rec("s", (lambda tc, pb_: lambda s: s.activation(
                x1T_sb.ap().rearrange("p (c f) -> p c f", c=NDC)[:, :, tc * 128:(tc + 1) * 128],
                pb_.rearrange("p (c f) -> p c f", c=NDC), AF.Copy))(tc, pb_),
                waits=[P.upto(tr)])
            x1t_tr_of[tc % 2] = ev
            x1tev_of[tc] = ev
            par = tc % 2
            wts = [P.upto(ev), gwt_done]
            if par in greg_use:
                wts.append(P.upto(greg_use[par]))
            mm = None
            for dc in range(NDC):
                mm = rec("t", (lambda tc, dc, par: lambda t: t.matmul(
                    bank(4 + par, 1)[:, 0:E_EXP],
                    x1T_sb[:, dc * T_OWN + tc * 128: dc * T_OWN + (tc + 1) * 128],
                    gwt_sb[:, dc * E_EXP:(dc + 1) * E_EXP],
                    start=(dc == 0), stop=(dc == NDC - 1), skip_group_check=True))(tc, dc, par),
                    waits=(wts if dc == 0 else ()))
            gate_mm_of[tc] = mm

        sq2_of = {}
        for tc in range(NT_OWN):
            ti = tc
            sl3 = tc % 3
            slp = ti % 2
            sl4 = ti % 4
            wts = [P.upto(yacc_of[tc])]
            if sl3 in x0b_use:
                wts.append(P.upto(x0b_use[sl3]))
            x0ld = rec("y", (lambda tc, sl3: lambda y: y.dma_start(
                x0s_sb[sl3][:], xb[tc * 128:(tc + 1) * 128, :]))(tc, sl3), waits=wts,
                dma=("x00", "x01", "x02")[sl3])
            wts = [P.upto(x0ld), P.upto(ld_b2b)]
            if slp in pinslot_use:
                wts.append(P.upto(pinslot_use[slp]))
            a1 = rec("v", (lambda tc, sl3, slp: lambda v: v.tensor_tensor(
                upool_sb[slp][:], xres_sb[:, tc * 1024:(tc + 1) * 1024], x0s_sb[sl3][:],
                ALU.add))(tc, sl3, slp), waits=wts)
            x0b_use[sl3] = a1
            a2 = rec("v", (lambda slp, sl4: lambda v: v.scalar_tensor_tensor(
                upool_sb[slp][:], upool_sb[slp][:], 1.0, b2_b[:], ALU.mult, ALU.add,
                accum_out=sums_sb[sl4][:, 0:1]))(slp, sl4), waits=[P.upto(a1)])
            sqr = rec("s", (lambda slp, sl4: lambda s: s.activation(
                junk_sb[:], upool_sb[slp][:], AF.Square, accum_out=ssq_sb[sl4][:, 0:1]))(slp, sl4),
                waits=[P.upto(a2)])
            rec("v", (lambda sl4: lambda v: v.tensor_scalar(
                mu_sb[sl4][:, 0:1], sums_sb[sl4][:, 0:1], 1.0 / D, None, ALU.mult))(sl4),
                waits=[P.upto(sqr)])
            tpo = rec("v", (lambda sl4: lambda v: v.tensor_scalar(
                tp1_sb[sl4][:, 0:1], ssq_sb[sl4][:, 0:1], 1.0 / D, None, ALU.mult))(sl4))
            c1 = rec("v", (lambda sl4: lambda v: v.tensor_tensor(
                r1_sb[sl4][:, 0:1], mu_sb[sl4][:, 0:1], mu_sb[sl4][:, 0:1], ALU.mult))(sl4),
                waits=[P.upto(tpo)])
            c2 = rec("v", (lambda sl4: lambda v: v.tensor_tensor(
                r1_sb[sl4][:, 0:1], tp1_sb[sl4][:, 0:1], r1_sb[sl4][:, 0:1], ALU.subtract))(sl4),
                waits=[P.upto(c1)])
            c3 = rec("v", (lambda sl4: lambda v: v.tensor_scalar(
                r1_sb[sl4][:, 0:1], r1_sb[sl4][:, 0:1], PLNEPS, None, ALU.add))(sl4),
                waits=[P.upto(c2)])
            c4 = rec("v", (lambda sl4: lambda v: v.reciprocal(r1_sb[sl4][:, 0:1], r1_sb[sl4][:, 0:1]))(sl4),
                waits=[P.upto(c3)])
            sq2 = rec("s", (lambda sl4: lambda s: s.activation(
                rs_sb[sl4][:, 0:1], r1_sb[sl4][:, 0:1], AF.Sqrt))(sl4), waits=[P.upto(c4)])
            sq2_of[tc] = sq2
            if tc >= 1:
                rec_finish(tc - 1)
            if tc >= 2:
                mk_last = rec_mask_chain(tc - 2)
        rec_finish(15)
        mk_last = rec_mask_chain(14)
        mk_last = rec_mask_chain(15)
        x1_last = xa_of[15]
        x1_done = P.upto(x1_last)
        x1T_done = P.upto(x1tev_of[15])
        mask_done = P.upto(mk_last)

        if "x1" in taps:
            tap_refs["x1"] = rec("g", lambda g: g.dma_start(tap_t["x1"], xres_sb[:]), waits=[x1_done], dma="tap_x1")
        if "mask" in taps:
            tap_refs["mask"] = rec("g", lambda g: g.dma_start(tap_t["mask"], mask_sb[:]), waits=[mask_done], dma="tap_mask")

        _maybe_stop("mlp", x1_done)
        _maybe_stop("gate", mask_done)

        accw = {}
        outreg_use = {}
        hhreg_use = {}
        w1buf2_use = {}
        w2buf2_of = {}
        hhslot_use = {}
        pc_use = None
        store_last = None
        interleave_pl2 = "acc0" not in taps

        def rec_postln2(g_i, ti):
            nonlocal pc_use, store_last
            tc = g_i * 8 + ti
            sl4 = ti % 4
            x1t = xres_sb[:, tc * 1024:(tc + 1) * 1024]
            wts = [P.upto(accw[ti])]
            if pc_use is not None:
                wts.append(P.upto(pc_use))
            wts.append(tap_refs.get("acc0"))
            f1 = rec("v", (lambda ti, x1t: lambda v: v.tensor_tensor(
                pool_c[:], x1t, acc_sb[:, ti * 1024:(ti + 1) * 1024], ALU.add))(ti, x1t),
                waits=wts)
            f2 = rec("v", lambda v: v.tensor_scalar(
                pool_c[:], pool_c[:], 100.0, -100.0, ALU.min, ALU.max),
                waits=[P.upto(f1)])
            sa = rec("v", (lambda ti, x1t, sl4: lambda v: v.scalar_tensor_tensor(
                acc_sb[:, ti * 1024:(ti + 1) * 1024], pool_c[:], 1.0, x1t, ALU.mult, ALU.add,
                accum_out=sums_sb[sl4][:, 0:1]))(ti, x1t, sl4), waits=[P.upto(f2)])
            pc_use = sa
            sqr = rec("s", (lambda ti, sl4: lambda s: s.activation(
                junk_sb[:], acc_sb[:, ti * 1024:(ti + 1) * 1024], AF.Square,
                accum_out=ssq_sb[sl4][:, 0:1]))(ti, sl4), waits=[P.upto(sa)])
            rec("v", (lambda sl4: lambda v: v.tensor_scalar(
                mu_sb[sl4][:, 0:1], sums_sb[sl4][:, 0:1], 1.0 / D, None, ALU.mult))(sl4),
                waits=[P.upto(sqr)])
            tpo = rec("v", (lambda sl4: lambda v: v.tensor_scalar(
                tp1_sb[sl4][:, 0:1], ssq_sb[sl4][:, 0:1], 1.0 / D, None, ALU.mult))(sl4))
            c1 = rec("v", (lambda sl4: lambda v: v.tensor_tensor(
                r1_sb[sl4][:, 0:1], mu_sb[sl4][:, 0:1], mu_sb[sl4][:, 0:1], ALU.mult))(sl4),
                waits=[P.upto(tpo)])
            c2 = rec("v", (lambda sl4: lambda v: v.tensor_tensor(
                r1_sb[sl4][:, 0:1], tp1_sb[sl4][:, 0:1], r1_sb[sl4][:, 0:1], ALU.subtract))(sl4),
                waits=[P.upto(c1)])
            c3 = rec("v", (lambda sl4: lambda v: v.tensor_scalar(
                r1_sb[sl4][:, 0:1], r1_sb[sl4][:, 0:1], PLNEPS, None, ALU.add))(sl4),
                waits=[P.upto(c2)])
            c4 = rec("v", (lambda sl4: lambda v: v.reciprocal(r1_sb[sl4][:, 0:1], r1_sb[sl4][:, 0:1]))(sl4),
                waits=[P.upto(c3)])
            sq2 = rec("s", (lambda sl4: lambda s: s.activation(
                rs_sb[sl4][:, 0:1], r1_sb[sl4][:, 0:1], AF.Sqrt))(sl4), waits=[P.upto(c4)])
            oa = rec("v", (lambda tc, ti, sl4: lambda v: v.tensor_scalar(
                xres_sb[:, tc * 1024:(tc + 1) * 1024], acc_sb[:, ti * 1024:(ti + 1) * 1024],
                mu_sb[sl4][:, 0:1], rs_sb[sl4][:, 0:1], ALU.subtract, ALU.mult))(tc, ti, sl4),
                waits=[P.upto(sq2), tap_refs.get("x1")])
            st_op = rec("y", (lambda tc: lambda y: y.dma_start(
                out_d[tc * 128:(tc + 1) * 128, :],
                xres_sb[:, tc * 1024:(tc + 1) * 1024]))(tc),
                waits=[P.upto(oa)], dma="st")
            store_last = st_op

        for g_i in range(2):
            for ti in range(8):
                tc = g_i * 8 + ti
                yreg = 2 + (ti % 2)
                wts = [mask_done, P.upto(ld_b2m)]
                if yreg in outreg_use:
                    wts.append(P.upto(outreg_use[yreg]))
                mm = None
                for half in range(2):
                    mm = rec("t", (lambda tc, half, yreg: lambda t: t.matmul(
                        bank(yreg * 2 + half, 1),
                        maskT_sb[0:E_EXP, tc * 128:(tc + 1) * 128],
                        b2m_sb[0:E_EXP, half * 512:(half + 1) * 512],
                        start=True, stop=True, skip_group_check=True))(tc, half, yreg),
                        waits=(wts if half == 0 else ()))
                wts = [P.upto(mm)]
                if ti in accw:
                    wts.append(P.upto(accw[ti]))
                av = rec("v", (lambda ti, yreg: lambda v: v.tensor_copy(
                    acc_sb[:, ti * 1024:(ti + 1) * 1024], bank(yreg * 2, 2)))(ti, yreg),
                    waits=wts)
                accw[ti] = av
                outreg_use[yreg] = av

            for e_i in range(E_EXP):
                for q4 in range(4):
                    hh_done_of = {}
                    ld2_of = {}
                    for hs in range(2):
                        q8 = q4 * 2 + hs
                        u = hs
                        wts = [P.upto(w1buf_use[u])]
                        if u in w1buf2_use:
                            wts.append(P.upto(w1buf2_use[u]))
                        ld1 = rec("g", (lambda e_i, q8, u: lambda g: g.dma_start(
                            mw1_sb[u].ap().rearrange("p (c f) -> p c f", c=NDC),
                            mw1_d[e_i].rearrange("(c p) f -> p c f", p=128)[:, :, q8 * 256:(q8 + 1) * 256]))(e_i, q8, u),
                            waits=wts, dma=("m1a" if u == 0 else "m1b"))
                        pr = (e_i * 4 + q4) % 2
                        wts = []
                        if hs == 0:
                            wts = [P.upto(w2buf_use[0])] if pr not in w2buf2_of else [P.upto(w2buf2_of[pr])]
                        ld2_of[hs] = rec("g", (lambda e_i, q8, hs, pr: lambda g: g.dma_start(
                            mw2_sb[pr].ap().rearrange("p (hs c f) -> p hs c f", hs=2, c=2)[:, hs],
                            mw2_d[e_i].rearrange("(c p) f -> p c f", p=128)[:, 2 * q8:2 * q8 + 2, :]))(e_i, q8, hs, pr),
                            waits=wts, dma=("m2" if pr == 0 else "m2b"))
                        hev = None
                        mm = None
                        for fc in range(2):
                            reg = fc
                            wts = [P.upto(ld1), b1m_done]
                            if reg in hhreg_use:
                                wts.append(P.upto(hhreg_use[reg]))
                            if hs in hhslot_use:
                                wts.append(P.upto(hhslot_use[hs]))
                            for dc in range(NDC):
                                for half in range(2):
                                    mm = rec("t", (lambda g_i, u, fc, dc, half, reg: lambda t: t.matmul(
                                        bank(reg * 2 + half, 1),
                                        mw1_sb[u][:, dc * 256 + fc * 128: dc * 256 + (fc + 1) * 128],
                                        x1T_sb[:, dc * T_OWN + g_i * 1024 + half * 512: dc * T_OWN + g_i * 1024 + (half + 1) * 512],
                                        start=(dc == 0), stop=(dc == NDC - 1), skip_group_check=True))(g_i, u, fc, dc, half, reg),
                                        waits=(wts if (dc == 0 and half == 0) else ()))
                            hev = rec("s", (lambda e_i, q8, fc, hs, reg: lambda s: s.activation(
                                hhT_sb[:, (2 * hs + fc) * 1024:(2 * hs + fc + 1) * 1024],
                                bank(reg * 2, 2), AF.Gelu,
                                bias=b1m_sb[:, (q8 * 2 + fc) * E_EXP + e_i: (q8 * 2 + fc) * E_EXP + e_i + 1]))(e_i, q8, fc, hs, reg),
                                waits=[P.upto(mm)])
                            hhreg_use[reg] = hev
                        w1buf2_use[u] = mm
                        hh_done_of[hs] = P.upto(hev)
                    mm = None
                    av = None
                    for ti in range(8):
                        tc = g_i * 8 + ti
                        yreg = 2 + (ti % 2)
                        wts = [hh_done_of[0], hh_done_of[1], P.upto(ld2_of[1])]
                        if yreg in outreg_use:
                            wts.append(P.upto(outreg_use[yreg]))
                        for hs in range(2):
                            for fc in range(2):
                                for half in range(2):
                                    mm = rec("t", (lambda hs, ti, fc, half, yreg, pr: lambda t: t.matmul(
                                        bank(yreg * 2 + half, 1),
                                        hhT_sb[:, (2 * hs + fc) * 1024 + ti * 128:(2 * hs + fc) * 1024 + (ti + 1) * 128],
                                        mw2_sb[pr].ap().rearrange("p (hs c f) -> p hs c f", hs=2, c=2)[:, hs, fc, half * 512:(half + 1) * 512],
                                        start=(hs == 0 and fc == 0), stop=(hs == 1 and fc == 1),
                                        skip_group_check=True))(hs, ti, fc, half, yreg, pr),
                                        waits=(wts if (hs == 0 and fc == 0 and half == 0) else ()))
                        wts = [P.upto(mm), P.upto(accw[ti])]
                        av = rec("v", (lambda e_i, tc, ti, yreg: lambda v: v.scalar_tensor_tensor(
                            acc_sb[:, ti * 1024:(ti + 1) * 1024], bank(yreg * 2, 2),
                            mask_sb[:, tc * E_EXP + e_i: tc * E_EXP + e_i + 1],
                            acc_sb[:, ti * 1024:(ti + 1) * 1024], ALU.mult, ALU.add))(e_i, tc, ti, yreg),
                            waits=wts)
                        accw[ti] = av
                        outreg_use[yreg] = av
                        if interleave_pl2 and e_i == E_EXP - 1 and q4 == 3:
                            rec_postln2(g_i, ti)
                    w2buf2_of[pr] = mm
                    hhslot_use[0] = mm
                    hhslot_use[1] = mm

            if g_i == 0 and "acc0" in taps:
                tap_refs["acc0"] = rec("g", lambda g: g.dma_start(tap_t["acc0"], acc_sb[:]),
                    waits=[P.upto(accw[7])], dma="tap_acc0")

            if not interleave_pl2:
                for ti in range(8):
                    rec_postln2(g_i, ti)

        rec("y", lambda y: y.nop(), waits=[P.upto(store_last)])

    except _StopBuild:
        pass

    with nc.Block() as block:
        P.emit(block)
    ctx.close()
    return nc


_NC_CACHE = {}


def _get_nc(debug_taps=()):
    key = tuple(sorted(debug_taps))
    if key not in _NC_CACHE:
        _NC_CACHE[key] = build_nc(debug_taps=key)
    return _NC_CACHE[key]


def kernel(
    L0_attn_inputs, L0_attn_ln1_scale, L0_attn_ln1_bias,
    L0_attn_Wq, L0_attn_bq, L0_attn_Wk, L0_attn_bk, L0_attn_Wv, L0_attn_bv,
    L0_attn_Wo, L0_attn_bo, L0_attn_ln2_scale, L0_attn_ln2_bias,
    L0_attn_W1, L0_attn_b1, L0_attn_W2, L0_attn_b2,
    L0_ffn_gate_w, L0_ffn_w1, L0_ffn_b1, L0_ffn_w2, L0_ffn_b2,
    _debug_taps=(),
):
    x = np.ascontiguousarray(np.asarray(L0_attn_inputs, dtype=np.float32))
    wk = np.asarray(L0_attn_Wk, dtype=np.float32).reshape(D, D)
    wv = np.asarray(L0_attn_Wv, dtype=np.float32).reshape(D, D)
    common = dict(
        wq=np.ascontiguousarray(np.asarray(L0_attn_Wq, dtype=np.float32).reshape(D, D)),
        wkv=np.ascontiguousarray(np.concatenate([wk, wv], axis=1)),
        bqv=np.ascontiguousarray(np.asarray(L0_attn_bq, dtype=np.float32).reshape(-1)),
        bkv=np.ascontiguousarray(np.concatenate(
            [np.asarray(L0_attn_bk, dtype=np.float32).reshape(-1),
             np.asarray(L0_attn_bv, dtype=np.float32).reshape(-1)])),
        ln1s=np.ascontiguousarray(np.asarray(L0_attn_ln1_scale, dtype=np.float32)),
        ln1b=np.ascontiguousarray(np.asarray(L0_attn_ln1_bias, dtype=np.float32)),
        wo=np.ascontiguousarray(np.asarray(L0_attn_Wo, dtype=np.float32).reshape(D, D)),
        bo=np.ascontiguousarray(np.asarray(L0_attn_bo, dtype=np.float32)),
        ln2s=np.ascontiguousarray(np.asarray(L0_attn_ln2_scale, dtype=np.float32)),
        ln2b=np.ascontiguousarray(np.asarray(L0_attn_ln2_bias, dtype=np.float32)),
        w1=np.ascontiguousarray(np.asarray(L0_attn_W1, dtype=np.float32)),
        b1=np.ascontiguousarray(np.asarray(L0_attn_b1, dtype=np.float32)),
        w2=np.ascontiguousarray(np.asarray(L0_attn_W2, dtype=np.float32)),
        b2=np.ascontiguousarray(np.asarray(L0_attn_b2, dtype=np.float32)),
        gw=np.ascontiguousarray(np.asarray(L0_ffn_gate_w, dtype=np.float32)),
        mw1=np.ascontiguousarray(np.asarray(L0_ffn_w1, dtype=np.float32)),
        mb1=np.ascontiguousarray(np.asarray(L0_ffn_b1, dtype=np.float32)),
        mw2=np.ascontiguousarray(np.asarray(L0_ffn_w2, dtype=np.float32)),
        mb2=np.ascontiguousarray(np.asarray(L0_ffn_b2, dtype=np.float32)),
    )
    in_maps = []
    for c in range(N_CORES):
        b = c // 4
        s0 = (c % 4) * T_OWN
        m = dict(common)
        m["xb"] = np.ascontiguousarray(np.roll(x[b], -s0, axis=0))
        in_maps.append(m)
    nc = _get_nc(_debug_taps)
    res = run_bass_kernel_spmd(nc, in_maps, core_ids=list(range(N_CORES)))
    outs = [res.results[c]["out"] for c in range(N_CORES)]
    y = np.concatenate(outs, axis=0).reshape(2, 8192, D)
    if _debug_taps:
        return y, res
    return y


# version 07
# speedup vs baseline: 1.1296x; 1.1296x over previous
import numpy as np
import ml_dtypes
import concourse.bass as bass
import concourse.mybir as mybir
from concourse.bass_utils import run_bass_kernel_spmd

F32 = mybir.dt.float32
BF16 = mybir.dt.bfloat16
AF = mybir.ActivationFunctionType
ALU = mybir.AluOpType
AX = mybir.AxisListType

N_CORES = 8
D = 1024
NDC = 8
T_OWN = 2048
NT_OWN = 16
NT_ALL = 64
H = 16
E_EXP = 14
KEPS = 0.001
LN1EPS = 1e-6
PLNEPS = 1e-5


class Prog:

    ENGINES = ("t", "v", "s", "g", "y")

    def __init__(self, nc):
        self.nc = nc
        self.ops = {e: [] for e in self.ENGINES}
        self.sems = {}
        self.dma_count = {}
        self.group_sems = set()

    def make_sems(self, ctx, dma_streams, group_streams):
        for e in self.ENGINES:
            self.sems[e] = ctx.enter_context(self.nc.semaphore(f"sem_{e}"))
        for nm in dma_streams:
            self.sems[nm] = ctx.enter_context(self.nc.semaphore(f"sem_{nm}"))
            self.dma_count[nm] = 0
        self.group_sems = set(group_streams)

    def op(self, eng, fn, waits=(), dma=False):
        assert eng in self.ENGINES
        w = [x for x in waits if x is not None]
        if dma:
            assert isinstance(dma, str) and dma in self.dma_count, dma
            self.dma_count[dma] += 1
            self.ops[eng].append([fn, w, True, dma])
            return ("D", dma, self.dma_count[dma])
        self.ops[eng].append([fn, w, False, None])
        return (eng, len(self.ops[eng]) - 1)

    def upto(self, ref):
        if ref[0] == "D":
            return ref
        eng, i = ref
        self.ops[eng][i][2] = True
        return ref

    def last(self, eng):
        assert self.ops[eng]
        return (eng, len(self.ops[eng]) - 1)

    def upto_last(self, eng):
        lst = self.ops[eng]
        for i in range(len(lst) - 1, -1, -1):
            if lst[i][3] is None:
                lst[i][2] = True
                return (eng, i)
        raise AssertionError(f"no compute op on {eng}")

    def emit(self, block):
        cum = {}
        for e in self.ENGINES:
            c = 0
            lst = []
            for rec in self.ops[e]:
                if rec[2] and rec[3] is None:
                    c += 1
                lst.append(c)
            cum[e] = lst

        def wait_val(ref):
            if ref[0] == "D":
                _, nm, idx = ref
                if nm in self.group_sems:
                    return (nm, 16 * self.dma_count[nm])
                return (nm, 16 * idx)
            we, wi = ref
            return (we, cum[we][wi])

        def run(eng_obj, e):
            for rec in self.ops[e]:
                fn, waits, marked, dmasem = rec
                done = set()
                for ref in waits:
                    nm, val = wait_val(ref)
                    if (nm, val) in done:
                        continue
                    done.add((nm, val))
                    eng_obj.wait_ge(self.sems[nm], val)
                instr = fn(eng_obj)
                if dmasem is not None:
                    instr.then_inc(self.sems[dmasem], 16)
                elif marked:
                    assert instr is not None, f"marked op on {e} returned None"
                    instr.then_inc(self.sems[e], 1)

        block.tensor(lambda t: run(t, "t"))
        block.vector(lambda v: run(v, "v"))
        block.scalar(lambda s: run(s, "s"))
        block.gpsimd(lambda g: run(g, "g"))
        block.sync(lambda y: run(y, "y"))


class _StopBuild(Exception):
    pass


def build_nc(debug_taps=(), stop_after=None):
    nc = bass.Bass("TRN2", target_bir_lowering=False)
    taps = set(debug_taps)

    xb = nc.dram_tensor("xb", [8192, D], F32, kind="ExternalInput").ap()
    wq_d = nc.dram_tensor("wq", [D, D], F32, kind="ExternalInput").ap()
    wkv_d = nc.dram_tensor("wkv", [D, 2 * D], F32, kind="ExternalInput").ap()
    bq_d = nc.dram_tensor("bqv", [D], F32, kind="ExternalInput").ap()
    bkv_d = nc.dram_tensor("bkv", [2 * D], F32, kind="ExternalInput").ap()
    ln1s_d = nc.dram_tensor("ln1s", [D], F32, kind="ExternalInput").ap()
    ln1b_d = nc.dram_tensor("ln1b", [D], F32, kind="ExternalInput").ap()
    wo_d = nc.dram_tensor("wo", [D, D], F32, kind="ExternalInput").ap()
    bo_d = nc.dram_tensor("bo", [D], F32, kind="ExternalInput").ap()
    ln2s_d = nc.dram_tensor("ln2s", [D], F32, kind="ExternalInput").ap()
    ln2b_d = nc.dram_tensor("ln2b", [D], F32, kind="ExternalInput").ap()
    w1_d = nc.dram_tensor("w1", [D, 4096], F32, kind="ExternalInput").ap()
    b1_d = nc.dram_tensor("b1", [4096], F32, kind="ExternalInput").ap()
    w2_d = nc.dram_tensor("w2", [4096, D], F32, kind="ExternalInput").ap()
    b2_d = nc.dram_tensor("b2", [D], F32, kind="ExternalInput").ap()
    gw_d = nc.dram_tensor("gw", [E_EXP, D], F32, kind="ExternalInput").ap()
    mw1_d = nc.dram_tensor("mw1", [E_EXP, D, 2048], F32, kind="ExternalInput").ap()
    mb1_d = nc.dram_tensor("mb1", [E_EXP, 2048], F32, kind="ExternalInput").ap()
    mw2_d = nc.dram_tensor("mw2", [E_EXP, 2048, D], F32, kind="ExternalInput").ap()
    mb2_d = nc.dram_tensor("mb2", [E_EXP, D], F32, kind="ExternalInput").ap()
    out_d = nc.dram_tensor("out", [T_OWN, D], F32, kind="ExternalOutput").ap()

    tap_t = {}
    tap_shapes = dict(
        xnT=([128, NDC * T_OWN], F32), phiq=([128, NDC * T_OWN], F32),
        kvsb=([128, 8 * 136], F32), attn=([128, NT_OWN * 1024], F32),
        xres=([128, NT_OWN * 1024], F32), x1=([128, NT_OWN * 1024], F32),
        mask=([128, NT_OWN * E_EXP], F32), acc0=([128, 8 * 1024], F32),
        y2T=([128, NDC * 1024], F32),
    )
    for name in taps:
        shp, dt_ = tap_shapes[name]
        tap_t[name] = nc.dram_tensor("tap_" + name, shp, dt_, kind="ExternalOutput").ap()

    identb = nc.inline_tensor(np.eye(128, dtype=ml_dtypes.bfloat16), name="identb")
    identf = nc.inline_tensor(np.eye(128, dtype=np.float32), name="identf")
    ones128 = nc.inline_tensor(np.ones((128, 128), dtype=ml_dtypes.bfloat16), name="ones128")

    A0, B0, C0, D0, E0 = 20480, 86016, 151552, 184320, 217088

    def sb(name, shape, dtype, off):
        return nc.alloc_sbuf_tensor_at(name, shape, dtype, offset=off)

    wkv_sb = sb("wkv_sb", [128, NDC * 2048], BF16, A0)
    xt_sb = [sb(f"xt{i}", [128, 1024], F32, A0 + 32768 + i * 4096) for i in range(2)]
    xc_sb = [sb(f"xc{i}", [128, 1024], BF16, A0 + 40960 + i * 2048) for i in range(2)]
    phik_sb = [sb(f"phik{i}", [128, 1024], BF16, A0 + 45056 + i * 2048) for i in range(2)]
    vtok_sb = [sb(f"vtok{i}", [128, 1024], BF16, A0 + 49152 + i * 2048) for i in range(2)]
    brep_sb = sb("brep", [128, 1024], BF16, A0 + 53248)
    kvbias_b = sb("kvbias_b", [128, 2048], F32, A0 + 55296)
    gwraw_sb = sb("gwraw", [E_EXP, 1024], BF16, A0 + 63488)
    kv_blk = sb("kv_blk", [128, 8 * 136], BF16, A0 + 55296)
    wo_sb = sb("wo_sb", [128, NDC * 1024], BF16, A0)
    x0s_sb = [sb(f"x0s{i}", [128, 1024], F32, A0 + 16384 + i * 4096) for i in range(3)]
    bo_b = sb("bo_b", [128, 1024], F32, A0 + 28672)
    b2_b = sb("b2_b", [128, 1024], F32, A0 + 32768)
    mb1raw_sb = sb("mb1raw", [E_EXP, 2048], F32, D0 + 24576)
    acc_sb = sb("acc_sb", [128, 8 * 1024], F32, A0)
    pool_c = sb("pool_c", [128, 1024], F32, A0 + 36864)
    pool_o = sb("pool_o", [128, 1024], F32, A0 + 40960)

    wq_sb = sb("wq_sb", [128, NDC * 1024], BF16, B0)
    vraw_sb = sb("vraw", [8, 5 * 128], F32, B0 + 16384)
    braw_sb = sb("braw", [32, 128], F32, B0 + 20480)
    xres_sb = sb("xres_sb", [128, NT_OWN * 1024], F32, B0)

    xnT_sb = sb("xnT_sb", [128, NDC * T_OWN], BF16, C0)
    attn_sb = sb("attn_sb", [128, NT_OWN * 1024], BF16, C0)
    y2T_sb = sb("y2T_sb", [128, NDC * 2048], BF16, C0)
    upool_sb = [sb(f"upool{i}", [128, 1024], F32, A0 + i * 4096) for i in range(2)]
    x1T_sb = sb("x1T_sb", [128, NDC * T_OWN], BF16, C0)

    phiq_sb = sb("phiq_sb", [128, NDC * T_OWN], BF16, D0)
    attnT_sb = sb("attnT_sb", [128, NDC * T_OWN], BF16, D0)
    w1p_sb = [sb(f"w1p{i}", [128, NDC * 256], BF16, D0 + i * 4096) for i in range(2)]
    w2p_sb = [sb(f"w2p{i}", [128, 2 * 1024], BF16, D0 + 8192 + i * 4096) for i in range(2)]
    h1T_sb = [sb(f"h1T{i}", [128, 2 * 2048], BF16, D0 + 16384 + i * 8192) for i in range(2)]
    junk_sb = sb("junk_sb", [128, 1024], BF16, A0 + 49152)
    maskT_sb = sb("maskT_sb", [E_EXP, NT_OWN * 128], BF16, A0 + 57472)
    mw1_sb = [sb(f"mw1s{i}", [128, NDC * 256], BF16, D0 + i * 4096) for i in range(2)]
    mw2_sb = [sb(f"mw2s{i}", [128, 4 * 1024], BF16, D0 + 8192 + i * 8192) for i in range(2)]
    hhT_sb = sb("hhT_sb", [128, 4 * 1024], BF16, D0 + 24576)

    e = E0
    identb_sb = sb("identb_sb", [128, 128], BF16, e); e += 256
    identf_sb = sb("identf_sb", [128, 128], F32, e); e += 512
    ones_sb = sb("ones_sb", [128, 128], BF16, e); e += 256
    gwt_sb = sb("gwt_sb", [128, NDC * E_EXP], BF16, e); e += 256
    b1m_sb = sb("b1m_sb", [128, 16 * E_EXP], F32, e); e += 896
    mask_sb = sb("mask_sb", [128, NT_OWN * E_EXP], F32, e); e += 896
    b2m_sb = sb("b2m_sb", [E_EXP, 1024], BF16, e); e += 2048
    kv_sb = sb("kv_sb", [128, 8 * 72], BF16, e); e += 1184
    bqe_sb = sb("bqe_sb", [128, NDC], F32, e); e += 32
    g1c_sb = sb("g1c_sb", [128, NDC], F32, e); e += 32
    b1c_sb = sb("b1c_sb", [128, NDC], BF16, e); e += 32
    b1cf_sb = sb("b1cf_sb", [128, NDC], F32, e); e += 32
    g2c_sb = sb("g2c_sb", [128, NDC], F32, e); e += 32
    b2c_sb = sb("b2c_sb", [128, NDC], F32, e); e += 32
    bqc_sb = sb("bqc_sb", [128, NDC], F32, e); e += 32
    b1mlp_sb = sb("b1mlp_sb", [128, 32], F32, e); e += 128
    st_sb = [sb(f"st{i}", [128, 12], F32, e + i * 64) for i in range(2)]; e += 128
    mv_sb = [sb(f"mv{i}", [128, 8], F32, e + i * 32) for i in range(4)]; e += 128
    r1_sb = [sb(f"r1_{i}", [128, 8], F32, e + i * 32) for i in range(4)]; e += 128
    rs_sb = [sb(f"rs{i}", [128, 8], F32, e + i * 32) for i in range(4)]; e += 128
    rd_sb = [sb(f"rd{i}", [128, 16], F32, e + i * 64) for i in range(2)]; e += 128
    sums_sb = [sb(f"sums{i}", [128, 8], F32, e + i * 32) for i in range(4)]; e += 128
    ssq_sb = [sb(f"ssq{i}", [128, 8], F32, e + i * 32) for i in range(4)]; e += 128
    mu_sb = [sb(f"mu{i}", [128, 8], F32, e + i * 32) for i in range(4)]; e += 128
    tp1_sb = [sb(f"tp1_{i}", [128, 8], F32, e + i * 32) for i in range(4)]; e += 128
    m1_sb = sb("m1_sb", [128, 8], F32, e); e += 32
    m2_sb = sb("m2_sb", [128, 8], F32, e); e += 32
    w0_sb = sb("w0_sb", [128, 8], F32, e); e += 32
    w1w_sb = sb("w1w_sb", [128, 8], F32, e); e += 32
    eq1_sb = sb("eq1_sb", [128, E_EXP], F32, e); e += 64
    eq2_sb = sb("eq2_sb", [128, E_EXP], F32, e); e += 64
    lm_sb = sb("lm_sb", [128, E_EXP], F32, e); e += 64
    eq2all_sb = sb("eq2all_sb", [128, NT_OWN * E_EXP], F32, e); e += 896
    dall_sb = sb("dall_sb", [128, NT_OWN], F32, e); e += 64
    pall_sb = sb("pall_sb", [128, NT_OWN], F32, e); e += 64
    qall_sb = sb("qall_sb", [128, NT_OWN], F32, e); e += 64
    assert e <= 229344, f"E overflow: {e}"

    ps = nc.alloc_psum_tensor("ps", [128, 4096], F32)

    def bank(b0, nb):
        return ps[:, b0 * 512:(b0 + nb) * 512]

    ptr_bf = ps[:, 7 * 512:8 * 512].bitcast(BF16)
    ptr_bf2 = ps[:, 6 * 512:7 * 512].bitcast(BF16)
    ptr_of = (ptr_bf, ptr_bf2)
    kv_ps = ps[0:64, 4 * 512:4 * 512 + 1024]
    ks_ps = ps[0:64, 6 * 512:6 * 512 + 16]

    P = Prog(nc)
    import contextlib
    ctx = contextlib.ExitStack()
    DMA_STREAMS = ["cst", "w0", "wo", "wol", "bo", "b2b", "mb1", "xb0", "xb1",
                   "x00", "x01", "x02", "st", "w1a", "w1b", "w2a", "w2b",
                   "m1a", "m1b", "m2", "m2b"] + ["tap_" + t for t in
                   ("xnT", "phiq", "kvsb", "attn", "xres", "x1", "mask", "acc0", "y2T")]
    P.make_sems(ctx, DMA_STREAMS, group_streams=["cst", "w0"])

    def rec(eng, fn, waits=(), dma=False):
        return P.op(eng, fn, waits=waits, dma=dma)

    def _maybe_stop(stage, wait=None):
        if stop_after != stage:
            return
        last = None
        for tc in range(NT_OWN):
            last = rec("y", (lambda tc: lambda y: y.dma_start(
                out_d[tc * 128:(tc + 1) * 128, :], acc_sb[:, 0:1024]))(tc),
                waits=([wait] if (wait is not None and tc == 0) else ()), dma="st")
        rec("y", lambda y: y.nop(), waits=[P.upto(last)])
        raise _StopBuild()

    try:
        ld_ident = rec("y", lambda y: y.dma_start(identb_sb[:], identb.ap()), dma="cst")
        ld_identf = rec("y", lambda y: y.dma_start(identf_sb[:], identf.ap()), dma="cst")
        ld_ones = rec("y", lambda y: y.dma_start(ones_sb[:], ones128.ap()), dma="cst")
        vsrcs = [ln1s_d, ln1b_d, ln2s_d, ln2b_d, bq_d]
        ld_vraw = None
        for i, src in enumerate(vsrcs):
            ld_vraw = rec("y", (lambda i, src: lambda y: y.dma_start(
                vraw_sb[0:8, i * 128:(i + 1) * 128],
                src.rearrange("(a b) -> a b", a=8)))(i, src), dma="cst")
        ld_braw = rec("y", lambda y: y.dma_start(
            braw_sb[:], b1_d.rearrange("(a b) -> a b", a=32)), dma="cst")
        ld_bias_tiles = rec("y", lambda y: y.dma_start(
            kvbias_b[:], bkv_d[None, :].broadcast_to([128, 2048])), dma="cst")
        ld_gw = rec("g", lambda g: g.dma_start(gwraw_sb[:], gw_d), dma="w0")
        ld_b2m = rec("g", lambda g: g.dma_start(b2m_sb[:], mb2_d), dma="w0")
        ld_wkv = rec("g", lambda g: g.dma_start(
            wkv_sb.ap().rearrange("p (c f) -> p c f", c=NDC),
            wkv_d.rearrange("(c p) f -> p c f", p=128)), dma="w0")
        ld_wq = rec("g", lambda g: g.dma_start(
            wq_sb.ap().rearrange("p (c f) -> p c f", c=NDC),
            wq_d.rearrange("(c p) f -> p c f", p=128)), dma="w0")

        tp_ev = {}
        vec_tr = None
        for i, dstc in enumerate([g1c_sb, None, g2c_sb, b2c_sb, bqc_sb]):
            vec_tr = rec("t", (lambda i: lambda t: t.transpose(
                ps[0:128, 6 * 512 + i * 8: 6 * 512 + (i + 1) * 8],
                vraw_sb[0:8, i * 128:(i + 1) * 128], identf_sb[0:8, 0:8]))(i),
                waits=[P.upto(ld_vraw), P.upto(ld_identf)] if i == 0 else ())
        vec_tr2 = rec("t", lambda t: t.transpose(
            ps[0:128, 6 * 512 + 40: 6 * 512 + 72], braw_sb[:], identf_sb[0:32, 0:32]),
            waits=[P.upto(ld_braw)])
        evs = []
        for i, dstc in enumerate([g1c_sb, None, g2c_sb, b2c_sb, bqc_sb]):
            if dstc is None:
                evs.append(rec("v", (lambda i: lambda v: v.tensor_copy(
                    b1c_sb[:, 0:NDC], ps[0:128, 6 * 512 + i * 8: 6 * 512 + i * 8 + 8]))(i),
                    waits=[P.upto(vec_tr2)] if i == 0 else ()))
                evs.append(rec("v", (lambda i: lambda v: v.tensor_copy(
                    b1cf_sb[:, 0:NDC], ps[0:128, 6 * 512 + i * 8: 6 * 512 + i * 8 + 8]))(i)))
            elif dstc is bqc_sb:
                evs.append(rec("v", (lambda i, dstc: lambda v: v.tensor_scalar(
                    dstc[:, 0:NDC], ps[0:128, 6 * 512 + i * 8: 6 * 512 + i * 8 + 8],
                    KEPS, None, ALU.add))(i, dstc),
                    waits=[P.upto(vec_tr2)] if i == 0 else ()))
            else:
                evs.append(rec("v", (lambda i, dstc: lambda v: v.tensor_copy(
                    dstc[:, 0:NDC], ps[0:128, 6 * 512 + i * 8: 6 * 512 + i * 8 + 8]))(i, dstc),
                    waits=[P.upto(vec_tr2)] if i == 0 else ()))
        evs.append(rec("v", lambda v: v.tensor_copy(
            b1mlp_sb[:], ps[0:128, 6 * 512 + 40: 6 * 512 + 72])))
        cols_done = P.upto_last("v")

        sfold = None
        for dc in range(NDC):
            sfold = rec("s", (lambda dc: lambda s: s.activation(
                wkv_sb[:, dc * 2048:(dc + 1) * 2048], wkv_sb[:, dc * 2048:(dc + 1) * 2048],
                AF.Copy, scale=g1c_sb[:, dc:dc + 1]))(dc),
                waits=[P.upto(ld_wkv), cols_done] if dc == 0 else ())
        for dc in range(NDC):
            sfold = rec("s", (lambda dc: lambda s: s.activation(
                wq_sb[:, dc * 1024:(dc + 1) * 1024], wq_sb[:, dc * 1024:(dc + 1) * 1024],
                AF.Copy, scale=g1c_sb[:, dc:dc + 1]))(dc),
                waits=[P.upto(ld_wq)] if dc == 0 else ())
        for dc in range(NDC):
            sfold = rec("s", (lambda dc: lambda s: s.activation(
                brep_sb[:, dc * 128:(dc + 1) * 128], ones_sb[:],
                AF.Copy, scale=b1cf_sb[:, dc:dc + 1]))(dc),
                waits=[P.upto(ld_ones)] if dc == 0 else ())
        w_folded = P.upto(sfold)

        mm = None
        for dc in range(NDC):
            for bk_ in range(4):
                mm = rec("t", (lambda dc, bk_: lambda t: t.matmul(
                    bank(bk_, 1), brep_sb[:, dc * 128:(dc + 1) * 128],
                    wkv_sb[:, dc * 2048 + bk_ * 512: dc * 2048 + (bk_ + 1) * 512],
                    start=(dc == 0), stop=(dc == NDC - 1), skip_group_check=True))(dc, bk_),
                    waits=[w_folded] if (dc == 0 and bk_ == 0) else ())
        kvb_add = rec("v", lambda v: v.tensor_tensor(
            kvbias_b[:], kvbias_b[:], bank(0, 4), ALU.add),
            waits=[P.upto(mm), P.upto(ld_bias_tiles)])
        kvbias_done = P.upto(kvb_add)

        mm = None
        for mc in range(NDC):
            for dc in range(NDC):
                mm = rec("t", (lambda mc, dc: lambda t: t.matmul(
                    bank(4, 1)[:, mc:mc + 1],
                    wq_sb[:, dc * 1024 + mc * 128: dc * 1024 + (mc + 1) * 128],
                    b1c_sb[:, dc:dc + 1],
                    start=(dc == 0), stop=(dc == NDC - 1), skip_group_check=True))(mc, dc),
                    waits=[w_folded] if (mc == 0 and dc == 0) else ())
        bq1 = rec("v", lambda v: v.tensor_tensor(
            bqe_sb[:, 0:NDC], bank(4, 1)[:, 0:NDC], bqc_sb[:, 0:NDC], ALU.add),
            waits=[P.upto(mm)])
        bqe_done = P.upto(bq1)

        _maybe_stop("p0", bqe_done)

        last_xt_use = {}
        last_xnsrc_use = {}
        last_ptr_use = None
        last_kvuse = {}
        last_pk = None
        last_pv = None

        for t_i in range(NT_ALL):
            sl = t_i % 2
            own = t_i < NT_OWN
            wts = [cols_done]
            if t_i >= 2:
                wts.append(P.upto(last_xt_use[sl]))
            x0ld = rec("y", (lambda t_i, sl: lambda y: y.dma_start(
                xt_sb[sl][:], xb[t_i * 128:(t_i + 1) * 128, :]))(t_i, sl), waits=wts,
                dma=("xb0" if sl == 0 else "xb1"))
            st = st_sb[sl]; mv = mv_sb[sl]; r1 = r1_sb[sl]; rs = rs_sb[sl]
            rec("v", (lambda sl, st: lambda v: v.bn_stats(st[:, 0:6], xt_sb[sl][:, 0:512]))(sl, st),
                waits=[P.upto(x0ld)])
            o1 = rec("v", (lambda sl, st: lambda v: v.bn_stats(st[:, 6:12], xt_sb[sl][:, 512:1024]))(sl, st))
            o2 = rec("v", (lambda st, mv: lambda v: v.bn_aggr(mv[:, 0:2], st[:]))(st, mv),
                     waits=[P.upto(o1)])
            o3 = rec("v", (lambda mv, r1: lambda v: v.tensor_scalar(
                r1[:, 0:1], mv[:, 1:2], LN1EPS, None, ALU.add))(mv, r1), waits=[P.upto(o2)])
            o4 = rec("v", (lambda r1: lambda v: v.reciprocal(r1[:, 0:1], r1[:, 0:1]))(r1),
                     waits=[P.upto(o3)])
            sq = rec("s", (lambda r1, rs: lambda s: s.activation(rs[:, 0:1], r1[:, 0:1], AF.Sqrt))(r1, rs),
                     waits=[P.upto(o4)])
            wts = [P.upto(sq)]
            if t_i >= 2:
                wts.append(P.upto(last_xnsrc_use[sl]))
            ap_op = rec("v", (lambda sl, mv, rs: lambda v: v.tensor_scalar(
                xc_sb[sl][:], xt_sb[sl][:], mv[:, 0:1], rs[:, 0:1], ALU.subtract, ALU.mult))(sl, mv, rs),
                waits=wts)
            last_xt_use[sl] = ap_op
            wts = [P.upto(ap_op)]
            if last_ptr_use is not None:
                wts.append(P.upto(last_ptr_use))
            tr = None
            for k in range(NDC):
                tr = rec("t", (lambda sl, k: lambda t: t.transpose(
                    ptr_bf[:, k * 128:(k + 1) * 128], xc_sb[sl][:, k * 128:(k + 1) * 128],
                    identb_sb[:]))(sl, k),
                    waits=(wts + [P.upto(ld_ident)]) if k == 0 else ())
            if own:
                dstap = xnT_sb.ap().rearrange("p (c f) -> p c f", c=NDC)[:, :, t_i * 128:(t_i + 1) * 128]
                ev = rec("s", (lambda dstap: lambda s: s.activation(
                    dstap, ptr_bf.rearrange("p (c f) -> p c f", c=NDC), AF.Copy))(dstap),
                    waits=[P.upto(tr)])
            else:
                ev = rec("s", (lambda sl: lambda s: s.activation(
                    xc_sb[sl][:], ptr_bf[:], AF.Copy))(sl), waits=[P.upto(tr)])
            last_ptr_use = ev

            def mk_lhs(dc, own=own, t_i=t_i, sl=sl):
                if own:
                    return xnT_sb[:, dc * T_OWN + t_i * 128: dc * T_OWN + (t_i + 1) * 128]
                return xc_sb[sl][:, dc * 128:(dc + 1) * 128]

            wts_k = [P.upto(ev), w_folded]
            if t_i == 0:
                wts_k += [kvbias_done, bqe_done]
            if last_pk is not None:
                wts_k.append(P.upto(last_pk))
            mmk = None
            for dc in range(NDC):
                lhs_ap = mk_lhs(dc)
                for half in range(2):
                    mmk = rec("t", (lambda lhs_ap, dc, half: lambda t: t.matmul(
                        bank(half, 1), lhs_ap,
                        wkv_sb[:, dc * 2048 + half * 512: dc * 2048 + (half + 1) * 512],
                        start=(dc == 0), stop=(dc == NDC - 1), skip_group_check=True))(lhs_ap, dc, half),
                        waits=(wts_k if (dc == 0 and half == 0) else ()))
            wts_v = []
            if last_pv is not None:
                wts_v.append(P.upto(last_pv))
            mmv = None
            for dc in range(NDC):
                lhs_ap = mk_lhs(dc)
                for half in range(2):
                    mmv = rec("t", (lambda lhs_ap, dc, half: lambda t: t.matmul(
                        bank(2 + half, 1), lhs_ap,
                        wkv_sb[:, dc * 2048 + 1024 + half * 512: dc * 2048 + 1024 + (half + 1) * 512],
                        start=(dc == 0), stop=(dc == NDC - 1), skip_group_check=True))(lhs_ap, dc, half),
                        waits=(wts_v if (dc == 0 and half == 0) else ()))
            wts = [P.upto(mmk)]
            if t_i >= 2:
                wts.append(P.upto(last_kvuse[sl]))
            pk1 = rec("v", (lambda sl: lambda v: v.scalar_tensor_tensor(
                phik_sb[sl][:], bank(0, 2), KEPS, kvbias_b[:, 0:1024], ALU.add, ALU.add))(sl),
                waits=wts)
            vv = rec("v", (lambda sl: lambda v: v.tensor_tensor(
                vtok_sb[sl][:], bank(2, 2), kvbias_b[:, 1024:2048], ALU.add))(sl),
                waits=[P.upto(mmv)])
            pk2 = rec("v", (lambda sl: lambda v: v.tensor_scalar(
                phik_sb[sl][:], phik_sb[sl][:], KEPS, None, ALU.max))(sl),
                waits=[P.upto(pk1)])
            last_pk = pk2
            last_pv = vv
            kvmm = None
            wts = [P.upto(pk2), P.upto(vv)]
            for h in range(H):
                kvmm = rec("t", (lambda sl, h, t_i: lambda t: t.matmul(
                    kv_ps[:, h * 64:(h + 1) * 64], phik_sb[sl][:, h * 64:(h + 1) * 64],
                    vtok_sb[sl][:, h * 64:(h + 1) * 64],
                    start=(t_i == 0 and h in (0, 8)), stop=(t_i == NT_ALL - 1),
                    skip_group_check=True))(sl, h, t_i),
                    waits=(wts if h == 0 else ()))
                kvmm = rec("t", (lambda sl, h, t_i: lambda t: t.matmul(
                    ks_ps[:, h:h + 1], phik_sb[sl][:, h * 64:(h + 1) * 64], ones_sb[:, 0:1],
                    start=(t_i == 0 and h == 0), stop=(t_i == NT_ALL - 1),
                    skip_group_check=True))(sl, h, t_i))
            last_kvuse[sl] = kvmm
            last_xnsrc_use[sl] = tr if own else mmv

        kv_done = P.upto_last("t")

        kvp = kv_ps.rearrange("p (g two c) -> p g two c", two=2, c=64)
        ksp = ks_ps.rearrange("p (g two) -> p g two", two=2)
        blk_t = kv_blk.ap().rearrange("p (g c) -> p g c", c=136)
        zz = rec("v", lambda v: v.memset(kv_blk[:], 0.0), waits=[kv_done])
        zz2 = rec("v", lambda v: v.memset(kv_sb[:], 0.0))
        rec("v", lambda v: v.tensor_copy(blk_t[0:64, :, 0:64], kvp[:, :, 0, :]),
            waits=[P.upto(zz)])
        rec("v", lambda v: v.tensor_copy(blk_t[0:64, :, 128:129], ksp[:, :, 0:1]))
        kvr = kv_sb[0:64, :].rearrange("p (g c) -> p g c", c=72)
        rec("v", lambda v: v.tensor_copy(kvr[:, :, 0:64], kvp[:, :, 1, :]),
            waits=[P.upto(zz2)])
        ev_kv = rec("v", lambda v: v.tensor_copy(kvr[:, :, 64:65], ksp[:, :, 1:2]))
        st_done = P.upto(ev_kv)
        d1m = rec("g", lambda g: g.dma_start(
            blk_t[64:128, :, 64:128],
            kv_sb[0:64, :].rearrange("p (g c) -> p g c", c=72)[:, :, 0:64]),
            waits=[st_done], dma="wo")
        d2m = rec("g", lambda g: g.dma_start(
            blk_t[64:128, :, 129:130],
            kv_sb[0:64, :].rearrange("p (g c) -> p g c", c=72)[:, :, 64:65],
            allow_slow_non_contiguous=True),
            waits=[], dma="wo")
        kvsb_done = P.upto(d2m)

        tap_refs = {}
        _maybe_stop("A", kvsb_done)

        if "kvsb" in taps:
            tap_refs["kvsb"] = rec("g", lambda g: g.dma_start(tap_t["kvsb"], kv_blk[:]), waits=[kvsb_done], dma="tap_kvsb")

        qreg_use = {}
        qev = None
        for i in range(NDC * 2):
            mc, th = i // 2, i % 2
            reg = i % 3
            wts = []
            if i == 0:
                wts = [kv_done, kvsb_done, w_folded, bqe_done]
            if reg in qreg_use:
                wts.append(P.upto(qreg_use[reg]))
            mm = None
            for dc in range(NDC):
                for half in range(2):
                    mm = rec("t", (lambda mc, th, dc, half, reg: lambda t: t.matmul(
                        bank(reg * 2 + half, 1),
                        wq_sb[:, dc * 1024 + mc * 128: dc * 1024 + (mc + 1) * 128],
                        xnT_sb[:, dc * T_OWN + th * 1024 + half * 512: dc * T_OWN + th * 1024 + (half + 1) * 512],
                        start=(dc == 0), stop=(dc == NDC - 1), skip_group_check=True))(mc, th, dc, half, reg),
                        waits=(wts if (dc == 0 and half == 0) else ()))
            qev = rec("v", (lambda mc, th, reg: lambda v: v.tensor_scalar(
                phiq_sb[:, mc * T_OWN + th * 1024: mc * T_OWN + (th + 1) * 1024],
                bank(reg * 2, 2), bqe_sb[:, mc:mc + 1], KEPS, ALU.add, ALU.max))(mc, th, reg),
                waits=[P.upto(mm)])
            qreg_use[reg] = qev
        phiq_done = P.upto(qev)

        if "xnT" in taps:
            tap_refs["xnT"] = rec("g", lambda g: g.dma_start(tap_t["xnT"], xnT_sb[:]), waits=[phiq_done], dma="tap_xnT")
        if "phiq" in taps:
            tap_refs["phiq"] = rec("g", lambda g: g.dma_start(tap_t["phiq"], phiq_sb[:]), waits=[phiq_done], dma="tap_phiq")

        _maybe_stop("q", phiq_done)

        num_reg_use = {}
        recip_of = {}
        mm_of = {}
        mul_last = None

        def emit_attn_muls(tc):
            nonlocal mul_last
            par = tc % 2
            evl = None
            for h in range(H):
                w = [P.upto(recip_of[tc]), tap_refs.get("xnT")] if h == 0 else ()
                evl = rec("v", (lambda tc, par, h: lambda v: v.tensor_scalar(
                    attn_sb[:, tc * 1024 + h * 64: tc * 1024 + (h + 1) * 64],
                    bank(par * 2, 2)[:, h * 64:(h + 1) * 64],
                    rd_sb[par][:, h:h + 1], None, ALU.mult))(tc, par, h), waits=w)
            num_reg_use[par] = evl
            mul_last = evl

        for tc in range(NT_OWN):
            par = tc % 2
            wts = [kvsb_done, phiq_done, tap_refs.get("xnT")]
            if par in num_reg_use:
                wts.append(P.upto(num_reg_use[par]))
            mm = None
            for mc in range(8):
                lhs = phiq_sb[:, mc * T_OWN + tc * 128: mc * T_OWN + (tc + 1) * 128]
                rkv = kv_blk[:, mc * 136: mc * 136 + 128]
                rks = kv_blk[:, mc * 136 + 128: mc * 136 + 130]
                mm = rec("t", (lambda lhs, rkv, par, mc: lambda t: t.matmul(
                    bank(par * 2, 2)[:, mc * 128:(mc + 1) * 128], lhs, rkv,
                    start=True, stop=True, skip_group_check=True))(lhs, rkv, par, mc),
                    waits=(wts if mc == 0 else ()))
                mm = rec("t", (lambda lhs, rks, par, mc: lambda t: t.matmul(
                    bank(4 + par, 1)[:, 2 * mc: 2 * mc + 2], lhs, rks,
                    start=True, stop=True, skip_group_check=True))(lhs, rks, par, mc))
            mm_of[tc] = mm
            recip_of[tc] = rec("v", (lambda par: lambda v: v.reciprocal(
                rd_sb[par][:], bank(4 + par, 1)[:, 0:16]))(par),
                waits=[P.upto(mm)])
            if tc >= 1:
                emit_attn_muls(tc - 1)
        emit_attn_muls(NT_OWN - 1)
        attn_done = P.upto(mul_last)

        if "attn" in taps:
            tap_refs["attn"] = rec("g", lambda g: g.dma_start(tap_t["attn"], attn_sb[:]), waits=[attn_done], dma="tap_attn")

        _maybe_stop("num", attn_done)

        ld_wo = rec("g", lambda g: g.dma_start(
            wo_sb.ap().rearrange("p (c f) -> p c f", c=NDC),
            wo_d.rearrange("(c p) f -> p c f", p=128)), waits=[kv_done], dma="wol")
        ld_bo = rec("y", lambda y: y.dma_start(
            bo_b[:], bo_d[None, :].broadcast_to([128, 1024])), waits=[kv_done], dma="bo")
        ld_b2b = rec("y", lambda y: y.dma_start(
            b2_b[:], b2_d[None, :].broadcast_to([128, 1024])), waits=[P.upto_last("v")], dma="b2b")

        x0_use = {}
        o_reg_use = {}
        trev = None
        trev_of = {}
        e2 = None
        for tc in range(NT_OWN):
            sl3 = tc % 3
            par = tc % 2
            wts = [attn_done]
            if sl3 in x0_use:
                wts.append(P.upto(x0_use[sl3]))
            x0ld = rec("y", (lambda tc, sl3: lambda y: y.dma_start(
                x0s_sb[sl3][:], xb[tc * 128:(tc + 1) * 128, :]))(tc, sl3), waits=wts,
                dma=("x00", "x01", "x02")[sl3])
            wts = [attn_done]
            if (tc % 2) in trev_of:
                wts.append(P.upto(trev_of[tc % 2]))
            pb_ = ptr_of[tc % 2]
            tr = None
            for k in range(NDC):
                tr = rec("t", (lambda tc, k, pb_: lambda t: t.transpose(
                    pb_[:, k * 128:(k + 1) * 128],
                    attn_sb[:, tc * 1024 + k * 128: tc * 1024 + (k + 1) * 128],
                    identb_sb[:]))(tc, k, pb_), waits=(wts if k == 0 else ()))
            ev = rec("s", (lambda tc, pb_: lambda s: s.activation(
                attnT_sb.ap().rearrange("p (c f) -> p c f", c=NDC)[:, :, tc * 128:(tc + 1) * 128],
                pb_.rearrange("p (c f) -> p c f", c=NDC), AF.Copy))(tc, pb_),
                waits=[P.upto(tr), tap_refs.get("phiq")])
            trev_of[tc % 2] = ev
            trev = ev
            wts = [P.upto(ev), P.upto(ld_wo)]
            if par in o_reg_use:
                wts.append(P.upto(o_reg_use[par]))
            mm = None
            for fc in range(NDC):
                for half in range(2):
                    mm = rec("t", (lambda tc, fc, half, par: lambda t: t.matmul(
                        bank(par * 2 + half, 1),
                        attnT_sb[:, fc * T_OWN + tc * 128: fc * T_OWN + (tc + 1) * 128],
                        wo_sb[:, fc * 1024 + half * 512: fc * 1024 + (half + 1) * 512],
                        start=(fc == 0), stop=(fc == NDC - 1), skip_group_check=True))(tc, fc, half, par),
                        waits=(wts if (fc == 0 and half == 0) else ()))
            e1 = rec("v", (lambda tc, par, sl3: lambda v: v.tensor_tensor(
                xres_sb[:, tc * 1024:(tc + 1) * 1024], bank(par * 2, 2), x0s_sb[sl3][:], ALU.add))(tc, par, sl3),
                waits=[P.upto(mm), P.upto(x0ld), P.upto(ld_bo)])
            e2 = rec("v", (lambda tc: lambda v: v.tensor_tensor(
                xres_sb[:, tc * 1024:(tc + 1) * 1024], xres_sb[:, tc * 1024:(tc + 1) * 1024],
                bo_b[:], ALU.add))(tc), waits=[P.upto(e1)])
            o_reg_use[par] = e2
            x0_use[sl3] = e1
        xres_done = P.upto(e2)

        if "xres" in taps:
            tap_refs["xres"] = rec("g", lambda g: g.dma_start(tap_t["xres"], xres_sb[:]), waits=[xres_done], dma="tap_xres")

        _maybe_stop("o", xres_done)

        y2xc_use = {}
        h1_done_of = {}
        w2ld_of = {}
        h1reg_use = {}
        yreg_use = {}
        w1buf_use = {}
        w2buf_use = {}
        h1buf_use = {}
        yacc_of = {}
        pinslot_use = {}
        poolo_use = {}
        x0b_use = {}
        ln2_tr_prev = None
        ln2_tr_of = {}
        xa_of = {}
        x1_last = None

        for th in range(1):
            ln2_ev = None
            for ti in range(16):
                tc = ti
                sl = ti % 2
                st = st_sb[sl]; mv = mv_sb[sl]; r1 = r1_sb[sl]; rs = rs_sb[sl]
                xr = xres_sb[:, tc * 1024:(tc + 1) * 1024]
                rec("v", (lambda st, xr: lambda v: v.bn_stats(st[:, 0:6], xr[:, 0:512]))(st, xr),
                    waits=[xres_done, tap_refs.get("attn")])
                o1 = rec("v", (lambda st, xr: lambda v: v.bn_stats(st[:, 6:12], xr[:, 512:1024]))(st, xr))
                o2 = rec("v", (lambda st, mv: lambda v: v.bn_aggr(mv[:, 0:2], st[:]))(st, mv),
                         waits=[P.upto(o1)])
                o3 = rec("v", (lambda mv, r1: lambda v: v.tensor_scalar(
                    r1[:, 0:1], mv[:, 1:2], LN1EPS, None, ALU.add))(mv, r1), waits=[P.upto(o2)])
                o4 = rec("v", (lambda r1: lambda v: v.reciprocal(r1[:, 0:1], r1[:, 0:1]))(r1),
                         waits=[P.upto(o3)])
                sq = rec("s", (lambda r1, rs: lambda s: s.activation(rs[:, 0:1], r1[:, 0:1], AF.Sqrt))(r1, rs),
                         waits=[P.upto(o4)])
                wts = [P.upto(sq)]
                if sl in y2xc_use:
                    wts.append(P.upto(y2xc_use[sl]))
                ap_op = rec("v", (lambda sl, mv, rs, xr: lambda v: v.tensor_scalar(
                    xc_sb[sl][:], xr, mv[:, 0:1], rs[:, 0:1], ALU.subtract, ALU.mult))(sl, mv, rs, xr),
                    waits=wts)
                wts = [P.upto(ap_op)]
                if (ti % 2) in ln2_tr_of:
                    wts.append(P.upto(ln2_tr_of[ti % 2]))
                pb_ = ptr_of[ti % 2]
                tr = None
                for k in range(NDC):
                    tr = rec("t", (lambda sl, k, pb_: lambda t: t.transpose(
                        pb_[:, k * 128:(k + 1) * 128], xc_sb[sl][:, k * 128:(k + 1) * 128],
                        identb_sb[:]))(sl, k, pb_), waits=(wts if k == 0 else ()))
                y2xc_use[sl] = tr
                evl = None
                for k in range(NDC):
                    evl = rec("s", (lambda ti, k, pb_: lambda s: s.activation(
                        y2T_sb[:, k * 2048 + ti * 128: k * 2048 + (ti + 1) * 128],
                        pb_[:, k * 128:(k + 1) * 128], AF.Identity,
                        bias=b2c_sb[:, k:k + 1], scale=g2c_sb[:, k:k + 1]))(ti, k, pb_),
                        waits=[P.upto(tr)] if k == 0 else ())
                ln2_tr_of[ti % 2] = evl
                ln2_tr_prev = evl
            y2_done = P.upto(ln2_tr_prev)

            if th == 0 and "y2T" in taps:
                rec("g", lambda g: g.dma_start(tap_t["y2T"], y2T_sb[:]), waits=[y2_done], dma="tap_y2T")

            for p in range(16):
                pp = p % 2
                wts = [] if (th or p) else [xres_done]
                if pp in w1buf_use:
                    wts.append(P.upto(w1buf_use[pp]))
                ld1 = rec("g", (lambda p, pp: lambda g: g.dma_start(
                    w1p_sb[pp].ap().rearrange("q (c f) -> q c f", c=NDC),
                    w1_d.rearrange("(c q) f -> q c f", q=128)[:, :, p * 256:(p + 1) * 256]))(p, pp),
                    waits=wts, dma=("w1a" if pp == 0 else "w1b"))
                wts = [] if (th or p) else [xres_done]
                if pp in w2buf_use:
                    wts.append(P.upto(w2buf_use[pp]))
                ld2 = rec("g", (lambda p, pp: lambda g: g.dma_start(
                    w2p_sb[pp].ap().rearrange("q (c f) -> q c f", c=2),
                    w2_d.rearrange("(c q) f -> q c f", q=128)[:, 2 * p:2 * p + 2, :]))(p, pp),
                    waits=wts, dma=("w2a" if pp == 0 else "w2b"))
                hev = None
                for fc in range(2):
                    for tb in range(2):
                        reg = (2 * fc + tb) % 2
                        wts = [P.upto(ld1), y2_done]
                        if reg in h1reg_use:
                            wts.append(P.upto(h1reg_use[reg]))
                        if pp in h1buf_use:
                            wts.append(P.upto(h1buf_use[pp]))
                        mm = None
                        for dc in range(NDC):
                            for half in range(2):
                                mm = rec("t", (lambda p, pp, fc, dc, half, reg, tb: lambda t: t.matmul(
                                    bank(reg * 2 + half, 1),
                                    w1p_sb[pp][:, dc * 256 + fc * 128: dc * 256 + (fc + 1) * 128],
                                    y2T_sb[:, dc * 2048 + tb * 1024 + half * 512: dc * 2048 + tb * 1024 + (half + 1) * 512],
                                    start=(dc == 0), stop=(dc == NDC - 1), skip_group_check=True))(p, pp, fc, dc, half, reg, tb),
                                    waits=(wts if (dc == 0 and half == 0) else ()))
                        hev = rec("s", (lambda p, pp, fc, tb, reg: lambda s: s.activation(
                            h1T_sb[pp][:, fc * 2048 + tb * 1024: fc * 2048 + (tb + 1) * 1024],
                            bank(reg * 2, 2),
                            AF.Gelu_apprx_tanh, bias=b1mlp_sb[:, p * 2 + fc: p * 2 + fc + 1]))(p, pp, fc, tb, reg),
                            waits=[P.upto(mm)])
                        h1reg_use[reg] = hev
                w1buf_use[pp] = mm
                h1_done_of[pp] = P.upto(hev)
                w2ld_of[pp] = ld2
                if pp == 0:
                    continue
                ya = None
                for ti in range(16):
                    tc = ti
                    yreg = 2 + (ti % 2)
                    wts = [h1_done_of[0], h1_done_of[1], P.upto(w2ld_of[0]), P.upto(w2ld_of[1])]
                    if yreg in yreg_use:
                        wts.append(P.upto(yreg_use[yreg]))
                    mm = None
                    for qq in range(2):
                        for fc in range(2):
                            for half in range(2):
                                mm = rec("t", (lambda qq, ti, fc, half, yreg: lambda t: t.matmul(
                                    bank(yreg * 2 + half, 1),
                                    h1T_sb[qq][:, fc * 2048 + ti * 128: fc * 2048 + (ti + 1) * 128],
                                    w2p_sb[qq][:, fc * 1024 + half * 512: fc * 1024 + (half + 1) * 512],
                                    start=(qq == 0 and fc == 0), stop=(qq == 1 and fc == 1),
                                    skip_group_check=True))(qq, ti, fc, half, yreg),
                                    waits=(wts if (qq == 0 and fc == 0 and half == 0) else ()))
                    wts = [P.upto(mm), tap_refs.get("xres")]
                    if tc in yacc_of:
                        wts.append(P.upto(yacc_of[tc]))
                    ya = rec("v", (lambda tc, yreg: lambda v: v.scalar_tensor_tensor(
                        xres_sb[:, tc * 1024:(tc + 1) * 1024], bank(yreg * 2, 2), 1.0,
                        xres_sb[:, tc * 1024:(tc + 1) * 1024], ALU.mult, ALU.add))(tc, yreg),
                        waits=wts)
                    yacc_of[tc] = ya
                    yreg_use[yreg] = ya
                for _b in (0, 1):
                    w2buf_use[_b] = mm
                    h1buf_use[_b] = mm

        ptr5_bf = ps[:, 5 * 512:6 * 512].bitcast(BF16)
        gw_ev_tr = None
        for k in range(NDC):
            gw_ev_tr = rec("t", (lambda k: lambda t: t.transpose(
                ptr5_bf[:, k * 14:(k + 1) * 14], gwraw_sb[0:E_EXP, k * 128:(k + 1) * 128],
                identb_sb[0:E_EXP, 0:E_EXP]))(k),
                waits=[P.upto(yacc_of[14]), P.upto(ld_gw)] if k == 0 else ())
        gw_ev = rec("v", lambda v: v.tensor_copy(
            gwt_sb[:, 0:NDC * E_EXP], ptr5_bf[:, 0:NDC * E_EXP]), waits=[P.upto(gw_ev_tr)])
        gwt_done = P.upto(gw_ev)
        ld_mb1 = rec("y", lambda y: y.dma_start(mb1raw_sb[:], mb1_d),
                     waits=[P.upto(h1buf_use[1])], dma="mb1")
        mmtr = None
        for fc in range(16):
            mmtr = rec("t", (lambda fc: lambda t: t.transpose(
                ps[0:128, 5 * 512 + 256 + fc * 14: 5 * 512 + 256 + (fc + 1) * 14],
                mb1raw_sb[0:E_EXP, fc * 128:(fc + 1) * 128], identf_sb[0:E_EXP, 0:E_EXP]))(fc),
                waits=[P.upto(ld_mb1)] if fc == 0 else ())
        b1m_ev = rec("v", lambda v: v.tensor_copy(
            b1m_sb[:, 0:224], ps[0:128, 5 * 512 + 256: 5 * 512 + 256 + 224]),
            waits=[P.upto(mmtr)])
        b1m_done = P.upto(b1m_ev)

        x1c_use = {}
        x1t_tr_of = {}
        x1tev_of = {}
        greg_use = {}
        gate_mm_of = {}
        mk_last = None

        def rec_mask_chain(tc):
            par = tc % 2
            lg = bank(4 + par, 1)[:, 0:E_EXP]
            wts0 = [P.upto(gate_mm_of[tc])]
            if (1 - par) in greg_use:
                wts0.append(P.upto(greg_use[1 - par]))
            d1 = rec("v", (lambda lg: lambda v: v.tensor_reduce(m1_sb[:, 0:1], lg, AX.X, ALU.max))(lg),
                waits=wts0)
            rec("v", (lambda lg: lambda v: v.tensor_copy(lm_sb[:], lg))(lg))
            d2 = rec("v", (lambda tc, lg: lambda v: v.tensor_scalar(
                mask_sb[:, tc * E_EXP:(tc + 1) * E_EXP], lg, m1_sb[:, 0:1], None, ALU.is_equal))(tc, lg),
                waits=[P.upto(d1)])
            d3 = rec("v", (lambda tc: lambda v: v.scalar_tensor_tensor(
                lm_sb[:], mask_sb[:, tc * E_EXP:(tc + 1) * E_EXP], -1e30, lm_sb[:],
                ALU.mult, ALU.add))(tc), waits=[P.upto(d2)])
            d4 = rec("v", lambda v: v.tensor_reduce(m2_sb[:, 0:1], lm_sb[:], AX.X, ALU.max),
                waits=[P.upto(d3)])
            d5 = rec("v", (lambda tc: lambda v: v.tensor_tensor(
                dall_sb[:, tc:tc + 1], m2_sb[:, 0:1], m1_sb[:, 0:1], ALU.subtract))(tc),
                waits=[P.upto(d4)])
            d6 = rec("v", (lambda tc: lambda v: v.tensor_scalar(
                eq2all_sb[:, tc * E_EXP:(tc + 1) * E_EXP], lm_sb[:], m2_sb[:, 0:1], None,
                ALU.is_equal))(tc))
            greg_use[par] = d6
            return d6

        def rec_finish(tc):
            slp = tc % 2
            sl4 = tc % 4
            xa = rec("v", (lambda tc, slp, sl4: lambda v: v.tensor_scalar(
                xres_sb[:, tc * 1024:(tc + 1) * 1024], upool_sb[slp][:],
                mu_sb[sl4][:, 0:1], rs_sb[sl4][:, 0:1], ALU.subtract, ALU.mult))(tc, slp, sl4),
                waits=[P.upto(sq2_of[tc])])
            pinslot_use[slp] = xa
            xa_of[tc] = xa
            sl = tc % 2
            wts = [P.upto(xa)]
            if sl in x1c_use:
                wts.append(P.upto(x1c_use[sl]))
            cp = rec("s", (lambda tc, sl: lambda s: s.activation(
                xc_sb[sl][:], xres_sb[:, tc * 1024:(tc + 1) * 1024], AF.Copy))(tc, sl),
                waits=wts)
            wts = [P.upto(cp)]
            if (tc % 2) in x1t_tr_of:
                wts.append(P.upto(x1t_tr_of[tc % 2]))
            pb_ = ptr_of[tc % 2]
            tr = None
            for k in range(NDC):
                tr = rec("t", (lambda sl, k, pb_: lambda t: t.transpose(
                    pb_[:, k * 128:(k + 1) * 128], xc_sb[sl][:, k * 128:(k + 1) * 128],
                    identb_sb[:]))(sl, k, pb_), waits=(wts if k == 0 else ()))
            x1c_use[sl] = tr
            ev = rec("s", (lambda tc, pb_: lambda s: s.activation(
                x1T_sb.ap().rearrange("p (c f) -> p c f", c=NDC)[:, :, tc * 128:(tc + 1) * 128],
                pb_.rearrange("p (c f) -> p c f", c=NDC), AF.Copy))(tc, pb_),
                waits=[P.upto(tr)])
            x1t_tr_of[tc % 2] = ev
            x1tev_of[tc] = ev
            par = tc % 2
            wts = [P.upto(ev), gwt_done]
            if par in greg_use:
                wts.append(P.upto(greg_use[par]))
            mm = None
            for dc in range(NDC):
                mm = rec("t", (lambda tc, dc, par: lambda t: t.matmul(
                    bank(4 + par, 1)[:, 0:E_EXP],
                    x1T_sb[:, dc * T_OWN + tc * 128: dc * T_OWN + (tc + 1) * 128],
                    gwt_sb[:, dc * E_EXP:(dc + 1) * E_EXP],
                    start=(dc == 0), stop=(dc == NDC - 1), skip_group_check=True))(tc, dc, par),
                    waits=(wts if dc == 0 else ()))
            gate_mm_of[tc] = mm

        sq2_of = {}
        for tc in range(NT_OWN):
            ti = tc
            sl3 = tc % 3
            slp = ti % 2
            sl4 = ti % 4
            wts = [P.upto(yacc_of[tc])]
            if sl3 in x0b_use:
                wts.append(P.upto(x0b_use[sl3]))
            x0ld = rec("y", (lambda tc, sl3: lambda y: y.dma_start(
                x0s_sb[sl3][:], xb[tc * 128:(tc + 1) * 128, :]))(tc, sl3), waits=wts,
                dma=("x00", "x01", "x02")[sl3])
            wts = [P.upto(x0ld), P.upto(ld_b2b)]
            if slp in pinslot_use:
                wts.append(P.upto(pinslot_use[slp]))
            a1 = rec("v", (lambda tc, sl3, slp: lambda v: v.tensor_tensor(
                upool_sb[slp][:], xres_sb[:, tc * 1024:(tc + 1) * 1024], x0s_sb[sl3][:],
                ALU.add))(tc, sl3, slp), waits=wts)
            x0b_use[sl3] = a1
            a2 = rec("v", (lambda slp, sl4: lambda v: v.scalar_tensor_tensor(
                upool_sb[slp][:], upool_sb[slp][:], 1.0, b2_b[:], ALU.mult, ALU.add,
                accum_out=sums_sb[sl4][:, 0:1]))(slp, sl4), waits=[P.upto(a1)])
            sqr = rec("s", (lambda slp, sl4: lambda s: s.activation(
                junk_sb[:], upool_sb[slp][:], AF.Square, accum_out=ssq_sb[sl4][:, 0:1]))(slp, sl4),
                waits=[P.upto(a2)])
            rec("v", (lambda sl4: lambda v: v.tensor_scalar(
                mu_sb[sl4][:, 0:1], sums_sb[sl4][:, 0:1], 1.0 / D, None, ALU.mult))(sl4),
                waits=[P.upto(sqr)])
            tpo = rec("v", (lambda sl4: lambda v: v.tensor_scalar(
                tp1_sb[sl4][:, 0:1], ssq_sb[sl4][:, 0:1], 1.0 / D, None, ALU.mult))(sl4))
            c1 = rec("v", (lambda sl4: lambda v: v.tensor_tensor(
                r1_sb[sl4][:, 0:1], mu_sb[sl4][:, 0:1], mu_sb[sl4][:, 0:1], ALU.mult))(sl4),
                waits=[P.upto(tpo)])
            c2 = rec("v", (lambda sl4: lambda v: v.tensor_tensor(
                r1_sb[sl4][:, 0:1], tp1_sb[sl4][:, 0:1], r1_sb[sl4][:, 0:1], ALU.subtract))(sl4),
                waits=[P.upto(c1)])
            c3 = rec("v", (lambda sl4: lambda v: v.tensor_scalar(
                r1_sb[sl4][:, 0:1], r1_sb[sl4][:, 0:1], PLNEPS, None, ALU.add))(sl4),
                waits=[P.upto(c2)])
            c4 = rec("v", (lambda sl4: lambda v: v.reciprocal(r1_sb[sl4][:, 0:1], r1_sb[sl4][:, 0:1]))(sl4),
                waits=[P.upto(c3)])
            sq2 = rec("s", (lambda sl4: lambda s: s.activation(
                rs_sb[sl4][:, 0:1], r1_sb[sl4][:, 0:1], AF.Sqrt))(sl4), waits=[P.upto(c4)])
            sq2_of[tc] = sq2
            if tc >= 1:
                rec_finish(tc - 1)
            if tc >= 2:
                mk_last = rec_mask_chain(tc - 2)
        rec_finish(15)
        rec_mask_chain(14)
        d6f = rec_mask_chain(15)
        x1_last = xa_of[15]
        x1_done = P.upto(x1_last)
        x1T_done = P.upto(x1tev_of[15])
        sgb = rec("s", lambda s_: s_.activation(pall_sb[:], dall_sb[:], AF.Sigmoid),
                  waits=[P.upto(d6f)])
        qb = rec("v", lambda v: v.tensor_scalar(
            qall_sb[:], pall_sb[:], -1.0, 1.0, ALU.mult, ALU.add), waits=[P.upto(sgb)])
        d7_of = {}
        for tc in range(NT_OWN):
            d7_of[tc] = rec("v", (lambda tc: lambda v: v.tensor_scalar(
                mask_sb[:, tc * E_EXP:(tc + 1) * E_EXP],
                mask_sb[:, tc * E_EXP:(tc + 1) * E_EXP],
                qall_sb[:, tc:tc + 1], None, ALU.mult))(tc),
                waits=[P.upto(qb)])
        mkf_of = {}
        for tc in range(NT_OWN):
            mkf_of[tc] = rec("v", (lambda tc: lambda v: v.scalar_tensor_tensor(
                mask_sb[:, tc * E_EXP:(tc + 1) * E_EXP], eq2all_sb[:, tc * E_EXP:(tc + 1) * E_EXP],
                pall_sb[:, tc:tc + 1], mask_sb[:, tc * E_EXP:(tc + 1) * E_EXP],
                ALU.mult, ALU.add))(tc), waits=[P.upto(d7_of[tc])])
        mk2_of = {}
        mk_last = None
        for tc in range(NT_OWN):
            wts = [P.upto(mkf_of[tc])]
            if tc == 0:
                wts.append(P.upto(x1tev_of[15]))
            if tc >= 2:
                wts.append(P.upto(mk2_of[tc - 2]))
            mtr = rec("t", (lambda tc: lambda t: t.transpose(
                ps[0:E_EXP, (6 + tc % 2) * 512: (6 + tc % 2) * 512 + 128],
                mask_sb[:, tc * E_EXP:(tc + 1) * E_EXP], identf_sb[:]))(tc),
                waits=wts)
            mk2_of[tc] = rec("v", (lambda tc: lambda v: v.tensor_copy(
                maskT_sb[0:E_EXP, tc * 128:(tc + 1) * 128],
                ps[0:E_EXP, (6 + tc % 2) * 512: (6 + tc % 2) * 512 + 128]))(tc),
                waits=[P.upto(mtr)])
            mk_last = mk2_of[tc]
        mask_done = P.upto(mk_last)

        if "x1" in taps:
            tap_refs["x1"] = rec("g", lambda g: g.dma_start(tap_t["x1"], xres_sb[:]), waits=[x1_done], dma="tap_x1")
        if "mask" in taps:
            tap_refs["mask"] = rec("g", lambda g: g.dma_start(tap_t["mask"], mask_sb[:]), waits=[mask_done], dma="tap_mask")

        _maybe_stop("mlp", x1_done)
        _maybe_stop("gate", mask_done)

        accw = {}
        outreg_use = {}
        hhreg_use = {}
        w1buf2_use = {}
        w2buf2_of = {}
        hhslot_use = {}
        pc_use = None
        store_last = None
        interleave_pl2 = "acc0" not in taps

        def rec_postln2(g_i, ti):
            nonlocal pc_use, store_last
            tc = g_i * 8 + ti
            sl4 = ti % 4
            x1t = xres_sb[:, tc * 1024:(tc + 1) * 1024]
            wts = [P.upto(accw[ti])]
            if pc_use is not None:
                wts.append(P.upto(pc_use))
            wts.append(tap_refs.get("acc0"))
            f1 = rec("v", (lambda ti, x1t: lambda v: v.tensor_tensor(
                pool_c[:], x1t, acc_sb[:, ti * 1024:(ti + 1) * 1024], ALU.add))(ti, x1t),
                waits=wts)
            f2 = rec("v", lambda v: v.tensor_scalar(
                pool_c[:], pool_c[:], 100.0, -100.0, ALU.min, ALU.max),
                waits=[P.upto(f1)])
            sa = rec("v", (lambda ti, x1t, sl4: lambda v: v.scalar_tensor_tensor(
                acc_sb[:, ti * 1024:(ti + 1) * 1024], pool_c[:], 1.0, x1t, ALU.mult, ALU.add,
                accum_out=sums_sb[sl4][:, 0:1]))(ti, x1t, sl4), waits=[P.upto(f2)])
            pc_use = sa
            sqr = rec("s", (lambda ti, sl4: lambda s: s.activation(
                junk_sb[:], acc_sb[:, ti * 1024:(ti + 1) * 1024], AF.Square,
                accum_out=ssq_sb[sl4][:, 0:1]))(ti, sl4), waits=[P.upto(sa)])
            rec("v", (lambda sl4: lambda v: v.tensor_scalar(
                mu_sb[sl4][:, 0:1], sums_sb[sl4][:, 0:1], 1.0 / D, None, ALU.mult))(sl4),
                waits=[P.upto(sqr)])
            tpo = rec("v", (lambda sl4: lambda v: v.tensor_scalar(
                tp1_sb[sl4][:, 0:1], ssq_sb[sl4][:, 0:1], 1.0 / D, None, ALU.mult))(sl4))
            c1 = rec("v", (lambda sl4: lambda v: v.tensor_tensor(
                r1_sb[sl4][:, 0:1], mu_sb[sl4][:, 0:1], mu_sb[sl4][:, 0:1], ALU.mult))(sl4),
                waits=[P.upto(tpo)])
            c2 = rec("v", (lambda sl4: lambda v: v.tensor_tensor(
                r1_sb[sl4][:, 0:1], tp1_sb[sl4][:, 0:1], r1_sb[sl4][:, 0:1], ALU.subtract))(sl4),
                waits=[P.upto(c1)])
            c3 = rec("v", (lambda sl4: lambda v: v.tensor_scalar(
                r1_sb[sl4][:, 0:1], r1_sb[sl4][:, 0:1], PLNEPS, None, ALU.add))(sl4),
                waits=[P.upto(c2)])
            c4 = rec("v", (lambda sl4: lambda v: v.reciprocal(r1_sb[sl4][:, 0:1], r1_sb[sl4][:, 0:1]))(sl4),
                waits=[P.upto(c3)])
            sq2 = rec("s", (lambda sl4: lambda s: s.activation(
                rs_sb[sl4][:, 0:1], r1_sb[sl4][:, 0:1], AF.Sqrt))(sl4), waits=[P.upto(c4)])
            oa = rec("v", (lambda tc, ti, sl4: lambda v: v.tensor_scalar(
                xres_sb[:, tc * 1024:(tc + 1) * 1024], acc_sb[:, ti * 1024:(ti + 1) * 1024],
                mu_sb[sl4][:, 0:1], rs_sb[sl4][:, 0:1], ALU.subtract, ALU.mult))(tc, ti, sl4),
                waits=[P.upto(sq2), tap_refs.get("x1")])
            st_op = rec("y", (lambda tc: lambda y: y.dma_start(
                out_d[tc * 128:(tc + 1) * 128, :],
                xres_sb[:, tc * 1024:(tc + 1) * 1024]))(tc),
                waits=[P.upto(oa)], dma="st")
            store_last = st_op

        for g_i in range(2):
            for ti in range(8):
                tc = g_i * 8 + ti
                yreg = 2 + (ti % 2)
                wts = [mask_done, P.upto(ld_b2m)]
                if yreg in outreg_use:
                    wts.append(P.upto(outreg_use[yreg]))
                mm = None
                for half in range(2):
                    mm = rec("t", (lambda tc, half, yreg: lambda t: t.matmul(
                        bank(yreg * 2 + half, 1),
                        maskT_sb[0:E_EXP, tc * 128:(tc + 1) * 128],
                        b2m_sb[0:E_EXP, half * 512:(half + 1) * 512],
                        start=True, stop=True, skip_group_check=True))(tc, half, yreg),
                        waits=(wts if half == 0 else ()))
                wts = [P.upto(mm)]
                if ti in accw:
                    wts.append(P.upto(accw[ti]))
                av = rec("v", (lambda ti, yreg: lambda v: v.tensor_copy(
                    acc_sb[:, ti * 1024:(ti + 1) * 1024], bank(yreg * 2, 2)))(ti, yreg),
                    waits=wts)
                accw[ti] = av
                outreg_use[yreg] = av

            for e_i in range(E_EXP):
                for q4 in range(4):
                    hh_done_of = {}
                    ld2_of = {}
                    for hs in range(2):
                        q8 = q4 * 2 + hs
                        u = hs
                        wts = [P.upto(w1buf_use[u])]
                        if u in w1buf2_use:
                            wts.append(P.upto(w1buf2_use[u]))
                        ld1 = rec("g", (lambda e_i, q8, u: lambda g: g.dma_start(
                            mw1_sb[u].ap().rearrange("p (c f) -> p c f", c=NDC),
                            mw1_d[e_i].rearrange("(c p) f -> p c f", p=128)[:, :, q8 * 256:(q8 + 1) * 256]))(e_i, q8, u),
                            waits=wts, dma=("m1a" if u == 0 else "m1b"))
                        pr = (e_i * 4 + q4) % 2
                        wts = []
                        if hs == 0:
                            wts = [P.upto(w2buf_use[0])] if pr not in w2buf2_of else [P.upto(w2buf2_of[pr])]
                        ld2_of[hs] = rec("g", (lambda e_i, q8, hs, pr: lambda g: g.dma_start(
                            mw2_sb[pr].ap().rearrange("p (hs c f) -> p hs c f", hs=2, c=2)[:, hs],
                            mw2_d[e_i].rearrange("(c p) f -> p c f", p=128)[:, 2 * q8:2 * q8 + 2, :]))(e_i, q8, hs, pr),
                            waits=wts, dma=("m2" if pr == 0 else "m2b"))
                        hev = None
                        mm = None
                        for fc in range(2):
                            reg = fc
                            wts = [P.upto(ld1), b1m_done]
                            if reg in hhreg_use:
                                wts.append(P.upto(hhreg_use[reg]))
                            if hs in hhslot_use:
                                wts.append(P.upto(hhslot_use[hs]))
                            for dc in range(NDC):
                                for half in range(2):
                                    mm = rec("t", (lambda g_i, u, fc, dc, half, reg: lambda t: t.matmul(
                                        bank(reg * 2 + half, 1),
                                        mw1_sb[u][:, dc * 256 + fc * 128: dc * 256 + (fc + 1) * 128],
                                        x1T_sb[:, dc * T_OWN + g_i * 1024 + half * 512: dc * T_OWN + g_i * 1024 + (half + 1) * 512],
                                        start=(dc == 0), stop=(dc == NDC - 1), skip_group_check=True))(g_i, u, fc, dc, half, reg),
                                        waits=(wts if (dc == 0 and half == 0) else ()))
                            hev = rec("s", (lambda e_i, q8, fc, hs, reg: lambda s: s.activation(
                                hhT_sb[:, (2 * hs + fc) * 1024:(2 * hs + fc + 1) * 1024],
                                bank(reg * 2, 2), AF.Gelu,
                                bias=b1m_sb[:, (q8 * 2 + fc) * E_EXP + e_i: (q8 * 2 + fc) * E_EXP + e_i + 1]))(e_i, q8, fc, hs, reg),
                                waits=[P.upto(mm)])
                            hhreg_use[reg] = hev
                        w1buf2_use[u] = mm
                        hh_done_of[hs] = P.upto(hev)
                    mm = None
                    av = None
                    for ti in range(8):
                        tc = g_i * 8 + ti
                        yreg = 2 + (ti % 2)
                        wts = [hh_done_of[0], hh_done_of[1], P.upto(ld2_of[1])]
                        if yreg in outreg_use:
                            wts.append(P.upto(outreg_use[yreg]))
                        for hs in range(2):
                            for fc in range(2):
                                for half in range(2):
                                    mm = rec("t", (lambda hs, ti, fc, half, yreg, pr: lambda t: t.matmul(
                                        bank(yreg * 2 + half, 1),
                                        hhT_sb[:, (2 * hs + fc) * 1024 + ti * 128:(2 * hs + fc) * 1024 + (ti + 1) * 128],
                                        mw2_sb[pr].ap().rearrange("p (hs c f) -> p hs c f", hs=2, c=2)[:, hs, fc, half * 512:(half + 1) * 512],
                                        start=(hs == 0 and fc == 0), stop=(hs == 1 and fc == 1),
                                        skip_group_check=True))(hs, ti, fc, half, yreg, pr),
                                        waits=(wts if (hs == 0 and fc == 0 and half == 0) else ()))
                        wts = [P.upto(mm), P.upto(accw[ti])]
                        av = rec("v", (lambda e_i, tc, ti, yreg: lambda v: v.scalar_tensor_tensor(
                            acc_sb[:, ti * 1024:(ti + 1) * 1024], bank(yreg * 2, 2),
                            mask_sb[:, tc * E_EXP + e_i: tc * E_EXP + e_i + 1],
                            acc_sb[:, ti * 1024:(ti + 1) * 1024], ALU.mult, ALU.add))(e_i, tc, ti, yreg),
                            waits=wts)
                        accw[ti] = av
                        outreg_use[yreg] = av
                        if interleave_pl2 and e_i == E_EXP - 1 and q4 == 3:
                            rec_postln2(g_i, ti)
                    w2buf2_of[pr] = mm
                    hhslot_use[0] = mm
                    hhslot_use[1] = mm

            if g_i == 0 and "acc0" in taps:
                tap_refs["acc0"] = rec("g", lambda g: g.dma_start(tap_t["acc0"], acc_sb[:]),
                    waits=[P.upto(accw[7])], dma="tap_acc0")

            if not interleave_pl2:
                for ti in range(8):
                    rec_postln2(g_i, ti)

        rec("y", lambda y: y.nop(), waits=[P.upto(store_last)])

    except _StopBuild:
        pass

    with nc.Block() as block:
        P.emit(block)
    ctx.close()
    return nc


_NC_CACHE = {}


def _get_nc(debug_taps=()):
    key = tuple(sorted(debug_taps))
    if key not in _NC_CACHE:
        _NC_CACHE[key] = build_nc(debug_taps=key)
    return _NC_CACHE[key]


def kernel(
    L0_attn_inputs, L0_attn_ln1_scale, L0_attn_ln1_bias,
    L0_attn_Wq, L0_attn_bq, L0_attn_Wk, L0_attn_bk, L0_attn_Wv, L0_attn_bv,
    L0_attn_Wo, L0_attn_bo, L0_attn_ln2_scale, L0_attn_ln2_bias,
    L0_attn_W1, L0_attn_b1, L0_attn_W2, L0_attn_b2,
    L0_ffn_gate_w, L0_ffn_w1, L0_ffn_b1, L0_ffn_w2, L0_ffn_b2,
    _debug_taps=(),
):
    x = np.ascontiguousarray(np.asarray(L0_attn_inputs, dtype=np.float32))
    wk = np.asarray(L0_attn_Wk, dtype=np.float32).reshape(D, D)
    wv = np.asarray(L0_attn_Wv, dtype=np.float32).reshape(D, D)
    common = dict(
        wq=np.ascontiguousarray(np.asarray(L0_attn_Wq, dtype=np.float32).reshape(D, D)),
        wkv=np.ascontiguousarray(np.concatenate([wk, wv], axis=1)),
        bqv=np.ascontiguousarray(np.asarray(L0_attn_bq, dtype=np.float32).reshape(-1)),
        bkv=np.ascontiguousarray(np.concatenate(
            [np.asarray(L0_attn_bk, dtype=np.float32).reshape(-1),
             np.asarray(L0_attn_bv, dtype=np.float32).reshape(-1)])),
        ln1s=np.ascontiguousarray(np.asarray(L0_attn_ln1_scale, dtype=np.float32)),
        ln1b=np.ascontiguousarray(np.asarray(L0_attn_ln1_bias, dtype=np.float32)),
        wo=np.ascontiguousarray(np.asarray(L0_attn_Wo, dtype=np.float32).reshape(D, D)),
        bo=np.ascontiguousarray(np.asarray(L0_attn_bo, dtype=np.float32)),
        ln2s=np.ascontiguousarray(np.asarray(L0_attn_ln2_scale, dtype=np.float32)),
        ln2b=np.ascontiguousarray(np.asarray(L0_attn_ln2_bias, dtype=np.float32)),
        w1=np.ascontiguousarray(np.asarray(L0_attn_W1, dtype=np.float32)),
        b1=np.ascontiguousarray(np.asarray(L0_attn_b1, dtype=np.float32)),
        w2=np.ascontiguousarray(np.asarray(L0_attn_W2, dtype=np.float32)),
        b2=np.ascontiguousarray(np.asarray(L0_attn_b2, dtype=np.float32)),
        gw=np.ascontiguousarray(np.asarray(L0_ffn_gate_w, dtype=np.float32)),
        mw1=np.ascontiguousarray(np.asarray(L0_ffn_w1, dtype=np.float32)),
        mb1=np.ascontiguousarray(np.asarray(L0_ffn_b1, dtype=np.float32)),
        mw2=np.ascontiguousarray(np.asarray(L0_ffn_w2, dtype=np.float32)),
        mb2=np.ascontiguousarray(np.asarray(L0_ffn_b2, dtype=np.float32)),
    )
    in_maps = []
    for c in range(N_CORES):
        b = c // 4
        s0 = (c % 4) * T_OWN
        m = dict(common)
        m["xb"] = np.ascontiguousarray(np.roll(x[b], -s0, axis=0))
        in_maps.append(m)
    nc = _get_nc(_debug_taps)
    res = run_bass_kernel_spmd(nc, in_maps, core_ids=list(range(N_CORES)))
    outs = [res.results[c]["out"] for c in range(N_CORES)]
    y = np.concatenate(outs, axis=0).reshape(2, 8192, D)
    if _debug_taps:
        return y, res
    return y


# version 08 (final)
# speedup vs baseline: 1.1357x; 1.0054x over previous
import numpy as np
import ml_dtypes
import concourse.bass as bass
import concourse.mybir as mybir
from concourse.bass_utils import run_bass_kernel_spmd

F32 = mybir.dt.float32
BF16 = mybir.dt.bfloat16
AF = mybir.ActivationFunctionType
ALU = mybir.AluOpType
AX = mybir.AxisListType

N_CORES = 8
D = 1024
NDC = 8
T_OWN = 2048
NT_OWN = 16
NT_ALL = 64
H = 16
E_EXP = 14
KEPS = 0.001
LN1EPS = 1e-6
PLNEPS = 1e-5


class Prog:

    ENGINES = ("t", "v", "s", "g", "y")

    def __init__(self, nc):
        self.nc = nc
        self.ops = {e: [] for e in self.ENGINES}
        self.sems = {}
        self.dma_count = {}
        self.group_sems = set()

    def make_sems(self, ctx, dma_streams, group_streams):
        for e in self.ENGINES:
            self.sems[e] = ctx.enter_context(self.nc.semaphore(f"sem_{e}"))
        for nm in dma_streams:
            self.sems[nm] = ctx.enter_context(self.nc.semaphore(f"sem_{nm}"))
            self.dma_count[nm] = 0
        self.group_sems = set(group_streams)

    def op(self, eng, fn, waits=(), dma=False):
        assert eng in self.ENGINES
        w = [x for x in waits if x is not None]
        if dma:
            assert isinstance(dma, str) and dma in self.dma_count, dma
            self.dma_count[dma] += 1
            self.ops[eng].append([fn, w, True, dma])
            return ("D", dma, self.dma_count[dma])
        self.ops[eng].append([fn, w, False, None])
        return (eng, len(self.ops[eng]) - 1)

    def upto(self, ref):
        if ref[0] == "D":
            return ref
        eng, i = ref
        self.ops[eng][i][2] = True
        return ref

    def last(self, eng):
        assert self.ops[eng]
        return (eng, len(self.ops[eng]) - 1)

    def upto_last(self, eng):
        lst = self.ops[eng]
        for i in range(len(lst) - 1, -1, -1):
            if lst[i][3] is None:
                lst[i][2] = True
                return (eng, i)
        raise AssertionError(f"no compute op on {eng}")

    def emit(self, block):
        cum = {}
        for e in self.ENGINES:
            c = 0
            lst = []
            for rec in self.ops[e]:
                if rec[2] and rec[3] is None:
                    c += 1
                lst.append(c)
            cum[e] = lst

        def wait_val(ref):
            if ref[0] == "D":
                _, nm, idx = ref
                if nm in self.group_sems:
                    return (nm, 16 * self.dma_count[nm])
                return (nm, 16 * idx)
            we, wi = ref
            return (we, cum[we][wi])

        def run(eng_obj, e):
            for rec in self.ops[e]:
                fn, waits, marked, dmasem = rec
                done = set()
                for ref in waits:
                    nm, val = wait_val(ref)
                    if (nm, val) in done:
                        continue
                    done.add((nm, val))
                    eng_obj.wait_ge(self.sems[nm], val)
                instr = fn(eng_obj)
                if dmasem is not None:
                    instr.then_inc(self.sems[dmasem], 16)
                elif marked:
                    assert instr is not None, f"marked op on {e} returned None"
                    instr.then_inc(self.sems[e], 1)

        block.tensor(lambda t: run(t, "t"))
        block.vector(lambda v: run(v, "v"))
        block.scalar(lambda s: run(s, "s"))
        block.gpsimd(lambda g: run(g, "g"))
        block.sync(lambda y: run(y, "y"))


class _StopBuild(Exception):
    pass


def build_nc(debug_taps=(), stop_after=None):
    nc = bass.Bass("TRN2", target_bir_lowering=False)
    taps = set(debug_taps)

    xb = nc.dram_tensor("xb", [8192, D], F32, kind="ExternalInput").ap()
    wq_d = nc.dram_tensor("wq", [D, D], F32, kind="ExternalInput").ap()
    wkv_d = nc.dram_tensor("wkv", [D, 2 * D], F32, kind="ExternalInput").ap()
    bq_d = nc.dram_tensor("bqv", [D], F32, kind="ExternalInput").ap()
    bkv_d = nc.dram_tensor("bkv", [2 * D], F32, kind="ExternalInput").ap()
    ln1s_d = nc.dram_tensor("ln1s", [D], F32, kind="ExternalInput").ap()
    ln1b_d = nc.dram_tensor("ln1b", [D], F32, kind="ExternalInput").ap()
    wo_d = nc.dram_tensor("wo", [D, D], F32, kind="ExternalInput").ap()
    bo_d = nc.dram_tensor("bo", [D], F32, kind="ExternalInput").ap()
    ln2s_d = nc.dram_tensor("ln2s", [D], F32, kind="ExternalInput").ap()
    ln2b_d = nc.dram_tensor("ln2b", [D], F32, kind="ExternalInput").ap()
    w1_d = nc.dram_tensor("w1", [D, 4096], F32, kind="ExternalInput").ap()
    b1_d = nc.dram_tensor("b1", [4096], F32, kind="ExternalInput").ap()
    w2_d = nc.dram_tensor("w2", [4096, D], F32, kind="ExternalInput").ap()
    b2_d = nc.dram_tensor("b2", [D], F32, kind="ExternalInput").ap()
    gw_d = nc.dram_tensor("gw", [E_EXP, D], F32, kind="ExternalInput").ap()
    mw1_d = nc.dram_tensor("mw1", [E_EXP, D, 2048], F32, kind="ExternalInput").ap()
    mb1_d = nc.dram_tensor("mb1", [E_EXP, 2048], F32, kind="ExternalInput").ap()
    mw2_d = nc.dram_tensor("mw2", [E_EXP, 2048, D], F32, kind="ExternalInput").ap()
    mb2_d = nc.dram_tensor("mb2", [E_EXP, D], F32, kind="ExternalInput").ap()
    out_d = nc.dram_tensor("out", [T_OWN, D], F32, kind="ExternalOutput").ap()

    tap_t = {}
    tap_shapes = dict(
        xnT=([128, NDC * T_OWN], F32), phiq=([128, NDC * T_OWN], F32),
        kvsb=([128, 8 * 136], F32), attn=([128, NT_OWN * 1024], F32),
        xres=([128, NT_OWN * 1024], F32), x1=([128, NT_OWN * 1024], F32),
        mask=([128, NT_OWN * E_EXP], F32), acc0=([128, 8 * 1024], F32),
        y2T=([128, NDC * 1024], F32),
    )
    for name in taps:
        shp, dt_ = tap_shapes[name]
        tap_t[name] = nc.dram_tensor("tap_" + name, shp, dt_, kind="ExternalOutput").ap()

    identb = nc.inline_tensor(np.eye(128, dtype=ml_dtypes.bfloat16), name="identb")
    identf = nc.inline_tensor(np.eye(128, dtype=np.float32), name="identf")
    ones128 = nc.inline_tensor(np.ones((128, 128), dtype=ml_dtypes.bfloat16), name="ones128")

    A0, B0, C0, D0, E0 = 20480, 86016, 151552, 184320, 217088

    def sb(name, shape, dtype, off):
        return nc.alloc_sbuf_tensor_at(name, shape, dtype, offset=off)

    wkv_sb = sb("wkv_sb", [128, NDC * 2048], BF16, A0)
    xt_sb = [sb(f"xt{i}", [128, 1024], F32, A0 + 32768 + i * 4096) for i in range(2)]
    xc_sb = [sb(f"xc{i}", [128, 1024], BF16, A0 + 40960 + i * 2048) for i in range(2)]
    phik_sb = [sb(f"phik{i}", [128, 1024], BF16, A0 + 45056 + i * 2048) for i in range(2)]
    vtok_sb = [sb(f"vtok{i}", [128, 1024], BF16, A0 + 49152 + i * 2048) for i in range(2)]
    brep_sb = sb("brep", [128, 1024], BF16, A0 + 53248)
    kvbias_b = sb("kvbias_b", [128, 2048], F32, A0 + 55296)
    gwraw_sb = sb("gwraw", [E_EXP, 1024], BF16, A0 + 63488)
    kv_blk = sb("kv_blk", [128, 8 * 136], BF16, A0 + 55296)
    wo_sb = sb("wo_sb", [128, NDC * 1024], BF16, A0)
    x0s_sb = [sb(f"x0s{i}", [128, 1024], F32, A0 + 16384 + i * 4096) for i in range(3)]
    bo_b = sb("bo_b", [128, 1024], F32, A0 + 28672)
    b2_b = sb("b2_b", [128, 1024], F32, A0 + 32768)
    mb1raw_sb = sb("mb1raw", [E_EXP, 2048], F32, A0 + 45056)
    acc_sb = sb("acc_sb", [128, 8 * 1024], F32, A0)
    pool_c = sb("pool_c", [128, 1024], F32, A0 + 36864)
    pool_o = sb("pool_o", [128, 1024], F32, A0 + 40960)

    wq_sb = sb("wq_sb", [128, NDC * 1024], BF16, B0)
    vraw_sb = sb("vraw", [8, 5 * 128], F32, B0 + 16384)
    braw_sb = sb("braw", [32, 128], F32, B0 + 20480)
    xres_sb = sb("xres_sb", [128, NT_OWN * 1024], F32, B0)

    xnT_sb = sb("xnT_sb", [128, NDC * T_OWN], BF16, C0)
    attn_sb = sb("attn_sb", [128, NT_OWN * 1024], BF16, C0)
    y2T_sb = sb("y2T_sb", [128, NDC * 2048], BF16, C0)
    upool_sb = [sb(f"upool{i}", [128, 1024], F32, A0 + i * 4096) for i in range(2)]
    x1T_sb = sb("x1T_sb", [128, NDC * T_OWN], BF16, C0)

    phiq_sb = sb("phiq_sb", [128, NDC * T_OWN], BF16, D0)
    attnT_sb = sb("attnT_sb", [128, NDC * T_OWN], BF16, D0)
    w1p_sb = [sb(f"w1p{i}", [128, NDC * 256], BF16, D0 + i * 4096) for i in range(2)]
    w2p_sb = [sb(f"w2p{i}", [128, 2 * 1024], BF16, D0 + 8192 + i * 4096) for i in range(2)]
    h1T_sb = [sb(f"h1T{i}", [128, 2 * 2048], BF16, D0 + 16384 + i * 8192) for i in range(2)]
    junk_sb = sb("junk_sb", [128, 1024], BF16, A0 + 49152)
    maskT_sb = sb("maskT_sb", [E_EXP, NT_OWN * 128], BF16, A0 + 57472)
    mw1_sb = [sb(f"mw1s{i}", [128, NDC * 256], BF16, D0 + i * 4096) for i in range(2)]
    mw2_sb = [sb(f"mw2s{i}", [128, 4 * 1024], BF16, D0 + 8192 + i * 8192) for i in range(2)]
    hhT_sb = sb("hhT_sb", [128, 4 * 1024], BF16, D0 + 24576)

    e = E0
    identb_sb = sb("identb_sb", [128, 128], BF16, e); e += 256
    identf_sb = sb("identf_sb", [128, 128], F32, e); e += 512
    ones_sb = sb("ones_sb", [128, 128], BF16, e); e += 256
    gwt_sb = sb("gwt_sb", [128, NDC * E_EXP], BF16, e); e += 256
    b1m_sb = sb("b1m_sb", [128, 16 * E_EXP], F32, e); e += 896
    mask_sb = sb("mask_sb", [128, NT_OWN * E_EXP], F32, e); e += 896
    b2m_sb = sb("b2m_sb", [E_EXP, 1024], BF16, e); e += 2048
    kv_sb = sb("kv_sb", [128, 8 * 72], BF16, e); e += 1184
    bqe_sb = sb("bqe_sb", [128, NDC], F32, e); e += 32
    g1c_sb = sb("g1c_sb", [128, NDC], F32, e); e += 32
    b1c_sb = sb("b1c_sb", [128, NDC], BF16, e); e += 32
    b1cf_sb = sb("b1cf_sb", [128, NDC], F32, e); e += 32
    g2c_sb = sb("g2c_sb", [128, NDC], F32, e); e += 32
    b2c_sb = sb("b2c_sb", [128, NDC], F32, e); e += 32
    bqc_sb = sb("bqc_sb", [128, NDC], F32, e); e += 32
    b1mlp_sb = sb("b1mlp_sb", [128, 32], F32, e); e += 128
    st_sb = [sb(f"st{i}", [128, 12], F32, e + i * 64) for i in range(2)]; e += 128
    mv_sb = [sb(f"mv{i}", [128, 8], F32, e + i * 32) for i in range(4)]; e += 128
    r1_sb = [sb(f"r1_{i}", [128, 8], F32, e + i * 32) for i in range(4)]; e += 128
    rs_sb = [sb(f"rs{i}", [128, 8], F32, e + i * 32) for i in range(4)]; e += 128
    rd_sb = [sb(f"rd{i}", [128, 16], F32, e + i * 64) for i in range(2)]; e += 128
    sums_sb = [sb(f"sums{i}", [128, 8], F32, e + i * 32) for i in range(4)]; e += 128
    ssq_sb = [sb(f"ssq{i}", [128, 8], F32, e + i * 32) for i in range(4)]; e += 128
    mu_sb = [sb(f"mu{i}", [128, 8], F32, e + i * 32) for i in range(4)]; e += 128
    tp1_sb = [sb(f"tp1_{i}", [128, 8], F32, e + i * 32) for i in range(4)]; e += 128
    m1_sb = sb("m1_sb", [128, 8], F32, e); e += 32
    m2_sb = sb("m2_sb", [128, 8], F32, e); e += 32
    w0_sb = sb("w0_sb", [128, 8], F32, e); e += 32
    w1w_sb = sb("w1w_sb", [128, 8], F32, e); e += 32
    eq1_sb = sb("eq1_sb", [128, E_EXP], F32, e); e += 64
    eq2_sb = sb("eq2_sb", [128, E_EXP], F32, e); e += 64
    lm_sb = sb("lm_sb", [128, E_EXP], F32, e); e += 64
    assert e <= 229344, f"E overflow: {e}"

    ps = nc.alloc_psum_tensor("ps", [128, 4096], F32)

    def bank(b0, nb):
        return ps[:, b0 * 512:(b0 + nb) * 512]

    ptr_bf = ps[:, 7 * 512:8 * 512].bitcast(BF16)
    ptr_bf2 = ps[:, 6 * 512:7 * 512].bitcast(BF16)
    ptr_of = (ptr_bf, ptr_bf2)
    kv_ps = ps[0:64, 4 * 512:4 * 512 + 1024]
    ks_ps = ps[0:64, 6 * 512:6 * 512 + 16]

    P = Prog(nc)
    import contextlib
    ctx = contextlib.ExitStack()
    DMA_STREAMS = ["cst", "w0", "wo", "wol", "bo", "b2b", "mb1", "xb0", "xb1",
                   "x00", "x01", "x02", "st", "w1a", "w1b", "w2a", "w2b",
                   "m1a", "m1b", "m2", "m2b"] + ["tap_" + t for t in
                   ("xnT", "phiq", "kvsb", "attn", "xres", "x1", "mask", "acc0", "y2T")]
    P.make_sems(ctx, DMA_STREAMS, group_streams=["cst", "w0"])

    def rec(eng, fn, waits=(), dma=False):
        return P.op(eng, fn, waits=waits, dma=dma)

    def _maybe_stop(stage, wait=None):
        if stop_after != stage:
            return
        last = None
        for tc in range(NT_OWN):
            last = rec("y", (lambda tc: lambda y: y.dma_start(
                out_d[tc * 128:(tc + 1) * 128, :], acc_sb[:, 0:1024]))(tc),
                waits=([wait] if (wait is not None and tc == 0) else ()), dma="st")
        rec("y", lambda y: y.nop(), waits=[P.upto(last)])
        raise _StopBuild()

    try:
        ld_ident = rec("y", lambda y: y.dma_start(identb_sb[:], identb.ap()), dma="cst")
        ld_identf = rec("y", lambda y: y.dma_start(identf_sb[:], identf.ap()), dma="cst")
        ld_ones = rec("y", lambda y: y.dma_start(ones_sb[:], ones128.ap()), dma="cst")
        vsrcs = [ln1s_d, ln1b_d, ln2s_d, ln2b_d, bq_d]
        ld_vraw = None
        for i, src in enumerate(vsrcs):
            ld_vraw = rec("y", (lambda i, src: lambda y: y.dma_start(
                vraw_sb[0:8, i * 128:(i + 1) * 128],
                src.rearrange("(a b) -> a b", a=8)))(i, src), dma="cst")
        ld_braw = rec("y", lambda y: y.dma_start(
            braw_sb[:], b1_d.rearrange("(a b) -> a b", a=32)), dma="cst")
        ld_bias_tiles = rec("y", lambda y: y.dma_start(
            kvbias_b[:], bkv_d[None, :].broadcast_to([128, 2048])), dma="cst")
        ld_gw = rec("g", lambda g: g.dma_start(gwraw_sb[:], gw_d), dma="w0")
        ld_b2m = rec("g", lambda g: g.dma_start(b2m_sb[:], mb2_d), dma="w0")
        ld_wkv = rec("g", lambda g: g.dma_start(
            wkv_sb.ap().rearrange("p (c f) -> p c f", c=NDC),
            wkv_d.rearrange("(c p) f -> p c f", p=128)), dma="w0")
        ld_wq = rec("g", lambda g: g.dma_start(
            wq_sb.ap().rearrange("p (c f) -> p c f", c=NDC),
            wq_d.rearrange("(c p) f -> p c f", p=128)), dma="w0")

        tp_ev = {}
        vec_tr = None
        for i, dstc in enumerate([g1c_sb, None, g2c_sb, b2c_sb, bqc_sb]):
            vec_tr = rec("t", (lambda i: lambda t: t.transpose(
                ps[0:128, 6 * 512 + i * 8: 6 * 512 + (i + 1) * 8],
                vraw_sb[0:8, i * 128:(i + 1) * 128], identf_sb[0:8, 0:8]))(i),
                waits=[P.upto(ld_vraw), P.upto(ld_identf)] if i == 0 else ())
        vec_tr2 = rec("t", lambda t: t.transpose(
            ps[0:128, 6 * 512 + 40: 6 * 512 + 72], braw_sb[:], identf_sb[0:32, 0:32]),
            waits=[P.upto(ld_braw)])
        evs = []
        for i, dstc in enumerate([g1c_sb, None, g2c_sb, b2c_sb, bqc_sb]):
            if dstc is None:
                evs.append(rec("v", (lambda i: lambda v: v.tensor_copy(
                    b1c_sb[:, 0:NDC], ps[0:128, 6 * 512 + i * 8: 6 * 512 + i * 8 + 8]))(i),
                    waits=[P.upto(vec_tr2)] if i == 0 else ()))
                evs.append(rec("v", (lambda i: lambda v: v.tensor_copy(
                    b1cf_sb[:, 0:NDC], ps[0:128, 6 * 512 + i * 8: 6 * 512 + i * 8 + 8]))(i)))
            elif dstc is bqc_sb:
                evs.append(rec("v", (lambda i, dstc: lambda v: v.tensor_scalar(
                    dstc[:, 0:NDC], ps[0:128, 6 * 512 + i * 8: 6 * 512 + i * 8 + 8],
                    KEPS, None, ALU.add))(i, dstc),
                    waits=[P.upto(vec_tr2)] if i == 0 else ()))
            else:
                evs.append(rec("v", (lambda i, dstc: lambda v: v.tensor_copy(
                    dstc[:, 0:NDC], ps[0:128, 6 * 512 + i * 8: 6 * 512 + i * 8 + 8]))(i, dstc),
                    waits=[P.upto(vec_tr2)] if i == 0 else ()))
        evs.append(rec("v", lambda v: v.tensor_copy(
            b1mlp_sb[:], ps[0:128, 6 * 512 + 40: 6 * 512 + 72])))
        cols_done = P.upto_last("v")

        sfold = None
        for dc in range(NDC):
            sfold = rec("s", (lambda dc: lambda s: s.activation(
                wkv_sb[:, dc * 2048:(dc + 1) * 2048], wkv_sb[:, dc * 2048:(dc + 1) * 2048],
                AF.Copy, scale=g1c_sb[:, dc:dc + 1]))(dc),
                waits=[P.upto(ld_wkv), cols_done] if dc == 0 else ())
        for dc in range(NDC):
            sfold = rec("s", (lambda dc: lambda s: s.activation(
                wq_sb[:, dc * 1024:(dc + 1) * 1024], wq_sb[:, dc * 1024:(dc + 1) * 1024],
                AF.Copy, scale=g1c_sb[:, dc:dc + 1]))(dc),
                waits=[P.upto(ld_wq)] if dc == 0 else ())
        for dc in range(NDC):
            sfold = rec("s", (lambda dc: lambda s: s.activation(
                brep_sb[:, dc * 128:(dc + 1) * 128], ones_sb[:],
                AF.Copy, scale=b1cf_sb[:, dc:dc + 1]))(dc),
                waits=[P.upto(ld_ones)] if dc == 0 else ())
        w_folded = P.upto(sfold)

        mm = None
        for dc in range(NDC):
            for bk_ in range(4):
                mm = rec("t", (lambda dc, bk_: lambda t: t.matmul(
                    bank(bk_, 1), brep_sb[:, dc * 128:(dc + 1) * 128],
                    wkv_sb[:, dc * 2048 + bk_ * 512: dc * 2048 + (bk_ + 1) * 512],
                    start=(dc == 0), stop=(dc == NDC - 1), skip_group_check=True))(dc, bk_),
                    waits=[w_folded] if (dc == 0 and bk_ == 0) else ())
        kvb_add = rec("v", lambda v: v.tensor_tensor(
            kvbias_b[:], kvbias_b[:], bank(0, 4), ALU.add),
            waits=[P.upto(mm), P.upto(ld_bias_tiles)])
        kvbias_done = P.upto(kvb_add)

        mm = None
        for mc in range(NDC):
            for dc in range(NDC):
                mm = rec("t", (lambda mc, dc: lambda t: t.matmul(
                    bank(4, 1)[:, mc:mc + 1],
                    wq_sb[:, dc * 1024 + mc * 128: dc * 1024 + (mc + 1) * 128],
                    b1c_sb[:, dc:dc + 1],
                    start=(dc == 0), stop=(dc == NDC - 1), skip_group_check=True))(mc, dc),
                    waits=[w_folded] if (mc == 0 and dc == 0) else ())
        bq1 = rec("v", lambda v: v.tensor_tensor(
            bqe_sb[:, 0:NDC], bank(4, 1)[:, 0:NDC], bqc_sb[:, 0:NDC], ALU.add),
            waits=[P.upto(mm)])
        bqe_done = P.upto(bq1)

        _maybe_stop("p0", bqe_done)

        last_xt_use = {}
        last_xnsrc_use = {}
        last_ptr_use = None
        last_kvuse = {}
        last_pk = None
        last_pv = None

        for t_i in range(NT_ALL):
            sl = t_i % 2
            own = t_i < NT_OWN
            wts = [cols_done]
            if t_i >= 2:
                wts.append(P.upto(last_xt_use[sl]))
            x0ld = rec("y", (lambda t_i, sl: lambda y: y.dma_start(
                xt_sb[sl][:], xb[t_i * 128:(t_i + 1) * 128, :]))(t_i, sl), waits=wts,
                dma=("xb0" if sl == 0 else "xb1"))
            st = st_sb[sl]; mv = mv_sb[sl]; r1 = r1_sb[sl]; rs = rs_sb[sl]
            rec("v", (lambda sl, st: lambda v: v.bn_stats(st[:, 0:6], xt_sb[sl][:, 0:512]))(sl, st),
                waits=[P.upto(x0ld)])
            o1 = rec("v", (lambda sl, st: lambda v: v.bn_stats(st[:, 6:12], xt_sb[sl][:, 512:1024]))(sl, st))
            o2 = rec("v", (lambda st, mv: lambda v: v.bn_aggr(mv[:, 0:2], st[:]))(st, mv),
                     waits=[P.upto(o1)])
            o3 = rec("v", (lambda mv, r1: lambda v: v.tensor_scalar(
                r1[:, 0:1], mv[:, 1:2], LN1EPS, None, ALU.add))(mv, r1), waits=[P.upto(o2)])
            o4 = rec("v", (lambda r1: lambda v: v.reciprocal(r1[:, 0:1], r1[:, 0:1]))(r1),
                     waits=[P.upto(o3)])
            sq = rec("s", (lambda r1, rs: lambda s: s.activation(rs[:, 0:1], r1[:, 0:1], AF.Sqrt))(r1, rs),
                     waits=[P.upto(o4)])
            wts = [P.upto(sq)]
            if t_i >= 2:
                wts.append(P.upto(last_xnsrc_use[sl]))
            ap_op = rec("v", (lambda sl, mv, rs: lambda v: v.tensor_scalar(
                xc_sb[sl][:], xt_sb[sl][:], mv[:, 0:1], rs[:, 0:1], ALU.subtract, ALU.mult))(sl, mv, rs),
                waits=wts)
            last_xt_use[sl] = ap_op
            wts = [P.upto(ap_op)]
            if last_ptr_use is not None:
                wts.append(P.upto(last_ptr_use))
            tr = None
            for k in range(NDC):
                tr = rec("t", (lambda sl, k: lambda t: t.transpose(
                    ptr_bf[:, k * 128:(k + 1) * 128], xc_sb[sl][:, k * 128:(k + 1) * 128],
                    identb_sb[:]))(sl, k),
                    waits=(wts + [P.upto(ld_ident)]) if k == 0 else ())
            if own:
                dstap = xnT_sb.ap().rearrange("p (c f) -> p c f", c=NDC)[:, :, t_i * 128:(t_i + 1) * 128]
                ev = rec("s", (lambda dstap: lambda s: s.activation(
                    dstap, ptr_bf.rearrange("p (c f) -> p c f", c=NDC), AF.Copy))(dstap),
                    waits=[P.upto(tr)])
            else:
                ev = rec("s", (lambda sl: lambda s: s.activation(
                    xc_sb[sl][:], ptr_bf[:], AF.Copy))(sl), waits=[P.upto(tr)])
            last_ptr_use = ev

            def mk_lhs(dc, own=own, t_i=t_i, sl=sl):
                if own:
                    return xnT_sb[:, dc * T_OWN + t_i * 128: dc * T_OWN + (t_i + 1) * 128]
                return xc_sb[sl][:, dc * 128:(dc + 1) * 128]

            wts_k = [P.upto(ev), w_folded]
            if t_i == 0:
                wts_k += [kvbias_done, bqe_done]
            if last_pk is not None:
                wts_k.append(P.upto(last_pk))
            mmk = None
            for dc in range(NDC):
                lhs_ap = mk_lhs(dc)
                for half in range(2):
                    mmk = rec("t", (lambda lhs_ap, dc, half: lambda t: t.matmul(
                        bank(half, 1), lhs_ap,
                        wkv_sb[:, dc * 2048 + half * 512: dc * 2048 + (half + 1) * 512],
                        start=(dc == 0), stop=(dc == NDC - 1), skip_group_check=True))(lhs_ap, dc, half),
                        waits=(wts_k if (dc == 0 and half == 0) else ()))
            wts_v = []
            if last_pv is not None:
                wts_v.append(P.upto(last_pv))
            mmv = None
            for dc in range(NDC):
                lhs_ap = mk_lhs(dc)
                for half in range(2):
                    mmv = rec("t", (lambda lhs_ap, dc, half: lambda t: t.matmul(
                        bank(2 + half, 1), lhs_ap,
                        wkv_sb[:, dc * 2048 + 1024 + half * 512: dc * 2048 + 1024 + (half + 1) * 512],
                        start=(dc == 0), stop=(dc == NDC - 1), skip_group_check=True))(lhs_ap, dc, half),
                        waits=(wts_v if (dc == 0 and half == 0) else ()))
            wts = [P.upto(mmk)]
            if t_i >= 2:
                wts.append(P.upto(last_kvuse[sl]))
            pk1 = rec("v", (lambda sl: lambda v: v.scalar_tensor_tensor(
                phik_sb[sl][:], bank(0, 2), KEPS, kvbias_b[:, 0:1024], ALU.add, ALU.add))(sl),
                waits=wts)
            vv = rec("v", (lambda sl: lambda v: v.tensor_tensor(
                vtok_sb[sl][:], bank(2, 2), kvbias_b[:, 1024:2048], ALU.add))(sl),
                waits=[P.upto(mmv)])
            pk2 = rec("v", (lambda sl: lambda v: v.tensor_scalar(
                phik_sb[sl][:], phik_sb[sl][:], KEPS, None, ALU.max))(sl),
                waits=[P.upto(pk1)])
            last_pk = pk2
            last_pv = vv
            kvmm = None
            wts = [P.upto(pk2), P.upto(vv)]
            for h in range(H):
                kvmm = rec("t", (lambda sl, h, t_i: lambda t: t.matmul(
                    kv_ps[:, h * 64:(h + 1) * 64], phik_sb[sl][:, h * 64:(h + 1) * 64],
                    vtok_sb[sl][:, h * 64:(h + 1) * 64],
                    start=(t_i == 0 and h in (0, 8)), stop=(t_i == NT_ALL - 1),
                    skip_group_check=True))(sl, h, t_i),
                    waits=(wts if h == 0 else ()))
                kvmm = rec("t", (lambda sl, h, t_i: lambda t: t.matmul(
                    ks_ps[:, h:h + 1], phik_sb[sl][:, h * 64:(h + 1) * 64], ones_sb[:, 0:1],
                    start=(t_i == 0 and h == 0), stop=(t_i == NT_ALL - 1),
                    skip_group_check=True))(sl, h, t_i))
            last_kvuse[sl] = kvmm
            last_xnsrc_use[sl] = tr if own else mmv

        kv_done = P.upto_last("t")

        kvp = kv_ps.rearrange("p (g two c) -> p g two c", two=2, c=64)
        ksp = ks_ps.rearrange("p (g two) -> p g two", two=2)
        blk_t = kv_blk.ap().rearrange("p (g c) -> p g c", c=136)
        zz = rec("v", lambda v: v.memset(kv_blk[:], 0.0), waits=[kv_done])
        zz2 = rec("v", lambda v: v.memset(kv_sb[:], 0.0))
        rec("v", lambda v: v.tensor_copy(blk_t[0:64, :, 0:64], kvp[:, :, 0, :]),
            waits=[P.upto(zz)])
        rec("v", lambda v: v.tensor_copy(blk_t[0:64, :, 128:129], ksp[:, :, 0:1]))
        kvr = kv_sb[0:64, :].rearrange("p (g c) -> p g c", c=72)
        rec("v", lambda v: v.tensor_copy(kvr[:, :, 0:64], kvp[:, :, 1, :]),
            waits=[P.upto(zz2)])
        ev_kv = rec("v", lambda v: v.tensor_copy(kvr[:, :, 64:65], ksp[:, :, 1:2]))
        st_done = P.upto(ev_kv)
        d1m = rec("g", lambda g: g.dma_start(
            blk_t[64:128, :, 64:128],
            kv_sb[0:64, :].rearrange("p (g c) -> p g c", c=72)[:, :, 0:64]),
            waits=[st_done], dma="wo")
        d2m = rec("g", lambda g: g.dma_start(
            blk_t[64:128, :, 129:130],
            kv_sb[0:64, :].rearrange("p (g c) -> p g c", c=72)[:, :, 64:65],
            allow_slow_non_contiguous=True),
            waits=[], dma="wo")
        kvsb_done = P.upto(d2m)

        tap_refs = {}
        _maybe_stop("A", kvsb_done)

        if "kvsb" in taps:
            tap_refs["kvsb"] = rec("g", lambda g: g.dma_start(tap_t["kvsb"], kv_blk[:]), waits=[kvsb_done], dma="tap_kvsb")

        qreg_use = {}
        qev = None
        for i in range(NDC * 2):
            mc, th = i // 2, i % 2
            reg = i % 3
            wts = []
            if i == 0:
                wts = [kv_done, kvsb_done, w_folded, bqe_done]
            if reg in qreg_use:
                wts.append(P.upto(qreg_use[reg]))
            mm = None
            for dc in range(NDC):
                for half in range(2):
                    mm = rec("t", (lambda mc, th, dc, half, reg: lambda t: t.matmul(
                        bank(reg * 2 + half, 1),
                        wq_sb[:, dc * 1024 + mc * 128: dc * 1024 + (mc + 1) * 128],
                        xnT_sb[:, dc * T_OWN + th * 1024 + half * 512: dc * T_OWN + th * 1024 + (half + 1) * 512],
                        start=(dc == 0), stop=(dc == NDC - 1), skip_group_check=True))(mc, th, dc, half, reg),
                        waits=(wts if (dc == 0 and half == 0) else ()))
            qev = rec("v", (lambda mc, th, reg: lambda v: v.tensor_scalar(
                phiq_sb[:, mc * T_OWN + th * 1024: mc * T_OWN + (th + 1) * 1024],
                bank(reg * 2, 2), bqe_sb[:, mc:mc + 1], KEPS, ALU.add, ALU.max))(mc, th, reg),
                waits=[P.upto(mm)])
            qreg_use[reg] = qev
        phiq_done = P.upto(qev)

        if "xnT" in taps:
            tap_refs["xnT"] = rec("g", lambda g: g.dma_start(tap_t["xnT"], xnT_sb[:]), waits=[phiq_done], dma="tap_xnT")
        if "phiq" in taps:
            tap_refs["phiq"] = rec("g", lambda g: g.dma_start(tap_t["phiq"], phiq_sb[:]), waits=[phiq_done], dma="tap_phiq")

        _maybe_stop("q", phiq_done)

        num_reg_use = {}
        recip_of = {}
        mm_of = {}
        mul_last = None

        def emit_attn_muls(tc):
            nonlocal mul_last
            par = tc % 2
            evl = None
            for h in range(H):
                w = [P.upto(recip_of[tc]), tap_refs.get("xnT")] if h == 0 else ()
                evl = rec("v", (lambda tc, par, h: lambda v: v.tensor_scalar(
                    attn_sb[:, tc * 1024 + h * 64: tc * 1024 + (h + 1) * 64],
                    bank(par * 2, 2)[:, h * 64:(h + 1) * 64],
                    rd_sb[par][:, h:h + 1], None, ALU.mult))(tc, par, h), waits=w)
            num_reg_use[par] = evl
            mul_last = evl

        for tc in range(NT_OWN):
            par = tc % 2
            wts = [kvsb_done, phiq_done, tap_refs.get("xnT")]
            if par in num_reg_use:
                wts.append(P.upto(num_reg_use[par]))
            mm = None
            for mc in range(8):
                lhs = phiq_sb[:, mc * T_OWN + tc * 128: mc * T_OWN + (tc + 1) * 128]
                rkv = kv_blk[:, mc * 136: mc * 136 + 128]
                rks = kv_blk[:, mc * 136 + 128: mc * 136 + 130]
                mm = rec("t", (lambda lhs, rkv, par, mc: lambda t: t.matmul(
                    bank(par * 2, 2)[:, mc * 128:(mc + 1) * 128], lhs, rkv,
                    start=True, stop=True, skip_group_check=True))(lhs, rkv, par, mc),
                    waits=(wts if mc == 0 else ()))
                mm = rec("t", (lambda lhs, rks, par, mc: lambda t: t.matmul(
                    bank(4 + par, 1)[:, 2 * mc: 2 * mc + 2], lhs, rks,
                    start=True, stop=True, skip_group_check=True))(lhs, rks, par, mc))
            mm_of[tc] = mm
            recip_of[tc] = rec("v", (lambda par: lambda v: v.reciprocal(
                rd_sb[par][:], bank(4 + par, 1)[:, 0:16]))(par),
                waits=[P.upto(mm)])
            if tc >= 1:
                emit_attn_muls(tc - 1)
        emit_attn_muls(NT_OWN - 1)
        attn_done = P.upto(mul_last)

        if "attn" in taps:
            tap_refs["attn"] = rec("g", lambda g: g.dma_start(tap_t["attn"], attn_sb[:]), waits=[attn_done], dma="tap_attn")

        _maybe_stop("num", attn_done)

        ld_wo = rec("g", lambda g: g.dma_start(
            wo_sb.ap().rearrange("p (c f) -> p c f", c=NDC),
            wo_d.rearrange("(c p) f -> p c f", p=128)), waits=[kv_done], dma="wol")
        ld_bo = rec("y", lambda y: y.dma_start(
            bo_b[:], bo_d[None, :].broadcast_to([128, 1024])), waits=[kv_done], dma="bo")
        ld_b2b = rec("y", lambda y: y.dma_start(
            b2_b[:], b2_d[None, :].broadcast_to([128, 1024])), waits=[P.upto_last("v")], dma="b2b")

        x0_use = {}
        o_reg_use = {}
        trev = None
        trev_of = {}
        e2 = None
        for tc in range(NT_OWN):
            sl3 = tc % 3
            par = tc % 2
            wts = [attn_done]
            if sl3 in x0_use:
                wts.append(P.upto(x0_use[sl3]))
            x0ld = rec("y", (lambda tc, sl3: lambda y: y.dma_start(
                x0s_sb[sl3][:], xb[tc * 128:(tc + 1) * 128, :]))(tc, sl3), waits=wts,
                dma=("x00", "x01", "x02")[sl3])
            wts = [attn_done]
            if (tc % 2) in trev_of:
                wts.append(P.upto(trev_of[tc % 2]))
            pb_ = ptr_of[tc % 2]
            tr = None
            for k in range(NDC):
                tr = rec("t", (lambda tc, k, pb_: lambda t: t.transpose(
                    pb_[:, k * 128:(k + 1) * 128],
                    attn_sb[:, tc * 1024 + k * 128: tc * 1024 + (k + 1) * 128],
                    identb_sb[:]))(tc, k, pb_), waits=(wts if k == 0 else ()))
            ev = rec("s", (lambda tc, pb_: lambda s: s.activation(
                attnT_sb.ap().rearrange("p (c f) -> p c f", c=NDC)[:, :, tc * 128:(tc + 1) * 128],
                pb_.rearrange("p (c f) -> p c f", c=NDC), AF.Copy))(tc, pb_),
                waits=[P.upto(tr), tap_refs.get("phiq")])
            trev_of[tc % 2] = ev
            trev = ev
            wts = [P.upto(ev), P.upto(ld_wo)]
            if par in o_reg_use:
                wts.append(P.upto(o_reg_use[par]))
            mm = None
            for fc in range(NDC):
                for half in range(2):
                    mm = rec("t", (lambda tc, fc, half, par: lambda t: t.matmul(
                        bank(par * 2 + half, 1),
                        attnT_sb[:, fc * T_OWN + tc * 128: fc * T_OWN + (tc + 1) * 128],
                        wo_sb[:, fc * 1024 + half * 512: fc * 1024 + (half + 1) * 512],
                        start=(fc == 0), stop=(fc == NDC - 1), skip_group_check=True))(tc, fc, half, par),
                        waits=(wts if (fc == 0 and half == 0) else ()))
            e1 = rec("v", (lambda tc, par, sl3: lambda v: v.tensor_tensor(
                xres_sb[:, tc * 1024:(tc + 1) * 1024], bank(par * 2, 2), x0s_sb[sl3][:], ALU.add))(tc, par, sl3),
                waits=[P.upto(mm), P.upto(x0ld), P.upto(ld_bo)])
            e2 = rec("v", (lambda tc: lambda v: v.tensor_tensor(
                xres_sb[:, tc * 1024:(tc + 1) * 1024], xres_sb[:, tc * 1024:(tc + 1) * 1024],
                bo_b[:], ALU.add))(tc), waits=[P.upto(e1)])
            o_reg_use[par] = e2
            x0_use[sl3] = e1
        xres_done = P.upto(e2)

        if "xres" in taps:
            tap_refs["xres"] = rec("g", lambda g: g.dma_start(tap_t["xres"], xres_sb[:]), waits=[xres_done], dma="tap_xres")

        _maybe_stop("o", xres_done)

        y2xc_use = {}
        h1_done_of = {}
        w2ld_of = {}
        h1reg_use = {}
        yreg_use = {}
        w1buf_use = {}
        w2buf_use = {}
        h1buf_use = {}
        yacc_of = {}
        pinslot_use = {}
        poolo_use = {}
        x0b_use = {}
        ln2_tr_prev = None
        ln2_tr_of = {}
        x1_last = None

        for th in range(1):
            ln2_ev = None
            for ti in range(16):
                tc = ti
                sl = ti % 2
                st = st_sb[sl]; mv = mv_sb[sl]; r1 = r1_sb[sl]; rs = rs_sb[sl]
                xr = xres_sb[:, tc * 1024:(tc + 1) * 1024]
                rec("v", (lambda st, xr: lambda v: v.bn_stats(st[:, 0:6], xr[:, 0:512]))(st, xr),
                    waits=[xres_done, tap_refs.get("attn")])
                o1 = rec("v", (lambda st, xr: lambda v: v.bn_stats(st[:, 6:12], xr[:, 512:1024]))(st, xr))
                o2 = rec("v", (lambda st, mv: lambda v: v.bn_aggr(mv[:, 0:2], st[:]))(st, mv),
                         waits=[P.upto(o1)])
                o3 = rec("v", (lambda mv, r1: lambda v: v.tensor_scalar(
                    r1[:, 0:1], mv[:, 1:2], LN1EPS, None, ALU.add))(mv, r1), waits=[P.upto(o2)])
                o4 = rec("v", (lambda r1: lambda v: v.reciprocal(r1[:, 0:1], r1[:, 0:1]))(r1),
                         waits=[P.upto(o3)])
                sq = rec("s", (lambda r1, rs: lambda s: s.activation(rs[:, 0:1], r1[:, 0:1], AF.Sqrt))(r1, rs),
                         waits=[P.upto(o4)])
                wts = [P.upto(sq)]
                if sl in y2xc_use:
                    wts.append(P.upto(y2xc_use[sl]))
                ap_op = rec("v", (lambda sl, mv, rs, xr: lambda v: v.tensor_scalar(
                    xc_sb[sl][:], xr, mv[:, 0:1], rs[:, 0:1], ALU.subtract, ALU.mult))(sl, mv, rs, xr),
                    waits=wts)
                wts = [P.upto(ap_op)]
                if (ti % 2) in ln2_tr_of:
                    wts.append(P.upto(ln2_tr_of[ti % 2]))
                pb_ = ptr_of[ti % 2]
                tr = None
                for k in range(NDC):
                    tr = rec("t", (lambda sl, k, pb_: lambda t: t.transpose(
                        pb_[:, k * 128:(k + 1) * 128], xc_sb[sl][:, k * 128:(k + 1) * 128],
                        identb_sb[:]))(sl, k, pb_), waits=(wts if k == 0 else ()))
                y2xc_use[sl] = tr
                evl = None
                for k in range(NDC):
                    evl = rec("s", (lambda ti, k, pb_: lambda s: s.activation(
                        y2T_sb[:, k * 2048 + ti * 128: k * 2048 + (ti + 1) * 128],
                        pb_[:, k * 128:(k + 1) * 128], AF.Identity,
                        bias=b2c_sb[:, k:k + 1], scale=g2c_sb[:, k:k + 1]))(ti, k, pb_),
                        waits=[P.upto(tr)] if k == 0 else ())
                ln2_tr_of[ti % 2] = evl
                ln2_tr_prev = evl
            y2_done = P.upto(ln2_tr_prev)

            if th == 0 and "y2T" in taps:
                rec("g", lambda g: g.dma_start(tap_t["y2T"], y2T_sb[:]), waits=[y2_done], dma="tap_y2T")

            for p in range(16):
                pp = p % 2
                wts = [] if (th or p) else [xres_done]
                if pp in w1buf_use:
                    wts.append(P.upto(w1buf_use[pp]))
                ld1 = rec("g", (lambda p, pp: lambda g: g.dma_start(
                    w1p_sb[pp].ap().rearrange("q (c f) -> q c f", c=NDC),
                    w1_d.rearrange("(c q) f -> q c f", q=128)[:, :, p * 256:(p + 1) * 256]))(p, pp),
                    waits=wts, dma=("w1a" if pp == 0 else "w1b"))
                wts = [] if (th or p) else [xres_done]
                if pp in w2buf_use:
                    wts.append(P.upto(w2buf_use[pp]))
                ld2 = rec("g", (lambda p, pp: lambda g: g.dma_start(
                    w2p_sb[pp].ap().rearrange("q (c f) -> q c f", c=2),
                    w2_d.rearrange("(c q) f -> q c f", q=128)[:, 2 * p:2 * p + 2, :]))(p, pp),
                    waits=wts, dma=("w2a" if pp == 0 else "w2b"))
                hev = None
                for fc in range(2):
                    for tb in range(2):
                        reg = (2 * fc + tb) % 2
                        wts = [P.upto(ld1), y2_done]
                        if reg in h1reg_use:
                            wts.append(P.upto(h1reg_use[reg]))
                        if pp in h1buf_use:
                            wts.append(P.upto(h1buf_use[pp]))
                        mm = None
                        for dc in range(NDC):
                            for half in range(2):
                                mm = rec("t", (lambda p, pp, fc, dc, half, reg, tb: lambda t: t.matmul(
                                    bank(reg * 2 + half, 1),
                                    w1p_sb[pp][:, dc * 256 + fc * 128: dc * 256 + (fc + 1) * 128],
                                    y2T_sb[:, dc * 2048 + tb * 1024 + half * 512: dc * 2048 + tb * 1024 + (half + 1) * 512],
                                    start=(dc == 0), stop=(dc == NDC - 1), skip_group_check=True))(p, pp, fc, dc, half, reg, tb),
                                    waits=(wts if (dc == 0 and half == 0) else ()))
                        hev = rec("s", (lambda p, pp, fc, tb, reg: lambda s: s.activation(
                            h1T_sb[pp][:, fc * 2048 + tb * 1024: fc * 2048 + (tb + 1) * 1024],
                            bank(reg * 2, 2),
                            AF.Gelu_apprx_tanh, bias=b1mlp_sb[:, p * 2 + fc: p * 2 + fc + 1]))(p, pp, fc, tb, reg),
                            waits=[P.upto(mm)])
                        h1reg_use[reg] = hev
                w1buf_use[pp] = mm
                h1_done_of[pp] = P.upto(hev)
                w2ld_of[pp] = ld2
                if pp == 0:
                    continue
                ya = None
                for ti in range(16):
                    tc = ti
                    yreg = 2 + (ti % 2)
                    wts = [h1_done_of[0], h1_done_of[1], P.upto(w2ld_of[0]), P.upto(w2ld_of[1])]
                    if yreg in yreg_use:
                        wts.append(P.upto(yreg_use[yreg]))
                    mm = None
                    for qq in range(2):
                        for fc in range(2):
                            for half in range(2):
                                mm = rec("t", (lambda qq, ti, fc, half, yreg: lambda t: t.matmul(
                                    bank(yreg * 2 + half, 1),
                                    h1T_sb[qq][:, fc * 2048 + ti * 128: fc * 2048 + (ti + 1) * 128],
                                    w2p_sb[qq][:, fc * 1024 + half * 512: fc * 1024 + (half + 1) * 512],
                                    start=(qq == 0 and fc == 0), stop=(qq == 1 and fc == 1),
                                    skip_group_check=True))(qq, ti, fc, half, yreg),
                                    waits=(wts if (qq == 0 and fc == 0 and half == 0) else ()))
                    wts = [P.upto(mm), tap_refs.get("xres")]
                    if tc in yacc_of:
                        wts.append(P.upto(yacc_of[tc]))
                    ya = rec("v", (lambda tc, yreg: lambda v: v.scalar_tensor_tensor(
                        xres_sb[:, tc * 1024:(tc + 1) * 1024], bank(yreg * 2, 2), 1.0,
                        xres_sb[:, tc * 1024:(tc + 1) * 1024], ALU.mult, ALU.add))(tc, yreg),
                        waits=wts)
                    yacc_of[tc] = ya
                    yreg_use[yreg] = ya
                for _b in (0, 1):
                    w2buf_use[_b] = mm
                    h1buf_use[_b] = mm

            for ti in range(16):
                tc = ti
                sl3 = tc % 3
                slp = ti % 2
                sl4 = ti % 4
                wts = [P.upto(yacc_of[tc])]
                if sl3 in x0b_use:
                    wts.append(P.upto(x0b_use[sl3]))
                x0ld = rec("y", (lambda tc, sl3: lambda y: y.dma_start(
                    x0s_sb[sl3][:], xb[tc * 128:(tc + 1) * 128, :]))(tc, sl3), waits=wts,
                    dma=("x00", "x01", "x02")[sl3])
                wts = [P.upto(x0ld), P.upto(ld_b2b)]
                if slp in pinslot_use:
                    wts.append(P.upto(pinslot_use[slp]))
                a1 = rec("v", (lambda tc, sl3, slp: lambda v: v.tensor_tensor(
                    upool_sb[slp][:], xres_sb[:, tc * 1024:(tc + 1) * 1024], x0s_sb[sl3][:],
                    ALU.add))(tc, sl3, slp), waits=wts)
                x0b_use[sl3] = a1
                a2 = rec("v", (lambda slp, sl4: lambda v: v.scalar_tensor_tensor(
                    upool_sb[slp][:], upool_sb[slp][:], 1.0, b2_b[:], ALU.mult, ALU.add,
                    accum_out=sums_sb[sl4][:, 0:1]))(slp, sl4), waits=[P.upto(a1)])
                sqr = rec("s", (lambda slp, sl4: lambda s: s.activation(
                    junk_sb[:], upool_sb[slp][:], AF.Square, accum_out=ssq_sb[sl4][:, 0:1]))(slp, sl4),
                    waits=[P.upto(a2)])
                rec("v", (lambda sl4: lambda v: v.tensor_scalar(
                    mu_sb[sl4][:, 0:1], sums_sb[sl4][:, 0:1], 1.0 / D, None, ALU.mult))(sl4),
                    waits=[P.upto(sqr)])
                tpo = rec("v", (lambda sl4: lambda v: v.tensor_scalar(
                    tp1_sb[sl4][:, 0:1], ssq_sb[sl4][:, 0:1], 1.0 / D, None, ALU.mult))(sl4))
                c1 = rec("v", (lambda sl4: lambda v: v.tensor_tensor(
                    r1_sb[sl4][:, 0:1], mu_sb[sl4][:, 0:1], mu_sb[sl4][:, 0:1], ALU.mult))(sl4),
                    waits=[P.upto(tpo)])
                c2 = rec("v", (lambda sl4: lambda v: v.tensor_tensor(
                    r1_sb[sl4][:, 0:1], tp1_sb[sl4][:, 0:1], r1_sb[sl4][:, 0:1], ALU.subtract))(sl4),
                    waits=[P.upto(c1)])
                c3 = rec("v", (lambda sl4: lambda v: v.tensor_scalar(
                    r1_sb[sl4][:, 0:1], r1_sb[sl4][:, 0:1], PLNEPS, None, ALU.add))(sl4),
                    waits=[P.upto(c2)])
                c4 = rec("v", (lambda sl4: lambda v: v.reciprocal(r1_sb[sl4][:, 0:1], r1_sb[sl4][:, 0:1]))(sl4),
                    waits=[P.upto(c3)])
                sq2 = rec("s", (lambda sl4: lambda s: s.activation(
                    rs_sb[sl4][:, 0:1], r1_sb[sl4][:, 0:1], AF.Sqrt))(sl4), waits=[P.upto(c4)])
                xa = rec("v", (lambda tc, slp, sl4: lambda v: v.tensor_scalar(
                    xres_sb[:, tc * 1024:(tc + 1) * 1024], upool_sb[slp][:],
                    mu_sb[sl4][:, 0:1], rs_sb[sl4][:, 0:1], ALU.subtract, ALU.mult))(tc, slp, sl4),
                    waits=[P.upto(sq2)])
                pinslot_use[slp] = xa
                x1_last = xa
        x1_done = P.upto(x1_last)

        if "x1" in taps:
            tap_refs["x1"] = rec("g", lambda g: g.dma_start(tap_t["x1"], xres_sb[:]), waits=[x1_done], dma="tap_x1")

        _maybe_stop("mlp", x1_done)

        ld_mb1 = rec("y", lambda y: y.dma_start(mb1raw_sb[:], mb1_d), waits=[x1_done], dma="mb1")

        x1c_use = {}
        x1t_tr_of = {}
        trp = None
        for tc in range(NT_OWN):
            sl = tc % 2
            wts = [x1_done]
            if sl in x1c_use:
                wts.append(P.upto(x1c_use[sl]))
            cp = rec("v", (lambda tc, sl: lambda v: v.tensor_copy(
                xc_sb[sl][:], xres_sb[:, tc * 1024:(tc + 1) * 1024]))(tc, sl), waits=wts)
            wts = [P.upto(cp)]
            if (tc % 2) in x1t_tr_of:
                wts.append(P.upto(x1t_tr_of[tc % 2]))
            pb_ = ptr_of[tc % 2]
            tr = None
            for k in range(NDC):
                tr = rec("t", (lambda sl, k, pb_: lambda t: t.transpose(
                    pb_[:, k * 128:(k + 1) * 128], xc_sb[sl][:, k * 128:(k + 1) * 128],
                    identb_sb[:]))(sl, k, pb_), waits=(wts if k == 0 else ()))
            x1c_use[sl] = tr
            ev = rec("s", (lambda tc, pb_: lambda s: s.activation(
                x1T_sb.ap().rearrange("p (c f) -> p c f", c=NDC)[:, :, tc * 128:(tc + 1) * 128],
                pb_.rearrange("p (c f) -> p c f", c=NDC), AF.Copy))(tc, pb_),
                waits=[P.upto(tr)])
            x1t_tr_of[tc % 2] = ev
            trp = ev
        x1T_done = P.upto(trp)

        gw_ev = None
        for k in range(NDC):
            gw_ev_tr = rec("t", (lambda k: lambda t: t.transpose(
                ptr_bf[:, k * 14:(k + 1) * 14], gwraw_sb[0:E_EXP, k * 128:(k + 1) * 128],
                identb_sb[0:E_EXP, 0:E_EXP]))(k),
                waits=[x1T_done, P.upto(ld_gw)] if k == 0 else ())
        gw_ev = rec("v", lambda v: v.tensor_copy(
            gwt_sb[:, 0:NDC * E_EXP], ptr_bf[:, 0:NDC * E_EXP]), waits=[P.upto(gw_ev_tr)])
        gwt_done = P.upto(gw_ev)
        b1m_ev = None
        mmtr = None
        for fc in range(16):
            mmtr = rec("t", (lambda fc: lambda t: t.transpose(
                ps[0:128, 6 * 512 + fc * 14: 6 * 512 + (fc + 1) * 14],
                mb1raw_sb[0:E_EXP, fc * 128:(fc + 1) * 128], identf_sb[0:E_EXP, 0:E_EXP]))(fc),
                waits=[P.upto(ld_mb1), gwt_done] if fc == 0 else ())
        b1m_ev = rec("v", lambda v: v.tensor_copy(
            b1m_sb[:, 0:224], ps[0:128, 6 * 512: 6 * 512 + 224]), waits=[P.upto(mmtr)])
        b1m_done = P.upto(b1m_ev)

        greg_use = {}
        mk_last = None
        for tc in range(NT_OWN):
            par = tc % 2
            wts = [x1T_done, gwt_done]
            if par in greg_use:
                wts.append(P.upto(greg_use[par]))
            mm = None
            for dc in range(NDC):
                mm = rec("t", (lambda tc, dc, par: lambda t: t.matmul(
                    bank(4 + par, 1)[:, 0:E_EXP],
                    x1T_sb[:, dc * T_OWN + tc * 128: dc * T_OWN + (tc + 1) * 128],
                    gwt_sb[:, dc * E_EXP:(dc + 1) * E_EXP],
                    start=(dc == 0), stop=(dc == NDC - 1), skip_group_check=True))(tc, dc, par),
                    waits=(wts if dc == 0 else ()))
            lg = bank(4 + par, 1)[:, 0:E_EXP]
            d1 = rec("v", (lambda lg: lambda v: v.tensor_reduce(m1_sb[:, 0:1], lg, AX.X, ALU.max))(lg),
                waits=[P.upto(mm)])
            rec("v", (lambda lg: lambda v: v.tensor_copy(lm_sb[:], lg))(lg))
            d2 = rec("v", (lambda lg: lambda v: v.tensor_scalar(
                eq1_sb[:], lg, m1_sb[:, 0:1], None, ALU.is_equal))(lg),
                waits=[P.upto(d1)])
            d3 = rec("v", lambda v: v.scalar_tensor_tensor(
                lm_sb[:], eq1_sb[:], -1e30, lm_sb[:], ALU.mult, ALU.add), waits=[P.upto(d2)])
            d4 = rec("v", lambda v: v.tensor_reduce(m2_sb[:, 0:1], lm_sb[:], AX.X, ALU.max),
                waits=[P.upto(d3)])
            d5 = rec("v", lambda v: v.tensor_tensor(w1w_sb[:, 0:1], m2_sb[:, 0:1], m1_sb[:, 0:1], ALU.subtract),
                waits=[P.upto(d4)])
            d6 = rec("v", (lambda lg: lambda v: v.tensor_scalar(
                eq2_sb[:], lm_sb[:], m2_sb[:, 0:1], None, ALU.is_equal))(lg))
            sg = rec("s", lambda s: s.activation(w0_sb[:, 0:1], w1w_sb[:, 0:1], AF.Sigmoid),
                     waits=[P.upto(d6)])
            d8 = rec("v", lambda v: v.tensor_scalar(w1w_sb[:, 0:1], w0_sb[:, 0:1], -1.0, 1.0, ALU.mult, ALU.add),
                waits=[P.upto(sg)])
            rec("v", lambda v: v.tensor_copy(lm_sb[:], eq1_sb[:]))
            d7 = rec("v", (lambda tc: lambda v: v.tensor_scalar(
                mask_sb[:, tc * E_EXP:(tc + 1) * E_EXP], eq1_sb[:], w1w_sb[:, 0:1], None,
                ALU.mult))(tc), waits=[P.upto(d8)])
            mk = rec("v", (lambda tc: lambda v: v.scalar_tensor_tensor(
                mask_sb[:, tc * E_EXP:(tc + 1) * E_EXP], eq2_sb[:], w0_sb[:, 0:1],
                mask_sb[:, tc * E_EXP:(tc + 1) * E_EXP], ALU.mult, ALU.add))(tc),
                waits=[P.upto(d7)])
            mtr = rec("t", (lambda tc: lambda t: t.transpose(
                ps[0:E_EXP, (6 + tc % 2) * 512: (6 + tc % 2) * 512 + 128],
                mask_sb[:, tc * E_EXP:(tc + 1) * E_EXP], identf_sb[:]))(tc),
                waits=[P.upto(mk), b1m_done if tc == 0 else None])
            mk2 = rec("v", (lambda tc: lambda v: v.tensor_copy(
                maskT_sb[0:E_EXP, tc * 128:(tc + 1) * 128],
                ps[0:E_EXP, (6 + tc % 2) * 512: (6 + tc % 2) * 512 + 128]))(tc),
                waits=[P.upto(mtr)])
            greg_use[par] = mk2
            mk_last = mk2
        mask_done = P.upto(mk_last)

        if "mask" in taps:
            tap_refs["mask"] = rec("g", lambda g: g.dma_start(tap_t["mask"], mask_sb[:]), waits=[mask_done], dma="tap_mask")

        _maybe_stop("gate", mask_done)

        accw = {}
        outreg_use = {}
        hhreg_use = {}
        w1buf2_use = {}
        w2buf2_of = {}
        hhslot_use = {}
        pc_use = None
        po_use = None
        store_last = None

        for g_i in range(2):
            for ti in range(8):
                tc = g_i * 8 + ti
                yreg = 2 + (ti % 2)
                wts = [mask_done, P.upto(ld_b2m)]
                if yreg in outreg_use:
                    wts.append(P.upto(outreg_use[yreg]))
                mm = None
                for half in range(2):
                    mm = rec("t", (lambda tc, half, yreg: lambda t: t.matmul(
                        bank(yreg * 2 + half, 1),
                        maskT_sb[0:E_EXP, tc * 128:(tc + 1) * 128],
                        b2m_sb[0:E_EXP, half * 512:(half + 1) * 512],
                        start=True, stop=True, skip_group_check=True))(tc, half, yreg),
                        waits=(wts if half == 0 else ()))
                wts = [P.upto(mm)]
                if ti in accw:
                    wts.append(P.upto(accw[ti]))
                av = rec("v", (lambda ti, yreg: lambda v: v.tensor_copy(
                    acc_sb[:, ti * 1024:(ti + 1) * 1024], bank(yreg * 2, 2)))(ti, yreg),
                    waits=wts)
                accw[ti] = av
                outreg_use[yreg] = av

            for e_i in range(E_EXP):
                for q4 in range(4):
                    hh_done_of = {}
                    ld2_of = {}
                    for hs in range(2):
                        q8 = q4 * 2 + hs
                        u = hs
                        wts = [x1T_done]
                        if u in w1buf2_use:
                            wts.append(P.upto(w1buf2_use[u]))
                        ld1 = rec("g", (lambda e_i, q8, u: lambda g: g.dma_start(
                            mw1_sb[u].ap().rearrange("p (c f) -> p c f", c=NDC),
                            mw1_d[e_i].rearrange("(c p) f -> p c f", p=128)[:, :, q8 * 256:(q8 + 1) * 256]))(e_i, q8, u),
                            waits=wts, dma=("m1a" if u == 0 else "m1b"))
                        pr = (e_i * 4 + q4) % 2
                        wts = []
                        if hs == 0:
                            wts = [x1T_done] if pr not in w2buf2_of else [P.upto(w2buf2_of[pr])]
                        ld2_of[hs] = rec("g", (lambda e_i, q8, hs, pr: lambda g: g.dma_start(
                            mw2_sb[pr].ap().rearrange("p (hs c f) -> p hs c f", hs=2, c=2)[:, hs],
                            mw2_d[e_i].rearrange("(c p) f -> p c f", p=128)[:, 2 * q8:2 * q8 + 2, :]))(e_i, q8, hs, pr),
                            waits=wts, dma=("m2" if pr == 0 else "m2b"))
                        hev = None
                        mm = None
                        for fc in range(2):
                            reg = fc
                            wts = [P.upto(ld1), b1m_done]
                            if reg in hhreg_use:
                                wts.append(P.upto(hhreg_use[reg]))
                            if hs in hhslot_use:
                                wts.append(P.upto(hhslot_use[hs]))
                            for dc in range(NDC):
                                for half in range(2):
                                    mm = rec("t", (lambda g_i, u, fc, dc, half, reg: lambda t: t.matmul(
                                        bank(reg * 2 + half, 1),
                                        mw1_sb[u][:, dc * 256 + fc * 128: dc * 256 + (fc + 1) * 128],
                                        x1T_sb[:, dc * T_OWN + g_i * 1024 + half * 512: dc * T_OWN + g_i * 1024 + (half + 1) * 512],
                                        start=(dc == 0), stop=(dc == NDC - 1), skip_group_check=True))(g_i, u, fc, dc, half, reg),
                                        waits=(wts if (dc == 0 and half == 0) else ()))
                            hev = rec("s", (lambda e_i, q8, fc, hs, reg: lambda s: s.activation(
                                hhT_sb[:, (2 * hs + fc) * 1024:(2 * hs + fc + 1) * 1024],
                                bank(reg * 2, 2), AF.Gelu,
                                bias=b1m_sb[:, (q8 * 2 + fc) * E_EXP + e_i: (q8 * 2 + fc) * E_EXP + e_i + 1]))(e_i, q8, fc, hs, reg),
                                waits=[P.upto(mm)])
                            hhreg_use[reg] = hev
                        w1buf2_use[u] = mm
                        hh_done_of[hs] = P.upto(hev)
                    mm = None
                    av = None
                    for ti in range(8):
                        tc = g_i * 8 + ti
                        yreg = 2 + (ti % 2)
                        wts = [hh_done_of[0], hh_done_of[1], P.upto(ld2_of[1])]
                        if yreg in outreg_use:
                            wts.append(P.upto(outreg_use[yreg]))
                        for hs in range(2):
                            for fc in range(2):
                                for half in range(2):
                                    mm = rec("t", (lambda hs, ti, fc, half, yreg, pr: lambda t: t.matmul(
                                        bank(yreg * 2 + half, 1),
                                        hhT_sb[:, (2 * hs + fc) * 1024 + ti * 128:(2 * hs + fc) * 1024 + (ti + 1) * 128],
                                        mw2_sb[pr].ap().rearrange("p (hs c f) -> p hs c f", hs=2, c=2)[:, hs, fc, half * 512:(half + 1) * 512],
                                        start=(hs == 0 and fc == 0), stop=(hs == 1 and fc == 1),
                                        skip_group_check=True))(hs, ti, fc, half, yreg, pr),
                                        waits=(wts if (hs == 0 and fc == 0 and half == 0) else ()))
                        wts = [P.upto(mm), P.upto(accw[ti])]
                        av = rec("v", (lambda e_i, tc, ti, yreg: lambda v: v.scalar_tensor_tensor(
                            acc_sb[:, ti * 1024:(ti + 1) * 1024], bank(yreg * 2, 2),
                            mask_sb[:, tc * E_EXP + e_i: tc * E_EXP + e_i + 1],
                            acc_sb[:, ti * 1024:(ti + 1) * 1024], ALU.mult, ALU.add))(e_i, tc, ti, yreg),
                            waits=wts)
                        accw[ti] = av
                        outreg_use[yreg] = av
                    w2buf2_of[pr] = mm
                    hhslot_use[0] = mm
                    hhslot_use[1] = mm

            if g_i == 0 and "acc0" in taps:
                tap_refs["acc0"] = rec("g", lambda g: g.dma_start(tap_t["acc0"], acc_sb[:]),
                    waits=[P.upto(accw[7])], dma="tap_acc0")

            for ti in range(8):
                tc = g_i * 8 + ti
                sl4 = ti % 4
                x1t = xres_sb[:, tc * 1024:(tc + 1) * 1024]
                wts = [P.upto(accw[ti])]
                if pc_use is not None:
                    wts.append(P.upto(pc_use))
                wts.append(tap_refs.get("acc0"))
                f1 = rec("v", (lambda ti, x1t: lambda v: v.tensor_tensor(
                    pool_c[:], x1t, acc_sb[:, ti * 1024:(ti + 1) * 1024], ALU.add))(ti, x1t),
                    waits=wts)
                f2 = rec("v", lambda v: v.tensor_scalar(
                    pool_c[:], pool_c[:], 100.0, -100.0, ALU.min, ALU.max),
                    waits=[P.upto(f1)])
                sa = rec("v", (lambda ti, x1t, sl4: lambda v: v.scalar_tensor_tensor(
                    acc_sb[:, ti * 1024:(ti + 1) * 1024], pool_c[:], 1.0, x1t, ALU.mult, ALU.add,
                    accum_out=sums_sb[sl4][:, 0:1]))(ti, x1t, sl4), waits=[P.upto(f2)])
                pc_use = sa
                sqr = rec("s", (lambda ti, sl4: lambda s: s.activation(
                    junk_sb[:], acc_sb[:, ti * 1024:(ti + 1) * 1024], AF.Square,
                    accum_out=ssq_sb[sl4][:, 0:1]))(ti, sl4), waits=[P.upto(sa)])
                rec("v", (lambda sl4: lambda v: v.tensor_scalar(
                    mu_sb[sl4][:, 0:1], sums_sb[sl4][:, 0:1], 1.0 / D, None, ALU.mult))(sl4),
                    waits=[P.upto(sqr)])
                tpo = rec("v", (lambda sl4: lambda v: v.tensor_scalar(
                    tp1_sb[sl4][:, 0:1], ssq_sb[sl4][:, 0:1], 1.0 / D, None, ALU.mult))(sl4))
                c1 = rec("v", (lambda sl4: lambda v: v.tensor_tensor(
                    r1_sb[sl4][:, 0:1], mu_sb[sl4][:, 0:1], mu_sb[sl4][:, 0:1], ALU.mult))(sl4),
                    waits=[P.upto(tpo)])
                c2 = rec("v", (lambda sl4: lambda v: v.tensor_tensor(
                    r1_sb[sl4][:, 0:1], tp1_sb[sl4][:, 0:1], r1_sb[sl4][:, 0:1], ALU.subtract))(sl4),
                    waits=[P.upto(c1)])
                c3 = rec("v", (lambda sl4: lambda v: v.tensor_scalar(
                    r1_sb[sl4][:, 0:1], r1_sb[sl4][:, 0:1], PLNEPS, None, ALU.add))(sl4),
                    waits=[P.upto(c2)])
                c4 = rec("v", (lambda sl4: lambda v: v.reciprocal(r1_sb[sl4][:, 0:1], r1_sb[sl4][:, 0:1]))(sl4),
                    waits=[P.upto(c3)])
                sq2 = rec("s", (lambda sl4: lambda s: s.activation(
                    rs_sb[sl4][:, 0:1], r1_sb[sl4][:, 0:1], AF.Sqrt))(sl4), waits=[P.upto(c4)])
                wts = [P.upto(sq2)]
                if po_use is not None:
                    wts.append(P.upto(po_use))
                oa = rec("v", (lambda ti, sl4: lambda v: v.tensor_scalar(
                    pool_o[:], acc_sb[:, ti * 1024:(ti + 1) * 1024],
                    mu_sb[sl4][:, 0:1], rs_sb[sl4][:, 0:1], ALU.subtract, ALU.mult))(ti, sl4),
                    waits=wts)
                st_op = rec("y", (lambda tc: lambda y: y.dma_start(
                    out_d[tc * 128:(tc + 1) * 128, :], pool_o[:]))(tc),
                    waits=[P.upto(oa)], dma="st")
                po_use = st_op
                store_last = st_op

        rec("y", lambda y: y.nop(), waits=[P.upto(store_last)])

    except _StopBuild:
        pass

    with nc.Block() as block:
        P.emit(block)
    ctx.close()
    return nc


_NC_CACHE = {}


def _get_nc(debug_taps=()):
    key = tuple(sorted(debug_taps))
    if key not in _NC_CACHE:
        _NC_CACHE[key] = build_nc(debug_taps=key)
    return _NC_CACHE[key]


def kernel(
    L0_attn_inputs, L0_attn_ln1_scale, L0_attn_ln1_bias,
    L0_attn_Wq, L0_attn_bq, L0_attn_Wk, L0_attn_bk, L0_attn_Wv, L0_attn_bv,
    L0_attn_Wo, L0_attn_bo, L0_attn_ln2_scale, L0_attn_ln2_bias,
    L0_attn_W1, L0_attn_b1, L0_attn_W2, L0_attn_b2,
    L0_ffn_gate_w, L0_ffn_w1, L0_ffn_b1, L0_ffn_w2, L0_ffn_b2,
    _debug_taps=(),
):
    x = np.ascontiguousarray(np.asarray(L0_attn_inputs, dtype=np.float32))
    wk = np.asarray(L0_attn_Wk, dtype=np.float32).reshape(D, D)
    wv = np.asarray(L0_attn_Wv, dtype=np.float32).reshape(D, D)
    common = dict(
        wq=np.ascontiguousarray(np.asarray(L0_attn_Wq, dtype=np.float32).reshape(D, D)),
        wkv=np.ascontiguousarray(np.concatenate([wk, wv], axis=1)),
        bqv=np.ascontiguousarray(np.asarray(L0_attn_bq, dtype=np.float32).reshape(-1)),
        bkv=np.ascontiguousarray(np.concatenate(
            [np.asarray(L0_attn_bk, dtype=np.float32).reshape(-1),
             np.asarray(L0_attn_bv, dtype=np.float32).reshape(-1)])),
        ln1s=np.ascontiguousarray(np.asarray(L0_attn_ln1_scale, dtype=np.float32)),
        ln1b=np.ascontiguousarray(np.asarray(L0_attn_ln1_bias, dtype=np.float32)),
        wo=np.ascontiguousarray(np.asarray(L0_attn_Wo, dtype=np.float32).reshape(D, D)),
        bo=np.ascontiguousarray(np.asarray(L0_attn_bo, dtype=np.float32)),
        ln2s=np.ascontiguousarray(np.asarray(L0_attn_ln2_scale, dtype=np.float32)),
        ln2b=np.ascontiguousarray(np.asarray(L0_attn_ln2_bias, dtype=np.float32)),
        w1=np.ascontiguousarray(np.asarray(L0_attn_W1, dtype=np.float32)),
        b1=np.ascontiguousarray(np.asarray(L0_attn_b1, dtype=np.float32)),
        w2=np.ascontiguousarray(np.asarray(L0_attn_W2, dtype=np.float32)),
        b2=np.ascontiguousarray(np.asarray(L0_attn_b2, dtype=np.float32)),
        gw=np.ascontiguousarray(np.asarray(L0_ffn_gate_w, dtype=np.float32)),
        mw1=np.ascontiguousarray(np.asarray(L0_ffn_w1, dtype=np.float32)),
        mb1=np.ascontiguousarray(np.asarray(L0_ffn_b1, dtype=np.float32)),
        mw2=np.ascontiguousarray(np.asarray(L0_ffn_w2, dtype=np.float32)),
        mb2=np.ascontiguousarray(np.asarray(L0_ffn_b2, dtype=np.float32)),
    )
    in_maps = []
    for c in range(N_CORES):
        b = c // 4
        s0 = (c % 4) * T_OWN
        m = dict(common)
        m["xb"] = np.ascontiguousarray(np.roll(x[b], -s0, axis=0))
        in_maps.append(m)
    nc = _get_nc(_debug_taps)
    res = run_bass_kernel_spmd(nc, in_maps, core_ids=list(range(N_CORES)))
    outs = [res.results[c]["out"] for c in range(N_CORES)]
    y = np.concatenate(outs, axis=0).reshape(2, 8192, D)
    if _debug_taps:
        return y, res
    return y
